# version 08
# baseline (speedup 1.0000x reference)
from collections import defaultdict
from contextlib import ExitStack

import numpy as np
import ml_dtypes

import concourse.bass as bass
import concourse.mybir as mybir
from concourse.bass_utils import run_bass_kernel_spmd

F32 = mybir.dt.float32
BF16 = mybir.dt.bfloat16
AF = mybir.ActivationFunctionType
ALU = mybir.AluOpType
AX = mybir.AxisListType.X

NCORES = 8
B, N, D = 2, 8192, 1024
TOK = 2048
H0 = 2176
T0 = H0 + TOK
XH_W = 1024
XB_W = T0 - XH_W
WIN = 512
HID = 2730
NEG = -1.0e30

QKV_OFF = {0: 128, 1: 1152}
ATT_OFF = {0: 896, 1: 2048}
OUT_OFF = {0: 1024, 1: 2176}
NWIN = {0: 8, 1: 6}
NQT = {0: 26, 1: 17}

ENGINES = ("tensor", "vector", "scalar", "gpsimd", "sync")
GROUPS = [[0, 1, 2, 3], [4, 5, 6, 7]]



def _rotary_tables(t0):
    t = np.arange(t0 - H0, t0 + TOK, dtype=np.float64)
    half = np.arange(0, 64, 2, dtype=np.float64)
    inv_freq = 1.0 / 10000.0 ** (half / 64.0)
    d = np.arange(64)
    f_d = inv_freq[d % 32]
    freqs = f_d[:, None] * t[None, :]
    base = (half + 0.4 * 64.0) / (1.4 * 64.0)
    base_d = base[d % 32]
    power = (t - N // 2) / float(WIN // 2)
    scale = base_d[:, None] ** power[None, :]
    cos, sin = np.cos(freqs), np.sin(freqs)
    tabs = [cos * scale, sin * scale, cos / scale, sin / scale]
    return [np.ascontiguousarray(np.concatenate([x, x], 0).astype(np.float32))
            for x in tabs]


def _masks(t0, l):
    nqt = NQT[l]
    out = np.zeros((nqt, 128, 640), dtype=np.float64)
    q0 = t0 - (H0 - ATT_OFF[l])
    for j in range(nqt):
        qg = q0 + 128 * j + np.arange(128)
        ws = (qg[0] // WIN) * WIN
        ti = (qg[0] - ws) // 128
        kpos = ws - WIN + 128 * ti + np.arange(640)
        ok = (kpos[None, :] >= 0) & (kpos[None, :] <= qg[:, None]) \
            & (qg[:, None] <= kpos[None, :] + WIN)
        out[j] = np.where(ok, 0.0, NEG)
    return out.astype(ml_dtypes.bfloat16)


def _rot_perm():
    P = np.zeros((128, 128), dtype=np.float32)
    for i in range(128):
        if i % 64 < 32:
            P[i, i + 32] = -1.0
        else:
            P[i, i - 32] = 1.0
    return np.ascontiguousarray(P.T).astype(ml_dtypes.bfloat16)


def _col_layout(vec, tiles):
    v = np.zeros(128 * tiles, dtype=np.float32)
    v[:len(vec)] = vec
    return np.ascontiguousarray(v.reshape(tiles, 128).T)


def stage_core(inputs, p):
    b, pos = divmod(p, 4)
    t0 = pos * TOK
    xT = np.ascontiguousarray(np.asarray(inputs["L0_attn_x"][b]).T)
    lo = t0 - H0
    if lo >= 0:
        xc = xT[:, lo:t0 + TOK]
    else:
        xc = np.concatenate(
            [np.zeros((D, -lo), dtype=np.float32), xT[:, :t0 + TOK]], axis=1)
    d = {"xcore": np.ascontiguousarray(xc)}
    d["f_wqkv"] = np.ascontiguousarray(inputs["L0_ffn_wqkv"])
    d["f_wo"] = np.ascontiguousarray(inputs["L0_ffn_wo"])
    d["f_w1"] = np.ascontiguousarray(inputs["L0_ffn_w1"])
    d["f_w2"] = np.ascontiguousarray(inputs["L0_ffn_w2"])
    d["f_qs"] = np.ascontiguousarray(
        np.concatenate([inputs["L0_ffn_q_scale"]] * 2, axis=1)[:, :, None])
    d["f_ks"] = np.ascontiguousarray(
        np.concatenate([inputs["L0_ffn_k_scale"]] * 2, axis=1)[:, :, None])
    d["f_lng"] = np.stack([_col_layout(inputs["L0_ffn_ln_g"][l], 8) for l in range(2)])
    d["f_lnb"] = np.stack([_col_layout(inputs["L0_ffn_ln_b"][l], 8) for l in range(2)])
    d["f_b1a"] = np.stack([_col_layout(inputs["L0_ffn_b1"][l][:HID], 22) for l in range(2)])
    d["f_b1g"] = np.stack([_col_layout(inputs["L0_ffn_b1"][l][HID:], 22) for l in range(2)])
    d["f_b2"] = np.stack([_col_layout(inputs["L0_ffn_b2"][l], 8) for l in range(2)])
    d["a_wqkv"] = np.ascontiguousarray(inputs["L0_attn_w_qkv"])
    d["a_wout"] = np.ascontiguousarray(inputs["L0_attn_w_out"])
    d["a_bout"] = np.stack([_col_layout(inputs["L0_attn_b_out"][l], 8) for l in range(6)])
    d["a_gamma"] = np.stack([_col_layout(inputs["L0_attn_gamma"][l], 8) for l in range(6)])
    d["a_beta"] = np.stack([_col_layout(inputs["L0_attn_beta"][l], 8) for l in range(6)])
    rt = _rotary_tables(t0)
    d["rot_q1"], d["rot_q2"], d["rot_k1"], d["rot_k2"] = rt
    d["mask0"] = np.ascontiguousarray(_masks(t0, 0))
    d["mask1"] = np.ascontiguousarray(_masks(t0, 1))
    d["rotp"] = _rot_perm()
    d["ident"] = np.ascontiguousarray(np.eye(128, dtype=np.float32)).astype(
        ml_dtypes.bfloat16)
    d["bo_col"] = np.ascontiguousarray(np.stack(
        [np.r_[np.ones(64), np.zeros(64)], np.r_[np.zeros(64), np.ones(64)]],
        axis=1)).astype(ml_dtypes.bfloat16)
    d["bo_row"] = np.ascontiguousarray(np.stack(
        [np.r_[np.ones(64), np.zeros(64)], np.r_[np.zeros(64), np.ones(64)]],
        axis=0)).astype(np.float32)
    return d



class Bld:
    def __init__(self, nc):
        self.nc = nc
        self.q = {e: [] for e in ENGINES}
        self.cnt = defaultdict(int)
        self.sems = {}
        self._sem_stack = ExitStack()

    def sem(self, name):
        if name not in self.sems:
            self.sems[name] = self._sem_stack.enter_context(self.nc.semaphore(name))
        return self.sems[name]

    def op(self, eng, fn, inc=None, amt=1):
        if inc is not None:
            s = self.sem(inc)
            self.cnt[inc] += amt

            def wrapped(e, fn=fn, s=s, amt=amt):
                fn(e).then_inc(s, amt)
            self.q[eng].append(wrapped)
            return self.cnt[inc]
        self.q[eng].append(fn)
        return None

    def dma(self, eng, out, in_, inc, **kw):
        return self.op(eng, lambda e, o=out, i=in_, kw=kw: e.dma_start(out=o, in_=i, **kw),
                       inc=inc, amt=16)

    def wait(self, eng, name, val):
        if val <= 0:
            return
        s = self.sem(name)
        self.q[eng].append(lambda e, s=s, v=val: e.wait_ge(s, v))

    def W(self, eng, *names):
        for n in names:
            self.wait(eng, n, self.cnt[n])

    def gate(self):
        allsems = ["tm", "dv", "ac", "gp", "dW", "dL", "dS"]
        for e in ENGINES:
            self.W(e, *allsems)

    def now(self, name):
        return self.cnt[name]

    def replay(self):
        with self.nc.Block() as block:
            for name in ENGINES:
                ops = self.q[name]
                if not ops:
                    continue

                def runner(e, ops=tuple(ops)):
                    for f in ops:
                        f(e)
                getattr(block, name)(runner)
        self.q = {e: [] for e in ENGINES}

    def close(self):
        self._sem_stack.close()


def x_slices(c0, w):
    out = []
    if c0 < XH_W:
        ww = min(w, XH_W - c0)
        out.append(("H", c0, ww))
        c0 += ww
        w -= ww
    if w > 0:
        out.append(("B", c0 - XH_W, w))
    return out



def build_program(dev_dumps=(), n_ffn=2, n_attn=6):
    nc = bass.Bass("TRN2", target_bir_lowering=False)
    dumps = set(dev_dumps)

    xcore = nc.dram_tensor("xcore", [D, T0], F32, kind="ExternalInput")
    f_wqkv = nc.dram_tensor("f_wqkv", [2, D, 1536], F32, kind="ExternalInput")
    f_wo = nc.dram_tensor("f_wo", [2, 512, D], F32, kind="ExternalInput")
    f_w1 = nc.dram_tensor("f_w1", [2, D, 5460], F32, kind="ExternalInput")
    f_w2 = nc.dram_tensor("f_w2", [2, HID, D], F32, kind="ExternalInput")
    f_qs = nc.dram_tensor("f_qs", [2, 128, 1], F32, kind="ExternalInput")
    f_ks = nc.dram_tensor("f_ks", [2, 128, 1], F32, kind="ExternalInput")
    f_lng = nc.dram_tensor("f_lng", [2, 128, 8], F32, kind="ExternalInput")
    f_lnb = nc.dram_tensor("f_lnb", [2, 128, 8], F32, kind="ExternalInput")
    f_b1a = nc.dram_tensor("f_b1a", [2, 128, 22], F32, kind="ExternalInput")
    f_b1g = nc.dram_tensor("f_b1g", [2, 128, 22], F32, kind="ExternalInput")
    f_b2 = nc.dram_tensor("f_b2", [2, 128, 8], F32, kind="ExternalInput")
    a_wqkv = nc.dram_tensor("a_wqkv", [6, D, 3072], F32, kind="ExternalInput")
    a_wout = nc.dram_tensor("a_wout", [6, D, D], F32, kind="ExternalInput")
    a_bout = nc.dram_tensor("a_bout", [6, 128, 8], F32, kind="ExternalInput")
    a_gamma = nc.dram_tensor("a_gamma", [6, 128, 8], F32, kind="ExternalInput")
    a_beta = nc.dram_tensor("a_beta", [6, 128, 8], F32, kind="ExternalInput")
    rot_t = [nc.dram_tensor(f"rot_{s}", [128, T0], F32, kind="ExternalInput")
             for s in ("q1", "q2", "k1", "k2")]
    mask_d = {0: nc.dram_tensor("mask0", [NQT[0], 128, 640], BF16, kind="ExternalInput"),
              1: nc.dram_tensor("mask1", [NQT[1], 128, 640], BF16, kind="ExternalInput")}
    rotp = nc.dram_tensor("rotp", [128, 128], BF16, kind="ExternalInput")
    ident = nc.dram_tensor("ident", [128, 128], BF16, kind="ExternalInput")
    bo_col = nc.dram_tensor("bo_col", [128, 2], BF16, kind="ExternalInput")
    bo_row = nc.dram_tensor("bo_row", [2, 128], F32, kind="ExternalInput")

    out_fm = nc.dram_tensor("out_fm", [D, TOK], F32, kind="ExternalOutput")
    xn_park = nc.dram_tensor("xn_park", [D, TOK], F32, kind="Internal")
    g_park = nc.dram_tensor("g_park", [D, TOK], F32, kind="Internal")
    ar_in = nc.dram_tensor("ar_in", [1025, 1024], BF16, kind="Internal")
    ar_out = nc.dram_tensor("ar_out", [1025, 1024], BF16, kind="Internal")

    dbg = nc.dram_tensor("dbg", [128, 4 * 640], F32, kind="ExternalOutput") \
        if "dbg" in dumps else None
    dbg2 = nc.dram_tensor("dbg2", [1024, 1664], F32, kind="ExternalOutput") \
        if "dbg2" in dumps else None
    xal = nc.dram_tensor("xal", [6, D, TOK], F32, kind="ExternalOutput") \
        if "xal" in dumps else None
    dump_t = {}
    for name in dumps:
        if name in ("dbg", "dbg2", "xal"):
            continue
        shp = {"xn_dump": [D, T0], "xl1_dump": [D, T0], "xl2_dump": [D, T0],
               "xa1_dump": [D, T0], "g_dump": [D, TOK],
               "q_dump": [512, T0], "k_dump": [512, T0],
               "v_dump": [T0, 512], "o_dump": [512, T0],
               "s1_dump": [D, TOK], "s_dump": [D, TOK]}[name]
        dump_t[name] = nc.dram_tensor(name, shp, F32, kind="ExternalOutput")

    bld = Bld(nc)
    pstack = ExitStack()

    def sb(stack, name, shape, dt):
        return stack.enter_context(nc.sbuf_tensor(name, shape, dt))

    def psum(stack, name, shape, dt=F32):
        return stack.enter_context(nc.psum_tensor(name, shape, dt))

    id_bf = sb(pstack, "id_bf", [128, 128], BF16)
    rotp_sb = sb(pstack, "rotp_sb", [128, 128], BF16)
    bo_col_sb = sb(pstack, "bo_col_sb", [128, 2], BF16)
    bo_row_sb = sb(pstack, "bo_row_sb", [2, 128], F32)
    ones1_f = sb(pstack, "ones1_f", [1, 128], F32)
    onesc_bf = sb(pstack, "onesc_bf", [128, 1], BF16)
    id1_f = sb(pstack, "id1_f", [1, 1], F32)
    eps_f = sb(pstack, "eps_f", [128, 1], F32)
    nc.const_aps.register(F32, 1e-5, eps_f[:, 0:1])
    scrA = sb(pstack, "scrA", [128, 512], F32)
    scrB = sb(pstack, "scrB", [128, 512], F32)
    scrC = sb(pstack, "scrC", [128, 512], F32)
    rowst = sb(pstack, "rowst", [128, 16], F32)
    rows1 = sb(pstack, "rows1", [1, 2048], F32)
    rows2 = sb(pstack, "rows2", [1, 2048], F32)

    bld.dma("gpsimd", id_bf[:, :], ident[:, :], "dW")
    bld.dma("gpsimd", rotp_sb[:, :], rotp[:, :], "dW")
    bld.dma("gpsimd", bo_col_sb[:, :], bo_col[:, :], "dW")
    bld.dma("gpsimd", bo_row_sb[:, :], bo_row[:, :], "dW")
    bld.op("gpsimd", lambda e: e.memset(ones1_f[:, :], 1.0))
    bld.op("gpsimd", lambda e: e.memset(onesc_bf[:, :], 1.0))
    bld.op("gpsimd", lambda e: e.memset(id1_f[:, :], 1.0))
    bld.op("gpsimd", lambda e: e.memset(eps_f[:, :], 1e-5), inc="gp")
    bld.gate()

    fstack = ExitStack()
    xH = [sb(fstack, f"xH{i}", [128, XH_W], BF16) for i in range(8)]
    xB = [sb(fstack, f"xB{i}", [128, XB_W], F32) for i in range(8)]

    def xap(region, ct, c0, w):
        t = (xH if region == "H" else xB)[ct]
        return t[:, c0:c0 + w]

    def x_read(ct, c0, w):
        return [(xap(r, ct, s, ww), ww) for r, s, ww in x_slices(c0, w)]

    def dump_x(name):
        bld.gate()
        for ct in range(8):
            bld.dma("gpsimd", dump_t[name][ct * 128:(ct + 1) * 128, 0:XH_W],
                    xH[ct][:, :], "dW")
            bld.dma("sync", dump_t[name][ct * 128:(ct + 1) * 128, XH_W:T0],
                    xB[ct][:, :], "dS")
        bld.gate()


    def emit_top_ln():
        ts_ = ExitStack()
        topx = [sb(ts_, f"topx{i}", [128, 512], F32) for i in range(8)]
        topb = [sb(ts_, f"topb{i}", [128, 512], BF16) for i in range(8)]
        tops = [sb(ts_, f"tops{i}", [128, 512], BF16) for i in range(8)]
        ps = psum(ts_, "ps_top", [128, 2048], F32)
        chunks = [(i * 512, 512) for i in range(8)] + [(4096, 128)]
        for (c0, w) in chunks:
            for ct in range(8):
                bld.dma("sync", topx[ct][:, 0:w],
                        xcore[ct * 128:(ct + 1) * 128, c0:c0 + w], "dL")
            bld.W("vector", "dL")
            bld.W("vector", "tm")
            for ct in range(8):
                bld.op("vector", lambda e, ct=ct, w=w: e.tensor_copy(
                    topb[ct][:, 0:w], topx[ct][:, 0:w]))
            for ct in range(8):
                bld.op("vector", lambda e, ct=ct, w=w: e.tensor_tensor(
                    tops[ct][:, 0:w], topb[ct][:, 0:w], topb[ct][:, 0:w], ALU.mult),
                    inc="dv" if ct == 7 else None)
            bld.W("tensor", "dv")
            for ct in range(8):
                bld.op("tensor", lambda e, ct=ct, w=w: e.matmul(
                    ps[0:1, 0:w], onesc_bf[:, 0:1], topb[ct][:, 0:w],
                    start=(ct == 0), stop=(ct == 7)))
            for ct in range(8):
                bld.op("tensor", lambda e, ct=ct, w=w: e.matmul(
                    ps[0:1, 512:512 + w], onesc_bf[:, 0:1], tops[ct][:, 0:w],
                    start=(ct == 0), stop=(ct == 7)),
                    inc="tm" if ct == 7 else None)
            bld.W("vector", "tm")
            bld.op("vector", lambda e, w=w: e.tensor_scalar(
                rows1[0:1, 0:w], ps[0:1, 0:w], 1.0 / D, None, op0=ALU.mult))
            bld.op("vector", lambda e, w=w: e.tensor_scalar(
                rows1[0:1, 1024:1024 + w], ps[0:1, 512:512 + w], 1.0 / D, None,
                op0=ALU.mult), inc="dv")
            bld.W("vector", "dv")
            bld.op("vector", lambda e, w=w: e.tensor_tensor(
                rows2[0:1, 0:w], rows1[0:1, 0:w], rows1[0:1, 0:w], ALU.mult),
                inc="dv")
            bld.W("vector", "dv")
            bld.op("vector", lambda e, w=w: e.tensor_tensor(
                rows2[0:1, 1024:1024 + w], rows1[0:1, 1024:1024 + w],
                rows2[0:1, 0:w], ALU.subtract), inc="dv")
            bld.W("scalar", "dv")
            bld.op("scalar", lambda e, w=w: e.activation(
                rows2[0:1, 0:w], rows2[0:1, 1024:1024 + w], AF.Ln, bias=1e-5,
                scale=1.0))
            bld.op("scalar", lambda e, w=w: e.activation(
                rows2[0:1, 0:w], rows2[0:1, 0:w], AF.Exp, bias=0.0, scale=-0.5),
                inc="ac")
            bld.W("vector", "dv")
            bld.op("vector", lambda e, w=w: e.tensor_scalar(
                rows1[0:1, 1024:1024 + w], rows1[0:1, 0:w], -1.0, None,
                op0=ALU.mult), inc="dv")
            bld.W("tensor", "ac", "dv")
            bld.op("tensor", lambda e, w=w: e.matmul(
                ps[:, 1024:1024 + w], ones1_f[0:1, :], rows1[0:1, 1024:1024 + w],
                start=True, stop=True))
            bld.op("tensor", lambda e, w=w: e.matmul(
                ps[:, 1536:1536 + w], ones1_f[0:1, :], rows2[0:1, 0:w],
                start=True, stop=True), inc="tm")
            bld.W("vector", "tm")
            for ct in range(8):
                bld.op("vector", lambda e, ct=ct, w=w: e.scalar_tensor_tensor(
                    topx[ct][:, 0:w], ps[:, 1024:1024 + w], 1.0, topx[ct][:, 0:w],
                    op0=ALU.mult, op1=ALU.add))
                (r, s, ww), = x_slices(c0, w)
                bld.op("vector", lambda e, ct=ct, r=r, s=s, ww=ww: e.tensor_tensor(
                    xap(r, ct, s, ww), topx[ct][:, 0:ww], ps[:, 1536:1536 + ww],
                    ALU.mult), inc="dv" if ct == 7 else None)
            bld.W("sync", "dv")
            if c0 + w > H0:
                s0 = max(c0, H0)
                for ct in range(8):
                    bld.dma("sync", xn_park[ct * 128:(ct + 1) * 128, s0 - H0:c0 + w - H0],
                            xB[ct][:, s0 - XH_W:c0 + w - XH_W], "dS")
        bld.gate()
        ts_.close()

    def qtiles_of_window(l, w):
        ws = QKV_OFF[l] + 512 * w
        out = []
        for j in range(NQT[l]):
            c = ATT_OFF[l] + 128 * j
            if ws <= c < ws + 512:
                out.append((j, c))
        return out

    def emit_ffn_pass1(l):
        p1 = ExitStack()
        ps_qkv = psum(p1, f"psqkv{l}", [128, 1024], F32)
        ps_ms = psum(p1, f"psms{l}", [128, 512], F32)
        ps_rot = psum(p1, f"psrot{l}", [128, 512], F32)
        ps_sim = psum(p1, f"pssim{l}", [128, 640], F32)
        ps_aT = psum(p1, f"psaT{l}", [128, 640], BF16)
        ps_o = psum(p1, f"pso{l}", [128, 512], F32)
        h_bf = [sb(p1, f"hbf{l}_{i}", [128, 512], BF16) for i in range(8)]
        kring = [sb(p1, f"kring{l}_{i}", [128, 1024], BF16) for i in range(4)]
        vring = sb(p1, f"vring{l}", [128, 8 * 512], BF16)
        qbuf = [sb(p1, f"qbuf{l}_{i}", [128, 512], BF16) for i in range(4)]
        o_sb = [sb(p1, f"osb{l}_{i}", [128, 512], BF16) for i in range(4)]
        masked = sb(p1, f"masked{l}", [128, 640], F32)
        attn_bf = sb(p1, f"attn_bf{l}", [128, 640], BF16)
        attnT_sb = sb(p1, f"attnT_sb{l}", [128, 640], BF16)
        tabs = [sb(p1, f"tab{l}_{i}", [128, 512], F32) for i in range(2)]
        mask_sb = sb(p1, f"mask_sb{l}", [128, 640], BF16)
        wqkv_sb = sb(p1, f"wqkv_sb{l}", [128, 4096], BF16)
        wo_sb = sb(p1, f"wo_sb{l}", [128, 4096], BF16)
        sq_bf = sb(p1, f"sq_bf{l}", [128, 512], BF16)
        q1_bf = sb(p1, f"q1_bf{l}", [128, 512], BF16)
        qsk_sb = sb(p1, f"qsk_sb{l}", [128, 2], F32)
        rowsp = sb(p1, f"rowsp{l}", [2, 1024], F32)

        def emit_hcast(w):
            ws = QKV_OFF[l] + 512 * w
            for ct in range(8):
                src0 = ws if ct < 4 else ws - 1
                off = 0
                for ap, ww in x_read(ct, src0, 512):
                    bld.op("vector", lambda e, ct=ct, ap=ap, ww=ww, off=off:
                           e.tensor_copy(h_bf[ct][:, off:off + ww], ap),
                           inc="dv" if (ct == 7 and off + ww == 512) else None)
                    off += ww

        def emit_post(kind, mt, bank, slot):
            t1, t2 = tabs[0], tabs[1]
            qs_col = qsk_sb[:, 0:1] if kind == "q" else qsk_sb[:, 1:2]
            dst = qbuf[mt][:, 0:512] if kind == "q" \
                else kring[mt][:, slot * 512:(slot + 1) * 512]
            bld.W("scalar", "tm")
            bld.op("scalar", lambda e, bank=bank: e.activation(
                sq_bf[:, :], bank, AF.Square), inc="ac")
            bld.W("tensor", "ac", "dv")
            bld.op("tensor", lambda e: e.matmul(
                ps_ms[0:2, 0:512], bo_col_sb[:, :], sq_bf[:, :], start=True,
                stop=True), inc="tm")
            bld.W("vector", "tm")
            bld.op("vector", lambda e: e.tensor_scalar(
                rowsp[0:2, 0:512], ps_ms[0:2, 0:512], 1e-24, None, op0=ALU.max),
                inc="dv")
            bld.W("scalar", "dv")
            bld.op("scalar", lambda e: e.activation(
                rowsp[0:2, 512:1024], rowsp[0:2, 0:512], AF.Ln, bias=0.0,
                scale=1.0))
            bld.op("scalar", lambda e: e.activation(
                rowsp[0:2, 512:1024], rowsp[0:2, 512:1024], AF.Exp, bias=0.0,
                scale=-0.5), inc="ac")
            bld.W("tensor", "ac")
            bld.op("tensor", lambda e: e.matmul(
                ps_ms[:, 0:512], bo_row_sb[:, :], rowsp[0:2, 512:1024],
                start=True, stop=True), inc="tm")
            bld.W("vector", "tm", "dL")
            bld.op("vector", lambda e: e.tensor_copy(scrB[:, 0:512], ps_ms[:, 0:512]))
            bld.op("vector", lambda e, bank=bank, qs_col=qs_col:
                   e.scalar_tensor_tensor(q1_bf[:, :], bank, qs_col,
                                          scrB[:, 0:512], op0=ALU.mult,
                                          op1=ALU.mult), inc="dv")
            bld.W("tensor", "dv")
            bld.op("tensor", lambda e: e.matmul(
                ps_rot[:, 0:512], rotp_sb[:, :], q1_bf[:, :], start=True,
                stop=True), inc="tm")
            bld.op("vector", lambda e, t1=t1: e.tensor_tensor(
                scrC[:, 0:512], q1_bf[:, :], t1[:, :], ALU.mult))
            bld.W("vector", "tm")
            bld.op("vector", lambda e, t2=t2: e.tensor_tensor(
                scrB[:, 0:512], ps_rot[:, 0:512], t2[:, :], ALU.mult))
            bld.op("vector", lambda e, dst=dst: e.tensor_tensor(
                dst, scrC[:, 0:512], scrB[:, 0:512], ALU.add), inc="dv")

        bld.dma("gpsimd", wo_sb[:, :].rearrange("p (k c) -> p k c", c=1024),
                f_wo[l].rearrange("(k p) c -> p k c", p=128), "dW")
        bld.dma("gpsimd", qsk_sb[:, 0:1], f_qs[l], "dW")
        bld.dma("gpsimd", qsk_sb[:, 1:2], f_ks[l], "dW")
        bld.W("vector", "dW")
        bld.W("tensor", "dW")

        emit_ffn_pass1.at = {"stt": [None, None], "exp": None, "tr": None,
                             "evac": None, "av": None, "oev": None}
        nw = NWIN[l]
        emit_hcast(0)
        for w in range(nw):
            ws = QKV_OFF[l] + 512 * w
            slot = w % 2
            qts = qtiles_of_window(l, w)
            chunks = ([0] if qts else []) + [1, 2]
            for ch in chunks:
                if ch < 2:
                    bld.W("sync", "dv")
                    for ti_ in range(2):
                        bld.dma("sync", tabs[ti_][:, :],
                                rot_t[ch * 2 + ti_][:, ws:ws + 512], "dL")
                buf = wqkv_sb
                bld.W("gpsimd", "tm")
                bld.dma("gpsimd",
                        buf[:, :].rearrange("p (k c) -> p k c", c=512),
                        f_wqkv[l].rearrange("(k p) c -> p k c", p=128)[:, :, ch * 512:(ch + 1) * 512],
                        "dW")
                bld.W("tensor", "dW", "dv")
                if ch < 2:
                    for mt in range(4):
                        bank = ps_qkv[:, (mt % 2) * 512:(mt % 2) * 512 + 512]
                        bld.W("tensor", "dv")
                        for kt in range(8):
                            bld.op("tensor", lambda e, b=bank, buf=buf, kt=kt, mt=mt:
                                   e.matmul(b, buf[:, kt * 512 + mt * 128:kt * 512 + mt * 128 + 128],
                                            h_bf[kt][:, 0:512],
                                            start=(kt == 0), stop=(kt == 7)),
                                   inc="tm" if kt == 7 else None)
                        emit_post("q" if ch == 0 else "k", mt, bank, slot)
                else:
                    for tt in range(4):
                        bank = ps_qkv[:, (tt % 2) * 512:(tt % 2) * 512 + 512]
                        bld.W("tensor", "dv")
                        for kt in range(8):
                            bld.op("tensor", lambda e, b=bank, buf=buf, kt=kt, tt=tt:
                                   e.matmul(b, h_bf[kt][:, tt * 128:tt * 128 + 128],
                                            buf[:, kt * 512:(kt + 1) * 512],
                                            start=(kt == 0), stop=(kt == 7)),
                                   inc="tm" if kt == 7 else None)
                        bld.W("vector", "tm")
                        bld.op("vector", lambda e, b=bank, slot=slot, tt=tt:
                               e.tensor_copy(vring[:, (slot * 4 + tt) * 512:(slot * 4 + tt + 1) * 512], b),
                               inc="dv")
                        if "v_dump" in dumps:
                            bld.W("gpsimd", "dv")
                            bld.dma("gpsimd",
                                    dump_t["v_dump"][ws + tt * 128:ws + tt * 128 + 128, :],
                                    vring[:, (slot * 4 + tt) * 512:(slot * 4 + tt + 1) * 512],
                                    "dW")
            if "q_dump" in dumps and qts:
                bld.W("gpsimd", "dv")
                for mt in range(4):
                    bld.dma("gpsimd", dump_t["q_dump"][mt * 128:(mt + 1) * 128, ws:ws + 512],
                            qbuf[mt][:, :], "dW")
            if "k_dump" in dumps:
                bld.W("gpsimd", "dv")
                for mt in range(4):
                    bld.dma("gpsimd", dump_t["k_dump"][mt * 128:(mt + 1) * 128, ws:ws + 512],
                            kring[mt][:, slot * 512:(slot + 1) * 512], "dW")

            at = emit_ffn_pass1.at
            if qts:
                dvnow = bld.now("dv")
                at["stt"][0] = max(at["stt"][0] or 0, dvnow)
                at["stt"][1] = max(at["stt"][1] or 0, dvnow)
            for (j, qcol) in qts:
                qoff = qcol - ws
                ti = qoff // 128
                bld.W("sync", "dv")
                mtok = bld.dma("sync", mask_sb[:, :], mask_d[l][j], "dL")
                for h in range(8):
                    ct, half = divmod(h, 2)
                    rb = half * 64
                    par = h % 2
                    simps = ps_sim if par == 0 else ps_qkv
                    p1w = 512 - 128 * ti
                    prev_slot = (w - 1) % 2
                    if at["stt"][par] is not None:
                        bld.wait("tensor", "dv", at["stt"][par])
                    pieces = [(0, p1w, kring[ct][rb:rb + 64,
                                                prev_slot * 512 + 128 * ti:(prev_slot + 1) * 512])]
                    if ti > 0:
                        pieces.append((p1w, 512, kring[ct][rb:rb + 64,
                                                          slot * 512:slot * 512 + 128 * ti]))
                    pieces.append((512, 640, kring[ct][rb:rb + 64,
                                                       slot * 512 + 128 * ti:slot * 512 + 128 * ti + 128]))
                    for pi, (sc0, sc1, kap) in enumerate(pieces):
                        bld.op("tensor", lambda e, ct=ct, rb=rb, qoff=qoff,
                               sc0=sc0, sc1=sc1, kap=kap, simps=simps:
                               e.matmul(simps[:, sc0:sc1],
                                        qbuf[ct][rb:rb + 64, qoff:qoff + 128],
                                        kap, start=True, stop=True,
                                        tile_position=(rb, 0)),
                               inc="tm" if pi == len(pieces) - 1 else None)
                    sim_tm = bld.now("tm")
                    bld.wait("vector", "tm", sim_tm)
                    bld.wait("vector", "dL", mtok)
                    if at["exp"] is not None:
                        bld.wait("vector", "ac", at["exp"])
                    bld.op("vector", lambda e, simps=simps: e.scalar_tensor_tensor(
                        masked[:, :], simps[:, 0:640], 8.0, mask_sb[:, :],
                        op0=ALU.mult, op1=ALU.add))
                    at["stt"][par] = bld.op("vector", lambda e: e.reduce_max(
                        rowst[:, 0:1], masked[:, :], axis=AX, negate=True),
                        inc="dv")
                    bld.wait("scalar", "dv", at["stt"][par])
                    if at["tr"] is not None:
                        bld.wait("scalar", "tm", at["tr"])
                    at["exp"] = bld.op("scalar", lambda e: e.activation(
                        attn_bf[:, :], masked[:, :], AF.Exp, bias=rowst[:, 0:1],
                        scale=1.0), inc="ac")
                    bld.wait("tensor", "ac", at["exp"])
                    if at["evac"] is not None:
                        bld.wait("tensor", "dv", at["evac"])
                    for kb in range(5):
                        bld.op("tensor", lambda e, kb=kb: e.transpose(
                            ps_aT[:, kb * 128:(kb + 1) * 128],
                            attn_bf[:, kb * 128:(kb + 1) * 128], id_bf[:, :]),
                            inc="tm" if kb == 4 else None)
                    at["tr"] = bld.now("tm")
                    bld.wait("vector", "tm", at["tr"])
                    at["evac"] = bld.op("vector", lambda e: e.tensor_copy(
                        attnT_sb[:, :], ps_aT[:, :]), inc="dv")
                    bld.wait("tensor", "dv", at["evac"])
                    if h == 0 and at["oev"] is not None:
                        bld.wait("tensor", "dv", at["oev"])
                    rps = ps_ms if h < 4 else ps_rot
                    rcol = (h % 4) * 128
                    for kb in range(5):
                        bld.op("tensor", lambda e, kb=kb, rps=rps, rcol=rcol:
                               e.matmul(rps[0:1, rcol:rcol + 128], onesc_bf[:, 0:1],
                                        attnT_sb[:, kb * 128:(kb + 1) * 128],
                                        start=(kb == 0), stop=(kb == 4)))
                    for kb in range(5):
                        kcol = ws - 512 + 128 * ti + 128 * kb
                        wk = (kcol - QKV_OFF[l]) // 512
                        twt = (kcol - QKV_OFF[l] - 512 * wk) // 128
                        vbase = ((wk % 2) * 4 + twt) * 512
                        bld.op("tensor", lambda e, kb=kb, vbase=vbase, h=h, rb=rb, ct=ct:
                               e.matmul(ps_o[rb:rb + 64, ct * 128:ct * 128 + 128],
                                        vring[:, vbase + h * 64:vbase + h * 64 + 64],
                                        attnT_sb[:, kb * 128:(kb + 1) * 128],
                                        start=(kb == 0), stop=(kb == 4),
                                        tile_position=(0, rb)),
                               inc="tm" if kb == 4 else None)
                    at["av"] = bld.now("tm")
                bld.wait("vector", "tm", at["av"])
                bld.op("vector", lambda e: e.reciprocal(
                    scrB[0:1, 0:512], ps_ms[0:1, 0:512]))
                bld.op("vector", lambda e: e.reciprocal(
                    scrB[1:2, 0:512] if False else scrC[0:1, 0:512],
                    ps_rot[0:1, 0:512]), inc="dv")
                rdv = bld.now("dv")
                bld.wait("tensor", "dv", rdv)
                for h2 in range(8):
                    ct2, half2 = divmod(h2, 2)
                    rb2 = half2 * 64
                    srow = scrB if h2 < 4 else scrC
                    scol = (h2 % 4) * 128
                    bld.op("tensor", lambda e, ct2=ct2, rb2=rb2, srow=srow, scol=scol:
                           e.matmul(ps_rot[rb2:rb2 + 64, ct2 * 128:ct2 * 128 + 128],
                                    ones1_f[0:1, 0:64], srow[0:1, scol:scol + 128],
                                    start=True, stop=True, tile_position=(0, rb2)),
                           inc="tm" if h2 == 7 else None)
                btm = bld.now("tm")
                bld.wait("vector", "tm", btm)
                bld.op("vector", lambda e: e.tensor_copy(
                    scrA[:, 0:512], ps_rot[:, 0:512]), inc="dv")
                bld.W("vector", "dv")
                for ct in range(4):
                    bld.op("vector", lambda e, ct=ct, qoff=qoff: e.tensor_tensor(
                        o_sb[ct][:, qoff:qoff + 128], ps_o[:, ct * 128:ct * 128 + 128],
                        scrA[:, ct * 128:ct * 128 + 128], ALU.mult),
                        inc="dv" if ct == 3 else None)
                at["oev"] = bld.now("dv")
                if "o_dump" in dumps:
                    bld.W("gpsimd", "dv")
                    for ct in range(4):
                        bld.dma("gpsimd", dump_t["o_dump"][ct * 128:(ct + 1) * 128, qcol:qcol + 128],
                                o_sb[ct][:, qoff:qoff + 128], "dW")
            bld.W("vector", "tm")
            if w + 1 < nw:
                emit_hcast(w + 1)
            if qts:
                qo0 = qts[0][1] - ws
                wlen = 512 - qo0
                for dt in range(8):
                    bld.W("tensor", "dv")
                    for ct in range(4):
                        bld.op("tensor", lambda e, dt=dt, ct=ct, qo0=qo0, wlen=wlen:
                               e.matmul(ps_o[:, 0:wlen],
                                        wo_sb[:, ct * 1024 + dt * 128:ct * 1024 + dt * 128 + 128],
                                        o_sb[ct][:, qo0:512],
                                        start=(ct == 0), stop=(ct == 3)),
                               inc="tm" if ct == 3 else None)
                    bld.W("vector", "tm")
                    off = 0
                    for ap, ww in x_read(dt, ws + qo0, wlen):
                        bld.op("vector", lambda e, ap=ap, ww=ww, off=off:
                               e.tensor_tensor(ap, ps_o[:, off:off + ww], ap, ALU.add),
                               inc="dv" if off + ww == wlen else None)
                        off += ww
        bld.gate()
        p1.close()

    def emit_ffn_pass2(l):
        p2 = ExitStack()
        ps_ln = psum(p2, f"psln{l}", [128, 1024], F32)
        ps_ag = psum(p2, f"psag{l}", [128, 1024], F32)
        ps_out = psum(p2, f"psout{l}", [128, 2048], F32)
        hh_sb = sb(p2, f"hh_sb{l}", [128, 22 * 512], BF16)
        h2 = [sb(p2, f"h2_{l}_{i}", [128, 512], BF16) for i in range(8)]
        aa_bf = sb(p2, f"aa_bf{l}", [128, 512], BF16)
        gg_bf = sb(p2, f"gg_bf{l}", [128, 512], BF16)
        w1_sb = [sb(p2, f"w1_sb{l}_{i}", [128, 2048], BF16) for i in range(2)]
        w2_sb = [sb(p2, f"w2_sb{l}_{i}", [128, 1024], BF16) for i in range(2)]
        xe2 = [sb(p2, f"xe2_{l}_{i}", [128, 8], F32) for i in range(4)]
        lng_sb = sb(p2, f"lng_sb{l}", [128, 8], F32)
        lnb_sb = sb(p2, f"lnb_sb{l}", [128, 8], F32)
        b1a_sb = sb(p2, f"b1a_sb{l}", [128, 22], F32)
        b1g_sb = sb(p2, f"b1g_sb{l}", [128, 22], F32)
        b2_sb = sb(p2, f"b2_sb{l}", [128, 8], F32)
        outbanks = [ps_out[:, i * 512:(i + 1) * 512] for i in range(4)] + \
                   [ps_ag[:, 0:512], ps_ag[:, 512:1024],
                    ps_ln[:, 0:512], ps_ln[:, 512:1024]]

        bld.dma("gpsimd", lng_sb[:, :], f_lng[l], "dW")
        bld.dma("gpsimd", lnb_sb[:, :], f_lnb[l], "dW")
        bld.dma("gpsimd", b1a_sb[:, :], f_b1a[l], "dW")
        bld.dma("gpsimd", b1g_sb[:, :], f_b1g[l], "dW")
        bld.dma("gpsimd", b2_sb[:, :], f_b2[l], "dW")
        bld.W("vector", "dW")
        bld.W("scalar", "dW")

        o0 = OUT_OFF[l]
        total = T0 - o0
        chunks = []
        c = o0
        while c < T0:
            cw = min(512, T0 - c)
            chunks.append((c, cw))
            c += cw

        for ci, (c0, cw) in enumerate(chunks):
            for ct in range(4, 8):
                (r, s, ww), = x_slices(c0 - 1, 1)
                bld.op("vector", lambda e, ct=ct, r=r, s=s, ci=ci: e.tensor_copy(
                    xe2[ct - 4][:, ci:ci + 1], xap(r, ct, s, 1)),
                    inc="dv" if (ct == 7 and ci == len(chunks) - 1) else None)

        for ci, (c0, cw) in enumerate(chunks):
            def shifted_pieces(ct, ci=ci, c0=c0, cw=cw):
                if ct < 4:
                    return x_read(ct, c0, cw)
                return [(xe2[ct - 4][:, ci:ci + 1], 1)] + \
                    (x_read(ct, c0, cw - 1) if cw > 1 else [])
            bld.W("vector", "tm", "dv")
            for ct in range(8):
                off = 0
                for ap, ww in shifted_pieces(ct):
                    bld.op("vector", lambda e, ct=ct, ap=ap, ww=ww, off=off:
                           e.tensor_copy(h2[ct][:, off:off + ww], ap))
                    off += ww
            for ct in range(8):
                if ct > 0:
                    bld.W("vector", "tm")
                bld.op("vector", lambda e, ct=ct, cw=cw: e.tensor_tensor(
                    aa_bf[:, 0:cw], h2[ct][:, 0:cw], h2[ct][:, 0:cw], ALU.mult),
                    inc="dv")
                bld.W("tensor", "dv")
                bld.op("tensor", lambda e, ct=ct, cw=cw: e.matmul(
                    ps_ln[0:1, 0:cw], onesc_bf[:, 0:1], h2[ct][:, 0:cw],
                    start=(ct == 0), stop=(ct == 7)))
                bld.op("tensor", lambda e, ct=ct, cw=cw: e.matmul(
                    ps_ln[0:1, 512:512 + cw], onesc_bf[:, 0:1], aa_bf[:, 0:cw],
                    start=(ct == 0), stop=(ct == 7)), inc="tm")
            bld.W("vector", "tm")
            bld.op("vector", lambda e, cw=cw: e.tensor_scalar(
                rows1[0:1, 0:cw], ps_ln[0:1, 0:cw], 1.0 / D, None, op0=ALU.mult))
            bld.op("vector", lambda e, cw=cw: e.tensor_scalar(
                rows1[0:1, 1024:1024 + cw], ps_ln[0:1, 512:512 + cw], 1.0 / D,
                None, op0=ALU.mult), inc="dv")
            bld.W("vector", "dv")
            bld.op("vector", lambda e, cw=cw: e.tensor_tensor(
                rows2[0:1, 0:cw], rows1[0:1, 0:cw], rows1[0:1, 0:cw], ALU.mult),
                inc="dv")
            bld.W("vector", "dv")
            bld.op("vector", lambda e, cw=cw: e.tensor_tensor(
                rows2[0:1, 1024:1024 + cw], rows1[0:1, 1024:1024 + cw],
                rows2[0:1, 0:cw], ALU.subtract), inc="dv")
            bld.W("vector", "dv")
            bld.op("vector", lambda e, cw=cw: e.tensor_scalar(
                rows1[0:1, 1024:1024 + cw], rows1[0:1, 0:cw], -1.0, None,
                op0=ALU.mult), inc="dv")
            bld.W("scalar", "dv")
            bld.op("scalar", lambda e, cw=cw: e.activation(
                rows2[0:1, 0:cw], rows2[0:1, 1024:1024 + cw], AF.Ln, bias=1e-5,
                scale=1.0))
            bld.op("scalar", lambda e, cw=cw: e.activation(
                rows2[0:1, 0:cw], rows2[0:1, 0:cw], AF.Exp, bias=0.0,
                scale=-0.5), inc="ac")
            bld.W("tensor", "ac")
            bld.op("tensor", lambda e, cw=cw: e.matmul(
                ps_ln[:, 0:cw], ones1_f[0:1, :], rows1[0:1, 1024:1024 + cw],
                start=True, stop=True))
            bld.op("tensor", lambda e, cw=cw: e.matmul(
                ps_ln[:, 512:512 + cw], ones1_f[0:1, :], rows2[0:1, 0:cw],
                start=True, stop=True), inc="tm")
            bld.W("vector", "tm")
            for ct in range(8):
                off = 0
                for ap, ww in shifted_pieces(ct):
                    bld.op("vector", lambda e, ap=ap, ww=ww, off=off:
                           e.scalar_tensor_tensor(
                               scrA[:, off:off + ww], ps_ln[:, off:off + ww], 1.0,
                               ap, op0=ALU.mult, op1=ALU.add))
                    off += ww
                bld.op("vector", lambda e, cw=cw: e.tensor_tensor(
                    scrB[:, 0:cw], scrA[:, 0:cw], ps_ln[:, 512:512 + cw], ALU.mult))
                bld.op("vector", lambda e, ct=ct, cw=cw: e.tensor_scalar(
                    h2[ct][:, 0:cw], scrB[:, 0:cw], lng_sb[:, ct:ct + 1],
                    lnb_sb[:, ct:ct + 1], op0=ALU.mult, op1=ALU.add),
                    inc="dv")
            if dbg2 is not None and l == 0 and ci == 2:
                bld.gate()
                for ct in range(8):
                    bld.dma("gpsimd", dbg2[ct * 128:(ct + 1) * 128, 0:512],
                            h2[ct][:, 0:512], "dW")
                bld.dma("sync", dbg2[0:1, 512:512 + cw], rows1[0:1, 1024:1024 + cw], "dS")
                bld.dma("sync", dbg2[1:2, 512:512 + cw], rows2[0:1, 0:cw], "dS")
                bld.gate()
            w1tok = [None, None]

            def load_w1(j):
                mw = 128 if j < 21 else 42
                buf = w1_sb[j % 2]
                if w1tok[j % 2] is not None:
                    bld.wait("gpsimd", "tm", w1tok[j % 2])
                bld.dma("gpsimd",
                        buf[:, :].rearrange("p (k c) -> p k c", c=256)[:, :, 0:mw],
                        f_w1[l].rearrange("(k p) c -> p k c", p=128)[:, :, j * 128:j * 128 + mw],
                        "dW")
                return bld.dma("gpsimd",
                               buf[:, :].rearrange("p (k c) -> p k c", c=256)[:, :, 128:128 + mw],
                               f_w1[l].rearrange("(k p) c -> p k c", p=128)[:, :, HID + j * 128:HID + j * 128 + mw],
                               "dW")

            w1dw = {0: load_w1(0)}
            gate_dv = None
            ag_ac = None
            tdv_h2 = bld.now("dv")
            for j in range(22):
                mw = 128 if j < 21 else 42
                buf = w1_sb[j % 2]
                bld.wait("tensor", "dW", w1dw.pop(j))
                if j == 0:
                    bld.wait("tensor", "dv", tdv_h2)
                if ag_ac is not None:
                    bld.wait("tensor", "ac", ag_ac)
                for kt in range(8):
                    bld.op("tensor", lambda e, buf=buf, kt=kt, mw=mw, cw=cw:
                           e.matmul(ps_ag[0:mw, 0:cw],
                                    buf[:, kt * 256:kt * 256 + mw],
                                    h2[kt][:, 0:cw], start=(kt == 0), stop=(kt == 7)))
                for kt in range(8):
                    bld.op("tensor", lambda e, buf=buf, kt=kt, mw=mw, cw=cw:
                           e.matmul(ps_ag[0:mw, 512:512 + cw],
                                    buf[:, kt * 256 + 128:kt * 256 + 128 + mw],
                                    h2[kt][:, 0:cw], start=(kt == 0), stop=(kt == 7)),
                           inc="tm" if kt == 7 else None)
                w1tok[j % 2] = bld.now("tm")
                if j + 1 < 22:
                    w1dw[j + 1] = load_w1(j + 1)
                bld.wait("scalar", "tm", w1tok[j % 2])
                if gate_dv is not None:
                    bld.wait("scalar", "dv", gate_dv)
                bld.op("scalar", lambda e, j=j, mw=mw, cw=cw: e.activation(
                    aa_bf[0:mw, 0:cw], ps_ag[0:mw, 0:cw], AF.Identity,
                    bias=b1a_sb[0:mw, j:j + 1], scale=1.0))
                ag_ac = bld.op("scalar", lambda e, j=j, mw=mw, cw=cw: e.activation(
                    gg_bf[0:mw, 0:cw], ps_ag[0:mw, 512:512 + cw], AF.Gelu,
                    bias=b1g_sb[0:mw, j:j + 1], scale=1.0), inc="ac")
                bld.wait("vector", "ac", ag_ac)
                gate_dv = bld.op("vector", lambda e, j=j, mw=mw, cw=cw: e.tensor_tensor(
                    hh_sb[0:mw, j * 512:j * 512 + cw], aa_bf[0:mw, 0:cw],
                    gg_bf[0:mw, 0:cw], ALU.mult), inc="dv")
                if dbg2 is not None and l == 0 and ci == 2 and j in (0, 1):
                    bld.gate()
                    bld.dma("gpsimd", dbg2[j * 128:(j + 1) * 128, 1024:1024 + cw],
                            hh_sb[0:mw, j * 512:j * 512 + cw], "dW")
                    bld.gate()
            bld.wait("tensor", "dv", gate_dv)
            w2tok = [None, None]

            def load_w2(j):
                mw = 128 if j < 21 else 42
                buf = w2_sb[j % 2]
                if w2tok[j % 2] is not None:
                    bld.wait("gpsimd", "tm", w2tok[j % 2])
                return bld.dma("gpsimd", buf[0:mw, :],
                               f_w2[l][j * 128:j * 128 + mw, :], "dW")

            w2dw = {0: load_w2(0)}
            for j in range(22):
                mw = 128 if j < 21 else 42
                buf = w2_sb[j % 2]
                bld.wait("tensor", "dW", w2dw.pop(j))
                for i in range(8):
                    bld.op("tensor", lambda e, buf=buf, i=i, j=j, mw=mw, cw=cw:
                           e.matmul(outbanks[i][:, 0:cw],
                                    buf[0:mw, i * 128:(i + 1) * 128],
                                    hh_sb[0:mw, j * 512:j * 512 + cw],
                                    start=(j == 0), stop=(j == 21)),
                           inc="tm" if i == 7 else None)
                w2tok[j % 2] = bld.now("tm")
                if j + 1 < 22:
                    w2dw[j + 1] = load_w2(j + 1)
            bld.W("vector", "tm")
            for i in range(8):
                (r, s, ww), = x_slices(c0, cw)
                bld.op("vector", lambda e, i=i, r=r, s=s, ww=ww: e.scalar_tensor_tensor(
                    xap(r, i, s, ww), outbanks[i][:, 0:ww], b2_sb[:, i:i + 1],
                    xap(r, i, s, ww), op0=ALU.add, op1=ALU.add), inc="dv")
        bld.gate()
        p2.close()

    def emit_silu_park():
        for nn in range((TOK + 511) // 512):
            c0 = H0 + nn * 512
            for ct in range(8):
                bld.W("scalar", "dS")
                bld.op("scalar", lambda e, ct=ct, c0=c0: e.activation(
                    scrA[:, 0:512], xB[ct][:, c0 - XH_W:c0 - XH_W + 512],
                    AF.Silu), inc="ac")
                bld.W("sync", "ac")
                bld.dma("sync", g_park[ct * 128:(ct + 1) * 128, c0 - H0:c0 - H0 + 512],
                        scrA[:, 0:512], "dS")
                if "g_dump" in dumps:
                    bld.dma("sync", dump_t["g_dump"][ct * 128:(ct + 1) * 128, c0 - H0:c0 - H0 + 512],
                            scrA[:, 0:512], "dS")
        bld.gate()

    emit_top_ln()
    if "xn_dump" in dumps:
        dump_x("xn_dump")
    for l in range(n_ffn):
        emit_ffn_pass1(l)
        if l == 0 and "xa1_dump" in dumps:
            dump_x("xa1_dump")
        emit_ffn_pass2(l)
        if l == 0 and "xl1_dump" in dumps:
            dump_x("xl1_dump")
    if n_ffn == 2 and "xl2_dump" in dumps:
        dump_x("xl2_dump")
    if n_ffn == 2:
        emit_silu_park()
    bld.replay()
    fstack.close()

    astack = ExitStack()
    x_a = [sb(astack, f"xa{i}", [128, TOK], F32) for i in range(8)]
    v_sb = [sb(astack, f"vsb{i}", [128, TOK], BF16) for i in range(8)]
    Eqp = [sb(astack, f"eqp{i}", [128, 1024], BF16) for i in range(8)]
    Ek = [sb(astack, f"ek{i}", [128, 1024], BF16) for i in range(8)]
    ha_bf = [sb(astack, f"habf{i}", [128, 1024], BF16) for i in range(8)]
    Msb = [sb(astack, f"msb{i}", [128, 1024], BF16) for i in range(8)]
    Ablk = sb(astack, "ablk", [128, 1024], BF16)
    zq_bf = Ablk
    wsb = sb(astack, "wsb", [128, 4096], BF16)
    wsb2 = sb(astack, "wsb2", [128, 4096], BF16)
    gam_sb = sb(astack, "gam_sb", [128, 8], F32)
    bet_sb = sb(astack, "bet_sb", [128, 8], F32)
    bo_sb = sb(astack, "bo_sb", [128, 8], F32)
    zkl_sb = sb(astack, "zkl_sb", [128, 8], F32)
    zk0_sb = sb(astack, "zk0_sb", [128, 8], F32)
    zkb_sb = sb(astack, "zkb_sb", [128, 8], BF16)
    zq_acc = sb(astack, "zq_acc", [1, 1024], F32)
    zqt = sb(astack, "zqt", [128, 8], F32)
    PS = psum(astack, "PS", [128, 4096], F32)

    def emit_attn_layer(l):
        btok = emit_attn_layer.btok
        wtok = emit_attn_layer.wtok

        def bank_wait(*banks):
            for bk in banks:
                if bk in btok:
                    s, v = btok[bk]
                    bld.wait("tensor", s, v)

        bld.dma("gpsimd", gam_sb[:, :], a_gamma[l], "dW")
        bld.dma("gpsimd", bet_sb[:, :], a_beta[l], "dW")
        tW = bld.dma("gpsimd", bo_sb[:, :], a_bout[l], "dW")
        bld.op("gpsimd", lambda e: e.memset(zq_acc[:, :], 0.0), inc="gp")
        tgp = bld.now("gp")
        bld.wait("vector", "gp", tgp)
        bld.wait("vector", "dW", tW)
        bld.wait("scalar", "dW", tW)

        for c in range(2):
            tc = c * 1024
            bld.W("vector", "tm", "dv")
            for ct in range(8):
                bld.op("vector", lambda e, ct=ct, tc=tc: e.tensor_copy(
                    ha_bf[ct][:, :], x_a[ct][:, tc:tc + 1024]))
            for ct in range(8):
                bld.op("vector", lambda e, ct=ct: e.tensor_tensor(
                    Ek[ct][:, :], ha_bf[ct][:, :], ha_bf[ct][:, :], ALU.mult),
                    inc="dv" if ct == 7 else None)
            tdv = bld.now("dv")
            bld.wait("tensor", "dv", tdv)
            bank_wait(0, 1, 6, 7)
            for hh in range(2):
                for ct in range(8):
                    bld.op("tensor", lambda e, ct=ct, hh=hh: e.matmul(
                        PS[0:1, 3072 + hh * 512:3584 + hh * 512],
                        onesc_bf[:, 0:1], ha_bf[ct][:, hh * 512:(hh + 1) * 512],
                        start=(ct == 0), stop=(ct == 7)))
                for ct in range(8):
                    bld.op("tensor", lambda e, ct=ct, hh=hh: e.matmul(
                        PS[0:1, hh * 512:512 + hh * 512],
                        onesc_bf[:, 0:1], Ek[ct][:, hh * 512:(hh + 1) * 512],
                        start=(ct == 0), stop=(ct == 7)),
                        inc="tm" if ct == 7 else None)
            ttm = bld.now("tm")
            bld.wait("vector", "tm", ttm)
            bld.op("vector", lambda e: e.tensor_scalar(
                rows1[0:1, 0:1024], PS[0:1, 3072:4096], 1.0 / D, None, op0=ALU.mult))
            bld.op("vector", lambda e: e.tensor_scalar(
                rows1[0:1, 1024:2048], PS[0:1, 0:1024], 1.0 / D, None,
                op0=ALU.mult), inc="dv")
            bld.W("vector", "dv")
            bld.op("vector", lambda e: e.tensor_tensor(
                rows2[0:1, 0:1024], rows1[0:1, 0:1024], rows1[0:1, 0:1024],
                ALU.mult), inc="dv")
            bld.W("vector", "dv")
            bld.op("vector", lambda e: e.tensor_tensor(
                rows2[0:1, 1024:2048], rows1[0:1, 1024:2048], rows2[0:1, 0:1024],
                ALU.subtract), inc="dv")
            bld.W("scalar", "dv")
            bld.op("scalar", lambda e: e.activation(
                rows2[0:1, 0:1024], rows2[0:1, 1024:2048], AF.Ln, bias=1e-5,
                scale=1.0))
            bld.op("scalar", lambda e: e.activation(
                rows2[0:1, 0:1024], rows2[0:1, 0:1024], AF.Exp, bias=0.0,
                scale=-0.5), inc="ac")
            bld.W("vector", "dv")
            bld.op("vector", lambda e: e.tensor_scalar(
                rows1[0:1, 1024:2048], rows1[0:1, 0:1024], -1.0, None,
                op0=ALU.mult), inc="dv")
            tdv = bld.now("dv"); tac = bld.now("ac")
            bld.wait("tensor", "ac", tac)
            bld.wait("tensor", "dv", tdv)
            bank_wait(4, 5, 6, 7)
            for hh in range(2):
                bld.op("tensor", lambda e, hh=hh: e.matmul(
                    PS[:, 2048 + hh * 512:2560 + hh * 512], ones1_f[0:1, :],
                    rows1[0:1, 1024 + hh * 512:1024 + (hh + 1) * 512],
                    start=True, stop=True))
                bld.op("tensor", lambda e, hh=hh: e.matmul(
                    PS[:, 3072 + hh * 512:3584 + hh * 512], ones1_f[0:1, :],
                    rows2[0:1, hh * 512:(hh + 1) * 512], start=True, stop=True),
                    inc="tm" if hh == 1 else None)
            ttm = bld.now("tm")
            bld.wait("vector", "tm", ttm)
            for ct in range(8):
                for hh in range(2):
                    bld.op("vector", lambda e, ct=ct, tc=tc, hh=hh: e.scalar_tensor_tensor(
                        scrA[:, 0:512], PS[:, 2048 + hh * 512:2560 + hh * 512], 1.0,
                        x_a[ct][:, tc + hh * 512:tc + (hh + 1) * 512],
                        op0=ALU.mult, op1=ALU.add))
                    bld.op("vector", lambda e, hh=hh: e.tensor_tensor(
                        scrA[:, 0:512], scrA[:, 0:512],
                        PS[:, 3072 + hh * 512:3584 + hh * 512], ALU.mult))
                    bld.op("vector", lambda e, ct=ct, hh=hh: e.tensor_scalar(
                        ha_bf[ct][:, hh * 512:(hh + 1) * 512], scrA[:, 0:512],
                        gam_sb[:, ct:ct + 1], bet_sb[:, ct:ct + 1],
                        op0=ALU.mult, op1=ALU.add),
                        inc="dv")
            tdv_h = bld.now("dv")
            for bk in (4, 5, 6, 7):
                btok[bk] = ("dv", tdv_h)
            h_waited = [False]

            def load_w(oc, ci):
                buf = wsb if ci % 2 == 0 else wsb2
                if wtok[ci % 2] is not None:
                    bld.wait("gpsimd", "tm", wtok[ci % 2])
                t = bld.dma("gpsimd",
                            buf[:, :].rearrange("p (k c) -> p k c", c=512),
                            a_wqkv[l].rearrange("(k p) c -> p k c", p=128)[:, :, oc * 512:(oc + 1) * 512],
                            "dW")
                return buf, t

            ocs = (2, 3, 0, 1, 4, 5)
            loads = {}
            loads[0] = load_w(ocs[0], emit_attn_layer.ci)
            emit_attn_layer.ci += 1
            for oci, oc in enumerate(ocs):
                buf, tw = loads.pop(oci)
                mybuf = 0 if buf is wsb else 1
                bld.wait("tensor", "dW", tw)
                if not h_waited[0]:
                    bld.wait("tensor", "dv", tdv_h)
                    h_waited[0] = True
                last_tm = None
                if oc < 4:
                    for tt in range(8):
                        bk = tt % 4
                        bank = PS[:, bk * 512:bk * 512 + 512]
                        bank_wait(bk)
                        for kt in range(8):
                            bld.op("tensor", lambda e, b=bank, kt=kt, tt=tt, mybuf=mybuf:
                                   e.matmul(b, ha_bf[kt][:, tt * 128:tt * 128 + 128],
                                            (wsb if mybuf == 0 else wsb2)[:, kt * 512:(kt + 1) * 512],
                                            start=(kt == 0), stop=(kt == 7)),
                                   inc="tm" if kt == 7 else None)
                        gtok = bld.now("tm")
                        last_tm = gtok
                        if tt == 0 and oci + 1 < len(ocs):
                            loads[oci + 1] = load_w(ocs[oci + 1], emit_attn_layer.ci)
                            emit_attn_layer.ci += 1
                        bld.wait("scalar", "tm", gtok)
                        if oc >= 2:
                            kh = oc - 2
                            tac = bld.op("scalar", lambda e, b=bank, tt=tt, kh=kh:
                                         e.activation(Ek[tt][:, kh * 512:(kh + 1) * 512],
                                                      b, AF.Exp, bias=0.0, scale=1.0),
                                         inc="ac")
                            btok[bk] = ("ac", tac)
                            tdv = bld.op("vector", lambda e, tt=tt, kh=kh: (
                                e.wait_ge(bld.sems["ac"], 0)) if False else None)                                 if False else None
                            bld.wait("vector", "ac", tac)
                            if kh == 0:
                                tdv = bld.op("vector", lambda e, tt=tt: e.reduce_sum(
                                    rowst[:, 8:9], Ek[tt][:, 0:512], axis=AX),
                                    inc="dv")
                                bld.W("vector", "dv")
                                bld.op("vector", lambda e, tt=tt: e.tensor_copy(
                                    zk0_sb[:, tt:tt + 1], rowst[:, 8:9]), inc="dv")
                            else:
                                bld.op("vector", lambda e, tt=tt: e.reduce_sum(
                                    rowst[:, 9:10], Ek[tt][:, 512:1024], axis=AX),
                                    inc="dv")
                                bld.W("vector", "dv")
                                bld.op("vector", lambda e, tt=tt: e.tensor_tensor(
                                    rowst[:, 10:11], rowst[:, 9:10],
                                    zk0_sb[:, tt:tt + 1], ALU.add), inc="dv")
                                bld.W("vector", "dv")
                                bld.op("vector", lambda e, tt=tt: e.tensor_copy(
                                    zkb_sb[:, tt:tt + 1], rowst[:, 10:11]),
                                    inc="dv")
                                tdv = bld.now("dv")
                                bld.wait("scalar", "dv", tdv)
                                bld.op("scalar", lambda e, tt=tt: e.activation(
                                    zkl_sb[:, tt:tt + 1], rowst[:, 10:11], AF.Ln,
                                    bias=0.0, scale=1.0), inc="ac")
                                tac2 = bld.now("ac")
                                bld.wait("vector", "ac", tac2)
                                bld.op("vector", lambda e, tt=tt: e.tensor_scalar(
                                    zkl_sb[:, tt:tt + 1], zkl_sb[:, tt:tt + 1],
                                    -1.0, None, op0=ALU.mult), inc="dv")
                                emit_attn_layer.zktok[tt] = bld.now("dv")
                        else:
                            qh = oc
                            bld.wait("scalar", "dv", emit_attn_layer.zktok[tt])
                            tac = bld.op("scalar", lambda e, b=bank, tt=tt, qh=qh:
                                         e.activation(Eqp[tt][:, qh * 512:(qh + 1) * 512],
                                                      b, AF.Exp, bias=zkl_sb[:, tt:tt + 1],
                                                      scale=1.0), inc="ac")
                            btok[bk] = ("ac", tac)
                else:
                    for mt in range(4):
                        vt = (oc - 4) * 4 + mt
                        for nn2 in range(2):
                            bk = 4 + (mt * 2 + nn2) % 2
                            bank = PS[:, bk * 512:bk * 512 + 512]
                            bank_wait(bk)
                            for kt in range(8):
                                bld.op("tensor", lambda e, b=bank, kt=kt, mt=mt,
                                       nn2=nn2, mybuf=mybuf:
                                       e.matmul(b, (wsb if mybuf == 0 else wsb2)[:, kt * 512 + mt * 128:kt * 512 + mt * 128 + 128],
                                                ha_bf[kt][:, nn2 * 512:(nn2 + 1) * 512],
                                                start=(kt == 0), stop=(kt == 7)),
                                       inc="tm" if kt == 7 else None)
                            gtok = bld.now("tm")
                            last_tm = gtok
                            if mt == 0 and nn2 == 0 and oci + 1 < len(ocs):
                                loads[oci + 1] = load_w(ocs[oci + 1], emit_attn_layer.ci)
                                emit_attn_layer.ci += 1
                            bld.wait("vector", "tm", gtok)
                            tdv = bld.op("vector", lambda e, b=bank, vt=vt, tc=tc, nn2=nn2:
                                         e.tensor_copy(v_sb[vt][:, tc + nn2 * 512:tc + (nn2 + 1) * 512], b),
                                         inc="dv")
                            btok[bk] = ("dv", tdv)
                wtok[mybuf] = last_tm
            bld.W("tensor", "ac", "dv")
            for ct in range(8):
                for half in range(2):
                    bk = (2 * ct + half) % 8
                    bank = PS[:, bk * 512:bk * 512 + 512]
                    bank_wait(bk)
                    for tt in range(8):
                        bld.op("tensor", lambda e, b=bank, ct=ct, half=half, tt=tt:
                               e.matmul(b, Eqp[tt][:, ct * 128:ct * 128 + 128],
                                        Ek[tt][:, half * 512:(half + 1) * 512],
                                        start=(tt == 0), stop=(tt == 7)),
                               inc="tm" if tt == 7 else None)
                gtok = bld.now("tm")
                bld.wait("vector", "tm", gtok)
                pcols = PS[:, ((2 * ct) % 8) * 512:((2 * ct) % 8) * 512 + 1024]
                if c == 0:
                    tdv = bld.op("vector", lambda e, ct=ct, pcols=pcols: e.tensor_copy(
                        Msb[ct][:, :], pcols), inc="dv")
                else:
                    tdv = bld.op("vector", lambda e, ct=ct, pcols=pcols: e.tensor_tensor(
                        Msb[ct][:, :], pcols, Msb[ct][:, :], ALU.add), inc="dv")
                btok[(2 * ct) % 8] = ("dv", tdv)
                btok[(2 * ct + 1) % 8] = ("dv", tdv)
            for half in range(2):
                bk = 4 + half
                bank_wait(bk)
                for tt in range(8):
                    bld.op("tensor", lambda e, half=half, tt=tt: e.matmul(
                        PS[0:1, 2048 + half * 512:2560 + half * 512],
                        zkb_sb[:, tt:tt + 1], Eqp[tt][:, half * 512:(half + 1) * 512],
                        start=(tt == 0), stop=(tt == 7)),
                        inc="tm" if tt == 7 else None)
            gtok = bld.now("tm")
            bld.wait("vector", "tm", gtok)
            tdv = bld.op("vector", lambda e: e.tensor_tensor(
                zq_acc[0:1, :], PS[0:1, 2048:3072], zq_acc[0:1, :], ALU.add),
                inc="dv")
            btok[4] = ("dv", tdv)
            btok[5] = ("dv", tdv)
        bld.W("vector", "dv")
        bld.op("vector", lambda e: e.tensor_copy(zq_bf[0:1, 0:1024], zq_acc[:, :]),
               inc="dv")
        bld.W("sync", "dv")
        for ct in range(8):
            bld.dma("sync", ar_in[ct * 128:(ct + 1) * 128, :], Msb[ct][:, :], "dS")
        bld.dma("sync", ar_in[1024:1025, :], zq_bf[0:1, 0:1024], "dS")
        bld.W("gpsimd", "dS")
        ccname = f"cc{l}"
        bld.op("gpsimd", lambda e: e.collective_compute(
            "AllReduce", ALU.add, replica_groups=GROUPS,
            ins=[ar_in.ap().opt()], outs=[ar_out.ap().opt()]), inc=ccname)
        bld.W("sync", ccname)
        for ct in range(8):
            bld.dma("sync", Msb[ct][:, :], ar_out[ct * 128:(ct + 1) * 128, :], "dL")
        bld.dma("sync", zq_bf[0:1, 0:1024], ar_out[1024:1025, :], "dL")
        bld.W("vector", "dL")
        bld.op("vector", lambda e: e.reciprocal(rows1[0:1, 0:1024], zq_bf[0:1, :]),
               inc="dv")
        tdv = bld.now("dv")
        bld.wait("tensor", "dv", tdv)
        bank_wait(6)
        for i in range(8):
            bld.op("tensor", lambda e, i=i: e.transpose(
                PS[:, 3072 + i:3073 + i], rows1[0:1, i * 128:(i + 1) * 128],
                id1_f[:, :]), inc="tm" if i == 7 else None)
        ttm = bld.now("tm")
        bld.wait("vector", "tm", ttm)
        bld.op("vector", lambda e: e.tensor_copy(zqt[:, :], PS[:, 3072:3080]),
               inc="dv")
        bld.W("vector", "dv")
        tz = None
        for ct in range(8):
            tz = bld.op("vector", lambda e, ct=ct: e.tensor_scalar(
                Msb[ct][:, :], Msb[ct][:, :], zqt[:, ct:ct + 1], None,
                op0=ALU.mult), inc="dv")
        btok[6] = ("dv", tz)
        bld.wait("tensor", "dv", tz)
        for nn in range(2):
            ci = emit_attn_layer.ci
            buf = wsb if ci % 2 == 0 else wsb2
            mybuf = ci % 2
            emit_attn_layer.ci += 1
            if wtok[mybuf] is not None:
                bld.wait("gpsimd", "tm", wtok[mybuf])
            tw = bld.dma("gpsimd",
                         buf[:, :].rearrange("p (k c) -> p k c", c=512),
                         a_wout[l].rearrange("(k p) c -> p k c", p=128)[:, :, nn * 512:(nn + 1) * 512],
                         "dW")
            bld.wait("tensor", "dW", tw)
            last_tm = None
            for dbi in range(4):
                db = nn * 4 + dbi
                bank_wait(0, 1)
                for j in range(8):
                    bld.op("tensor", lambda e, j=j, dbi=dbi, mybuf=mybuf: [
                        e.matmul(PS[:, j * 128:(j + 1) * 128],
                                 Msb[ct2][:, j * 128:j * 128 + 128],
                                 (wsb if mybuf == 0 else wsb2)[:, ct2 * 512 + dbi * 128:ct2 * 512 + dbi * 128 + 128],
                                 start=(ct2 == 0), stop=(ct2 == 7))
                        for ct2 in range(8)][-1],
                           inc="tm")
                gtok = bld.now("tm")
                last_tm = gtok
                bld.wait("vector", "tm", gtok)
                tdv = bld.op("vector", lambda e: e.tensor_copy(Ablk[:, :], PS[:, 0:1024]),
                             inc="dv")
                btok[0] = ("dv", tdv)
                btok[1] = ("dv", tdv)
                bld.wait("tensor", "dv", tdv)
                for nn2 in range(4):
                    bk = 2 + nn2 % 2
                    bank = PS[:, bk * 512:bk * 512 + 512]
                    bank_wait(bk)
                    for j in range(8):
                        bld.op("tensor", lambda e, b=bank, j=j, nn2=nn2: e.matmul(
                            b, Ablk[:, j * 128:(j + 1) * 128],
                            v_sb[j][:, nn2 * 512:(nn2 + 1) * 512],
                            start=(j == 0), stop=(j == 7)),
                            inc="tm" if j == 7 else None)
                    gtok = bld.now("tm")
                    last_tm = gtok
                    bld.wait("vector", "tm", gtok)
                    tdv = bld.op("vector", lambda e, b=bank, db=db, nn2=nn2:
                                 e.scalar_tensor_tensor(
                                     x_a[db][:, nn2 * 512:(nn2 + 1) * 512], b,
                                     bo_sb[:, db:db + 1],
                                     x_a[db][:, nn2 * 512:(nn2 + 1) * 512],
                                     op0=ALU.add, op1=ALU.add), inc="dv")
                    btok[bk] = ("dv", tdv)
            wtok[mybuf] = last_tm
        bld.gate()
        emit_attn_layer.btok = {}
        emit_attn_layer.wtok = [None, None]
    emit_attn_layer.btok = {}
    emit_attn_layer.wtok = [None, None]
    emit_attn_layer.ci = 0
    emit_attn_layer.zktok = [0] * 8

    for ct in range(8):
        bld.dma("sync", x_a[ct][:, :], xn_park[ct * 128:(ct + 1) * 128, :], "dL")
    bld.W("vector", "dL")
    bld.W("tensor", "dL")

    for l in range(n_attn):
        emit_attn_layer(l)
        if xal is not None:
            for ct in range(8):
                bld.dma("sync", xal[l][ct * 128:(ct + 1) * 128, :], x_a[ct][:, :], "dS")
            bld.W("sync", "dS")
        if l == 0 and "s1_dump" in dumps:
            bld.gate()
            for ct in range(8):
                bld.dma("sync", dump_t["s1_dump"][ct * 128:(ct + 1) * 128, :],
                        x_a[ct][:, :], "dS")
            bld.gate()
    if "s_dump" in dumps:
        bld.gate()
        for ct in range(8):
            bld.dma("sync", dump_t["s_dump"][ct * 128:(ct + 1) * 128, :],
                    x_a[ct][:, :], "dS")
        bld.gate()

    for nn in range(4):
        for ct in range(8):
            bld.W("sync", "dv")
            bld.dma("sync", scrA[:, 0:512],
                    xcore[ct * 128:(ct + 1) * 128, H0 + nn * 512:H0 + (nn + 1) * 512],
                    "dL")
            bld.dma("sync", scrB[:, 0:512],
                    g_park[ct * 128:(ct + 1) * 128, nn * 512:(nn + 1) * 512], "dL")
            bld.W("vector", "dL")
            bld.W("vector", "dS")
            bld.op("vector", lambda e, ct=ct, nn=nn: e.tensor_tensor(
                scrC[:, 0:512], x_a[ct][:, nn * 512:(nn + 1) * 512], scrB[:, 0:512],
                ALU.mult))
            bld.op("vector", lambda e: e.tensor_tensor(
                scrC[:, 0:512], scrC[:, 0:512], scrA[:, 0:512], ALU.add), inc="dv")
            bld.W("sync", "dv")
            bld.dma("sync", out_fm[ct * 128:(ct + 1) * 128, nn * 512:(nn + 1) * 512],
                    scrC[:, 0:512], "dS")
    bld.W("sync", "dS")
    bld.W("gpsimd", "dW")

    bld.replay()
    astack.close()
    pstack.close()
    bld.close()
    return nc



def kernel(**inputs):
    in_maps = [stage_core(inputs, p) for p in range(NCORES)]
    nc = build_program()
    res = run_bass_kernel_spmd(nc, in_maps, core_ids=list(range(NCORES)))
    out = np.empty((B, N, D), dtype=np.float32)
    for p in range(NCORES):
        b, pos = divmod(p, 4)
        t0 = pos * TOK
        out[b, t0:t0 + TOK, :] = res.results[p]["out_fm"].T
    return out


# version 09
# speedup vs baseline: 1.1608x; 1.1608x over previous
from collections import defaultdict
from contextlib import ExitStack

import numpy as np
import ml_dtypes

import concourse.bass as bass
import concourse.mybir as mybir
from concourse.bass_utils import run_bass_kernel_spmd

F32 = mybir.dt.float32
BF16 = mybir.dt.bfloat16
AF = mybir.ActivationFunctionType
ALU = mybir.AluOpType
AX = mybir.AxisListType.X

NCORES = 8
B, N, D = 2, 8192, 1024
TOK = 2048
H0 = 2176
T0 = H0 + TOK
XH_W = 1024
XB_W = T0 - XH_W
WIN = 512
HID = 2730
NEG = -1.0e30

QKV_OFF = {0: 128, 1: 1152}
ATT_OFF = {0: 896, 1: 2048}
OUT_OFF = {0: 1024, 1: 2176}
NWIN = {0: 8, 1: 6}
NQT = {0: 26, 1: 17}

ENGINES = ("tensor", "vector", "scalar", "gpsimd", "sync")
GROUPS = [[0, 1, 2, 3], [4, 5, 6, 7]]



def _rotary_tables(t0):
    t = np.arange(t0 - H0, t0 + TOK, dtype=np.float64)
    half = np.arange(0, 64, 2, dtype=np.float64)
    inv_freq = 1.0 / 10000.0 ** (half / 64.0)
    d = np.arange(64)
    f_d = inv_freq[d % 32]
    freqs = f_d[:, None] * t[None, :]
    base = (half + 0.4 * 64.0) / (1.4 * 64.0)
    base_d = base[d % 32]
    power = (t - N // 2) / float(WIN // 2)
    scale = base_d[:, None] ** power[None, :]
    cos, sin = np.cos(freqs), np.sin(freqs)
    tabs = [cos * scale, sin * scale, cos / scale, sin / scale]
    return [np.ascontiguousarray(np.concatenate([x, x], 0).astype(np.float32))
            for x in tabs]


def _masks(t0, l):
    nqt = NQT[l]
    out = np.zeros((nqt, 128, 640), dtype=np.float64)
    q0 = t0 - (H0 - ATT_OFF[l])
    for j in range(nqt):
        qg = q0 + 128 * j + np.arange(128)
        ws = (qg[0] // WIN) * WIN
        ti = (qg[0] - ws) // 128
        kpos = ws - WIN + 128 * ti + np.arange(640)
        ok = (kpos[None, :] >= 0) & (kpos[None, :] <= qg[:, None]) \
            & (qg[:, None] <= kpos[None, :] + WIN)
        out[j] = np.where(ok, 0.0, NEG)
    return out.astype(ml_dtypes.bfloat16)


def _rot_perm():
    P = np.zeros((128, 128), dtype=np.float32)
    for i in range(128):
        if i % 64 < 32:
            P[i, i + 32] = -1.0
        else:
            P[i, i - 32] = 1.0
    return np.ascontiguousarray(P.T).astype(ml_dtypes.bfloat16)


def _col_layout(vec, tiles):
    v = np.zeros(128 * tiles, dtype=np.float32)
    v[:len(vec)] = vec
    return np.ascontiguousarray(v.reshape(tiles, 128).T)


def stage_core(inputs, p):
    b, pos = divmod(p, 4)
    t0 = pos * TOK
    xT = np.ascontiguousarray(np.asarray(inputs["L0_attn_x"][b]).T)
    lo = t0 - H0
    if lo >= 0:
        xc = xT[:, lo:t0 + TOK]
    else:
        xc = np.concatenate(
            [np.zeros((D, -lo), dtype=np.float32), xT[:, :t0 + TOK]], axis=1)
    d = {"xcore": np.ascontiguousarray(xc)}
    d["f_wqkv"] = np.ascontiguousarray(inputs["L0_ffn_wqkv"])
    d["f_wo"] = np.ascontiguousarray(inputs["L0_ffn_wo"])
    d["f_w1"] = np.ascontiguousarray(inputs["L0_ffn_w1"])
    d["f_w2"] = np.ascontiguousarray(inputs["L0_ffn_w2"])
    d["f_qs"] = np.ascontiguousarray(
        np.concatenate([inputs["L0_ffn_q_scale"]] * 2, axis=1)[:, :, None])
    d["f_ks"] = np.ascontiguousarray(
        np.concatenate([inputs["L0_ffn_k_scale"]] * 2, axis=1)[:, :, None])
    d["f_lng"] = np.stack([_col_layout(inputs["L0_ffn_ln_g"][l], 8) for l in range(2)])
    d["f_lnb"] = np.stack([_col_layout(inputs["L0_ffn_ln_b"][l], 8) for l in range(2)])
    d["f_b1a"] = np.stack([_col_layout(inputs["L0_ffn_b1"][l][:HID], 22) for l in range(2)])
    d["f_b1g"] = np.stack([_col_layout(inputs["L0_ffn_b1"][l][HID:], 22) for l in range(2)])
    d["f_b2"] = np.stack([_col_layout(inputs["L0_ffn_b2"][l], 8) for l in range(2)])
    d["a_wqkv"] = np.ascontiguousarray(inputs["L0_attn_w_qkv"])
    d["a_wout"] = np.ascontiguousarray(inputs["L0_attn_w_out"])
    d["a_bout"] = np.stack([_col_layout(inputs["L0_attn_b_out"][l], 8) for l in range(6)])
    d["a_gamma"] = np.stack([_col_layout(inputs["L0_attn_gamma"][l], 8) for l in range(6)])
    d["a_beta"] = np.stack([_col_layout(inputs["L0_attn_beta"][l], 8) for l in range(6)])
    rt = _rotary_tables(t0)
    d["rot_q1"], d["rot_q2"], d["rot_k1"], d["rot_k2"] = rt
    d["mask0"] = np.ascontiguousarray(_masks(t0, 0))
    d["mask1"] = np.ascontiguousarray(_masks(t0, 1))
    d["rotp"] = _rot_perm()
    d["ident"] = np.ascontiguousarray(np.eye(128, dtype=np.float32)).astype(
        ml_dtypes.bfloat16)
    d["bo_col"] = np.ascontiguousarray(np.stack(
        [np.r_[np.ones(64), np.zeros(64)], np.r_[np.zeros(64), np.ones(64)]],
        axis=1)).astype(ml_dtypes.bfloat16)
    d["bo_row"] = np.ascontiguousarray(np.stack(
        [np.r_[np.ones(64), np.zeros(64)], np.r_[np.zeros(64), np.ones(64)]],
        axis=0)).astype(np.float32)
    return d



class Bld:
    def __init__(self, nc):
        self.nc = nc
        self.q = {e: [] for e in ENGINES}
        self.cnt = defaultdict(int)
        self.sems = {}
        self._sem_stack = ExitStack()

    def sem(self, name):
        if name not in self.sems:
            self.sems[name] = self._sem_stack.enter_context(self.nc.semaphore(name))
        return self.sems[name]

    def op(self, eng, fn, inc=None, amt=1):
        if inc is not None:
            s = self.sem(inc)
            self.cnt[inc] += amt

            def wrapped(e, fn=fn, s=s, amt=amt):
                fn(e).then_inc(s, amt)
            self.q[eng].append(wrapped)
            return self.cnt[inc]
        self.q[eng].append(fn)
        return None

    def dma(self, eng, out, in_, inc, **kw):
        return self.op(eng, lambda e, o=out, i=in_, kw=kw: e.dma_start(out=o, in_=i, **kw),
                       inc=inc, amt=16)

    def wait(self, eng, name, val):
        if val <= 0:
            return
        s = self.sem(name)
        self.q[eng].append(lambda e, s=s, v=val: e.wait_ge(s, v))

    def W(self, eng, *names):
        for n in names:
            self.wait(eng, n, self.cnt[n])

    def gate(self):
        allsems = ["tm", "dv", "ac", "gp", "dW", "dL", "dS"]
        for e in ENGINES:
            self.W(e, *allsems)

    def now(self, name):
        return self.cnt[name]

    def replay(self):
        with self.nc.Block() as block:
            for name in ENGINES:
                ops = self.q[name]
                if not ops:
                    continue

                def runner(e, ops=tuple(ops)):
                    for f in ops:
                        f(e)
                getattr(block, name)(runner)
        self.q = {e: [] for e in ENGINES}

    def close(self):
        self._sem_stack.close()


def x_slices(c0, w):
    out = []
    if c0 < XH_W:
        ww = min(w, XH_W - c0)
        out.append(("H", c0, ww))
        c0 += ww
        w -= ww
    if w > 0:
        out.append(("B", c0 - XH_W, w))
    return out



def build_program(dev_dumps=(), n_ffn=2, n_attn=6):
    nc = bass.Bass("TRN2", target_bir_lowering=False)
    dumps = set(dev_dumps)

    xcore = nc.dram_tensor("xcore", [D, T0], F32, kind="ExternalInput")
    f_wqkv = nc.dram_tensor("f_wqkv", [2, D, 1536], F32, kind="ExternalInput")
    f_wo = nc.dram_tensor("f_wo", [2, 512, D], F32, kind="ExternalInput")
    f_w1 = nc.dram_tensor("f_w1", [2, D, 5460], F32, kind="ExternalInput")
    f_w2 = nc.dram_tensor("f_w2", [2, HID, D], F32, kind="ExternalInput")
    f_qs = nc.dram_tensor("f_qs", [2, 128, 1], F32, kind="ExternalInput")
    f_ks = nc.dram_tensor("f_ks", [2, 128, 1], F32, kind="ExternalInput")
    f_lng = nc.dram_tensor("f_lng", [2, 128, 8], F32, kind="ExternalInput")
    f_lnb = nc.dram_tensor("f_lnb", [2, 128, 8], F32, kind="ExternalInput")
    f_b1a = nc.dram_tensor("f_b1a", [2, 128, 22], F32, kind="ExternalInput")
    f_b1g = nc.dram_tensor("f_b1g", [2, 128, 22], F32, kind="ExternalInput")
    f_b2 = nc.dram_tensor("f_b2", [2, 128, 8], F32, kind="ExternalInput")
    a_wqkv = nc.dram_tensor("a_wqkv", [6, D, 3072], F32, kind="ExternalInput")
    a_wout = nc.dram_tensor("a_wout", [6, D, D], F32, kind="ExternalInput")
    a_bout = nc.dram_tensor("a_bout", [6, 128, 8], F32, kind="ExternalInput")
    a_gamma = nc.dram_tensor("a_gamma", [6, 128, 8], F32, kind="ExternalInput")
    a_beta = nc.dram_tensor("a_beta", [6, 128, 8], F32, kind="ExternalInput")
    rot_t = [nc.dram_tensor(f"rot_{s}", [128, T0], F32, kind="ExternalInput")
             for s in ("q1", "q2", "k1", "k2")]
    mask_d = {0: nc.dram_tensor("mask0", [NQT[0], 128, 640], BF16, kind="ExternalInput"),
              1: nc.dram_tensor("mask1", [NQT[1], 128, 640], BF16, kind="ExternalInput")}
    rotp = nc.dram_tensor("rotp", [128, 128], BF16, kind="ExternalInput")
    ident = nc.dram_tensor("ident", [128, 128], BF16, kind="ExternalInput")
    bo_col = nc.dram_tensor("bo_col", [128, 2], BF16, kind="ExternalInput")
    bo_row = nc.dram_tensor("bo_row", [2, 128], F32, kind="ExternalInput")

    out_fm = nc.dram_tensor("out_fm", [D, TOK], F32, kind="ExternalOutput")
    xn_park = nc.dram_tensor("xn_park", [D, TOK], F32, kind="Internal")
    g_park = nc.dram_tensor("g_park", [D, TOK], F32, kind="Internal")
    ar_in = nc.dram_tensor("ar_in", [1025, 1024], BF16, kind="Internal")
    ar_out = nc.dram_tensor("ar_out", [1025, 1024], BF16, kind="Internal")

    dbg = nc.dram_tensor("dbg", [128, 4 * 640], F32, kind="ExternalOutput") \
        if "dbg" in dumps else None
    dbg2 = nc.dram_tensor("dbg2", [1024, 1664], F32, kind="ExternalOutput") \
        if "dbg2" in dumps else None
    xal = nc.dram_tensor("xal", [6, D, TOK], F32, kind="ExternalOutput") \
        if "xal" in dumps else None
    dump_t = {}
    for name in dumps:
        if name in ("dbg", "dbg2", "xal"):
            continue
        shp = {"xn_dump": [D, T0], "xl1_dump": [D, T0], "xl2_dump": [D, T0],
               "xa1_dump": [D, T0], "g_dump": [D, TOK],
               "q_dump": [512, T0], "k_dump": [512, T0],
               "v_dump": [T0, 512], "o_dump": [512, T0],
               "s1_dump": [D, TOK], "s_dump": [D, TOK]}[name]
        dump_t[name] = nc.dram_tensor(name, shp, F32, kind="ExternalOutput")

    bld = Bld(nc)
    pstack = ExitStack()

    def sb(stack, name, shape, dt):
        return stack.enter_context(nc.sbuf_tensor(name, shape, dt))

    def psum(stack, name, shape, dt=F32):
        return stack.enter_context(nc.psum_tensor(name, shape, dt))

    id_bf = sb(pstack, "id_bf", [128, 128], BF16)
    rotp_sb = sb(pstack, "rotp_sb", [128, 128], BF16)
    bo_col_sb = sb(pstack, "bo_col_sb", [128, 2], BF16)
    bo_row_sb = sb(pstack, "bo_row_sb", [2, 128], F32)
    ones1_f = sb(pstack, "ones1_f", [1, 128], F32)
    onesc_bf = sb(pstack, "onesc_bf", [128, 1], BF16)
    id1_f = sb(pstack, "id1_f", [1, 1], F32)
    eps_f = sb(pstack, "eps_f", [128, 1], F32)
    nc.const_aps.register(F32, 1e-5, eps_f[:, 0:1])
    scrA = sb(pstack, "scrA", [128, 512], F32)
    scrB = sb(pstack, "scrB", [128, 512], F32)
    scrC = sb(pstack, "scrC", [128, 512], F32)
    rowst = sb(pstack, "rowst", [128, 16], F32)
    rows1 = sb(pstack, "rows1", [1, 2048], F32)
    rows2 = sb(pstack, "rows2", [1, 2048], F32)

    bld.dma("gpsimd", id_bf[:, :], ident[:, :], "dW")
    bld.dma("gpsimd", rotp_sb[:, :], rotp[:, :], "dW")
    bld.dma("gpsimd", bo_col_sb[:, :], bo_col[:, :], "dW")
    bld.dma("gpsimd", bo_row_sb[:, :], bo_row[:, :], "dW")
    bld.op("gpsimd", lambda e: e.memset(ones1_f[:, :], 1.0))
    bld.op("gpsimd", lambda e: e.memset(onesc_bf[:, :], 1.0))
    bld.op("gpsimd", lambda e: e.memset(id1_f[:, :], 1.0))
    bld.op("gpsimd", lambda e: e.memset(eps_f[:, :], 1e-5), inc="gp")
    bld.gate()

    fstack = ExitStack()
    xH = [sb(fstack, f"xH{i}", [128, XH_W], BF16) for i in range(8)]
    xB = [sb(fstack, f"xB{i}", [128, XB_W], F32) for i in range(8)]

    def xap(region, ct, c0, w):
        t = (xH if region == "H" else xB)[ct]
        return t[:, c0:c0 + w]

    def x_read(ct, c0, w):
        return [(xap(r, ct, s, ww), ww) for r, s, ww in x_slices(c0, w)]

    def dump_x(name):
        bld.gate()
        for ct in range(8):
            bld.dma("gpsimd", dump_t[name][ct * 128:(ct + 1) * 128, 0:XH_W],
                    xH[ct][:, :], "dW")
            bld.dma("sync", dump_t[name][ct * 128:(ct + 1) * 128, XH_W:T0],
                    xB[ct][:, :], "dS")
        bld.gate()


    def emit_top_ln():
        ts_ = ExitStack()
        topx = [sb(ts_, f"topx{i}", [128, 512], F32) for i in range(8)]
        topb = [sb(ts_, f"topb{i}", [128, 512], BF16) for i in range(8)]
        tops = [sb(ts_, f"tops{i}", [128, 512], BF16) for i in range(8)]
        ps = psum(ts_, "ps_top", [128, 2048], F32)
        chunks = [(i * 512, 512) for i in range(8)] + [(4096, 128)]
        for (c0, w) in chunks:
            for ct in range(8):
                bld.dma("sync", topx[ct][:, 0:w],
                        xcore[ct * 128:(ct + 1) * 128, c0:c0 + w], "dL")
            bld.W("vector", "dL")
            bld.W("vector", "tm")
            for ct in range(8):
                bld.op("vector", lambda e, ct=ct, w=w: e.tensor_copy(
                    topb[ct][:, 0:w], topx[ct][:, 0:w]))
            for ct in range(8):
                bld.op("vector", lambda e, ct=ct, w=w: e.tensor_tensor(
                    tops[ct][:, 0:w], topb[ct][:, 0:w], topb[ct][:, 0:w], ALU.mult),
                    inc="dv" if ct == 7 else None)
            bld.W("tensor", "dv")
            for ct in range(8):
                bld.op("tensor", lambda e, ct=ct, w=w: e.matmul(
                    ps[0:1, 0:w], onesc_bf[:, 0:1], topb[ct][:, 0:w],
                    start=(ct == 0), stop=(ct == 7)))
            for ct in range(8):
                bld.op("tensor", lambda e, ct=ct, w=w: e.matmul(
                    ps[0:1, 512:512 + w], onesc_bf[:, 0:1], tops[ct][:, 0:w],
                    start=(ct == 0), stop=(ct == 7)),
                    inc="tm" if ct == 7 else None)
            bld.W("vector", "tm")
            bld.op("vector", lambda e, w=w: e.tensor_scalar(
                rows1[0:1, 0:w], ps[0:1, 0:w], 1.0 / D, None, op0=ALU.mult))
            bld.op("vector", lambda e, w=w: e.tensor_scalar(
                rows1[0:1, 1024:1024 + w], ps[0:1, 512:512 + w], 1.0 / D, None,
                op0=ALU.mult), inc="dv")
            bld.W("vector", "dv")
            bld.op("vector", lambda e, w=w: e.tensor_tensor(
                rows2[0:1, 0:w], rows1[0:1, 0:w], rows1[0:1, 0:w], ALU.mult),
                inc="dv")
            bld.W("vector", "dv")
            bld.op("vector", lambda e, w=w: e.tensor_tensor(
                rows2[0:1, 1024:1024 + w], rows1[0:1, 1024:1024 + w],
                rows2[0:1, 0:w], ALU.subtract), inc="dv")
            bld.W("scalar", "dv")
            bld.op("scalar", lambda e, w=w: e.activation(
                rows2[0:1, 0:w], rows2[0:1, 1024:1024 + w], AF.Ln, bias=1e-5,
                scale=1.0))
            bld.op("scalar", lambda e, w=w: e.activation(
                rows2[0:1, 0:w], rows2[0:1, 0:w], AF.Exp, bias=0.0, scale=-0.5),
                inc="ac")
            bld.W("vector", "dv")
            bld.op("vector", lambda e, w=w: e.tensor_scalar(
                rows1[0:1, 1024:1024 + w], rows1[0:1, 0:w], -1.0, None,
                op0=ALU.mult), inc="dv")
            bld.W("tensor", "ac", "dv")
            bld.op("tensor", lambda e, w=w: e.matmul(
                ps[:, 1024:1024 + w], ones1_f[0:1, :], rows1[0:1, 1024:1024 + w],
                start=True, stop=True))
            bld.op("tensor", lambda e, w=w: e.matmul(
                ps[:, 1536:1536 + w], ones1_f[0:1, :], rows2[0:1, 0:w],
                start=True, stop=True), inc="tm")
            bld.W("vector", "tm")
            for ct in range(8):
                bld.op("vector", lambda e, ct=ct, w=w: e.scalar_tensor_tensor(
                    topx[ct][:, 0:w], ps[:, 1024:1024 + w], 1.0, topx[ct][:, 0:w],
                    op0=ALU.mult, op1=ALU.add))
                (r, s, ww), = x_slices(c0, w)
                bld.op("vector", lambda e, ct=ct, r=r, s=s, ww=ww: e.tensor_tensor(
                    xap(r, ct, s, ww), topx[ct][:, 0:ww], ps[:, 1536:1536 + ww],
                    ALU.mult), inc="dv" if ct == 7 else None)
            bld.W("sync", "dv")
            if c0 + w > H0:
                s0 = max(c0, H0)
                for ct in range(8):
                    bld.dma("sync", xn_park[ct * 128:(ct + 1) * 128, s0 - H0:c0 + w - H0],
                            xB[ct][:, s0 - XH_W:c0 + w - XH_W], "dS")
        bld.gate()
        ts_.close()

    def qtiles_of_window(l, w):
        ws = QKV_OFF[l] + 512 * w
        out = []
        for j in range(NQT[l]):
            c = ATT_OFF[l] + 128 * j
            if ws <= c < ws + 512:
                out.append((j, c))
        return out

    def emit_ffn_pass1(l):
        p1 = ExitStack()
        ps_qkv = psum(p1, f"psqkv{l}", [128, 1024], F32)
        ps_ms = psum(p1, f"psms{l}", [128, 512], F32)
        ps_rot = psum(p1, f"psrot{l}", [128, 512], F32)
        ps_sim = psum(p1, f"pssim{l}", [128, 640], F32)
        ps_aT = psum(p1, f"psaT{l}", [128, 640], BF16)
        ps_o = psum(p1, f"pso{l}", [128, 512], F32)
        h_bf = [sb(p1, f"hbf{l}_{i}", [128, 512], BF16) for i in range(8)]
        kring = [sb(p1, f"kring{l}_{i}", [128, 1024], BF16) for i in range(4)]
        vring = sb(p1, f"vring{l}", [128, 8 * 512], BF16)
        qbuf = [sb(p1, f"qbuf{l}_{i}", [128, 512], BF16) for i in range(4)]
        o_sb = [sb(p1, f"osb{l}_{i}", [128, 512], BF16) for i in range(4)]
        masked = sb(p1, f"masked{l}", [128, 640], F32)
        attn_bf = sb(p1, f"attn_bf{l}", [128, 640], BF16)
        attnT_sb = sb(p1, f"attnT_sb{l}", [128, 640], BF16)
        tabs = [sb(p1, f"tab{l}_{i}", [128, 512], F32) for i in range(2)]
        mask_sb = sb(p1, f"mask_sb{l}", [128, 640], BF16)
        wqkv_sb = sb(p1, f"wqkv_sb{l}", [128, 4096], BF16)
        wo_sb = sb(p1, f"wo_sb{l}", [128, 4096], BF16)
        sq_bf = sb(p1, f"sq_bf{l}", [128, 512], BF16)
        q1_bf = sb(p1, f"q1_bf{l}", [128, 512], BF16)
        qsk_sb = sb(p1, f"qsk_sb{l}", [128, 2], F32)
        rowsp = sb(p1, f"rowsp{l}", [2, 1024], F32)

        def emit_hcast(w):
            ws = QKV_OFF[l] + 512 * w
            for ct in range(8):
                src0 = ws if ct < 4 else ws - 1
                off = 0
                for ap, ww in x_read(ct, src0, 512):
                    bld.op("vector", lambda e, ct=ct, ap=ap, ww=ww, off=off:
                           e.tensor_copy(h_bf[ct][:, off:off + ww], ap),
                           inc="dv" if (ct == 7 and off + ww == 512) else None)
                    off += ww

        pt = {"sq_tr": None, "msA": None, "msB": None, "rot": None,
              "q1": None, "bc": None, "tab": None}

        def emit_post(kind, mt, bank, grp_tm, slot):
            t1, t2 = tabs[0], tabs[1]
            qs_col = qsk_sb[:, 0:1] if kind == "q" else qsk_sb[:, 1:2]
            dst = qbuf[mt][:, 0:512] if kind == "q" \
                else kring[mt][:, slot * 512:(slot + 1) * 512]
            bld.wait("scalar", "tm", grp_tm)
            if pt["msA"] is not None:
                bld.wait("scalar", "tm", pt["msA"])
            sq_ac = bld.op("scalar", lambda e, bank=bank: e.activation(
                sq_bf[:, :], bank, AF.Square), inc="ac")
            bld.wait("tensor", "ac", sq_ac)
            if pt["bc"] is not None:
                bld.wait("tensor", "dv", pt["bc"])
            bld.op("tensor", lambda e: e.matmul(
                ps_ms[0:2, 0:512], bo_col_sb[:, :], sq_bf[:, :], start=True,
                stop=True), inc="tm")
            pt["msA"] = bld.now("tm")
            bld.wait("vector", "tm", pt["msA"])
            mx_dv = bld.op("vector", lambda e: e.tensor_scalar(
                rowsp[0:2, 0:512], ps_ms[0:2, 0:512], 1e-24, None, op0=ALU.max),
                inc="dv")
            bld.wait("scalar", "dv", mx_dv)
            bld.op("scalar", lambda e: e.activation(
                rowsp[0:2, 512:1024], rowsp[0:2, 0:512], AF.Ln, bias=0.0,
                scale=1.0))
            rs_ac = bld.op("scalar", lambda e: e.activation(
                rowsp[0:2, 512:1024], rowsp[0:2, 512:1024], AF.Exp, bias=0.0,
                scale=-0.5), inc="ac")
            bld.wait("tensor", "ac", rs_ac)
            bld.wait("tensor", "dv", mx_dv)
            bld.op("tensor", lambda e: e.matmul(
                ps_ms[:, 0:512], bo_row_sb[:, :], rowsp[0:2, 512:1024],
                start=True, stop=True), inc="tm")
            msB = bld.now("tm")
            bld.wait("vector", "tm", msB)
            if pt["tab"] is not None:
                bld.wait("vector", "dL", pt["tab"])
            bld.op("vector", lambda e: e.tensor_copy(scrB[:, 0:512], ps_ms[:, 0:512]))
            pt["bc"] = bld.now("dv") + 1
            q1_dv = bld.op("vector", lambda e, bank=bank, qs_col=qs_col:
                           e.scalar_tensor_tensor(q1_bf[:, :], bank, qs_col,
                                                  scrB[:, 0:512], op0=ALU.mult,
                                                  op1=ALU.mult), inc="dv")
            bld.wait("tensor", "dv", q1_dv)
            bld.op("tensor", lambda e: e.matmul(
                ps_rot[:, 0:512], rotp_sb[:, :], q1_bf[:, :], start=True,
                stop=True), inc="tm")
            rot_tm = bld.now("tm")
            bld.op("vector", lambda e, t1=t1: e.tensor_tensor(
                scrC[:, 0:512], q1_bf[:, :], t1[:, :], ALU.mult))
            bld.wait("vector", "tm", rot_tm)
            bld.op("vector", lambda e, t2=t2: e.tensor_tensor(
                scrB[:, 0:512], ps_rot[:, 0:512], t2[:, :], ALU.mult),
                inc="dv")
            pt["rot"] = bld.now("dv")
            bld.op("vector", lambda e, dst=dst: e.tensor_tensor(
                dst, scrC[:, 0:512], scrB[:, 0:512], ALU.add), inc="dv")
            return bld.now("dv")

        bld.dma("gpsimd", wo_sb[:, :].rearrange("p (k c) -> p k c", c=1024),
                f_wo[l].rearrange("(k p) c -> p k c", p=128), "dW")
        bld.dma("gpsimd", qsk_sb[:, 0:1], f_qs[l], "dW")
        bld.dma("gpsimd", qsk_sb[:, 1:2], f_ks[l], "dW")
        bld.W("vector", "dW")
        bld.W("tensor", "dW")

        emit_ffn_pass1.at = {"stt": [None, None], "exp": None, "tr": None,
                             "evac": None, "av": None, "oev": None}
        emit_ffn_pass1.qb = [None, None]
        nw = NWIN[l]
        emit_hcast(0)
        for w in range(nw):
            ws = QKV_OFF[l] + 512 * w
            slot = w % 2
            qts = qtiles_of_window(l, w)
            chunks = ([0] if qts else []) + [1, 2]
            for ch in chunks:
                if ch < 2:
                    if pt["rot"] is not None:
                        bld.wait("sync", "dv", pt["rot"])
                    for ti_ in range(2):
                        pt["tab"] = bld.dma("sync", tabs[ti_][:, :],
                                            rot_t[ch * 2 + ti_][:, ws:ws + 512], "dL")
                buf = wqkv_sb
                bld.W("gpsimd", "tm")
                bld.dma("gpsimd",
                        buf[:, :].rearrange("p (k c) -> p k c", c=512),
                        f_wqkv[l].rearrange("(k p) c -> p k c", p=128)[:, :, ch * 512:(ch + 1) * 512],
                        "dW")
                bld.W("tensor", "dW", "dv")
                if ch < 2:
                    for mt in range(4):
                        par = mt % 2
                        bank = ps_qkv[:, par * 512:par * 512 + 512]
                        if emit_ffn_pass1.qb[par] is not None:
                            bld.wait("tensor", "dv", emit_ffn_pass1.qb[par])
                        for kt in range(8):
                            bld.op("tensor", lambda e, b=bank, buf=buf, kt=kt, mt=mt:
                                   e.matmul(b, buf[:, kt * 512 + mt * 128:kt * 512 + mt * 128 + 128],
                                            h_bf[kt][:, 0:512],
                                            start=(kt == 0), stop=(kt == 7)),
                                   inc="tm" if kt == 7 else None)
                        grp_tm = bld.now("tm")
                        emit_ffn_pass1.qb[par] = emit_post(
                            "q" if ch == 0 else "k", mt, bank, grp_tm, slot)
                else:
                    for tt in range(4):
                        par = tt % 2
                        bank = ps_qkv[:, par * 512:par * 512 + 512]
                        if emit_ffn_pass1.qb[par] is not None:
                            bld.wait("tensor", "dv", emit_ffn_pass1.qb[par])
                        for kt in range(8):
                            bld.op("tensor", lambda e, b=bank, buf=buf, kt=kt, tt=tt:
                                   e.matmul(b, h_bf[kt][:, tt * 128:tt * 128 + 128],
                                            buf[:, kt * 512:(kt + 1) * 512],
                                            start=(kt == 0), stop=(kt == 7)),
                                   inc="tm" if kt == 7 else None)
                        bld.W("vector", "tm")
                        bld.op("vector", lambda e, b=bank, slot=slot, tt=tt:
                               e.tensor_copy(vring[:, (slot * 4 + tt) * 512:(slot * 4 + tt + 1) * 512], b),
                               inc="dv")
                        emit_ffn_pass1.qb[par] = bld.now("dv")
                        if "v_dump" in dumps:
                            bld.W("gpsimd", "dv")
                            bld.dma("gpsimd",
                                    dump_t["v_dump"][ws + tt * 128:ws + tt * 128 + 128, :],
                                    vring[:, (slot * 4 + tt) * 512:(slot * 4 + tt + 1) * 512],
                                    "dW")
            if "q_dump" in dumps and qts:
                bld.W("gpsimd", "dv")
                for mt in range(4):
                    bld.dma("gpsimd", dump_t["q_dump"][mt * 128:(mt + 1) * 128, ws:ws + 512],
                            qbuf[mt][:, :], "dW")
            if "k_dump" in dumps:
                bld.W("gpsimd", "dv")
                for mt in range(4):
                    bld.dma("gpsimd", dump_t["k_dump"][mt * 128:(mt + 1) * 128, ws:ws + 512],
                            kring[mt][:, slot * 512:(slot + 1) * 512], "dW")

            at = emit_ffn_pass1.at
            if qts:
                dvnow = bld.now("dv")
                at["stt"][0] = max(at["stt"][0] or 0, dvnow)
                at["stt"][1] = max(at["stt"][1] or 0, dvnow)
            for (j, qcol) in qts:
                qoff = qcol - ws
                ti = qoff // 128
                bld.W("sync", "dv")
                mtok = bld.dma("sync", mask_sb[:, :], mask_d[l][j], "dL")
                for h in range(8):
                    ct, half = divmod(h, 2)
                    rb = half * 64
                    par = h % 2
                    simps = ps_sim if par == 0 else ps_qkv
                    p1w = 512 - 128 * ti
                    prev_slot = (w - 1) % 2
                    if at["stt"][par] is not None:
                        bld.wait("tensor", "dv", at["stt"][par])
                    pieces = [(0, p1w, kring[ct][rb:rb + 64,
                                                prev_slot * 512 + 128 * ti:(prev_slot + 1) * 512])]
                    if ti > 0:
                        pieces.append((p1w, 512, kring[ct][rb:rb + 64,
                                                          slot * 512:slot * 512 + 128 * ti]))
                    pieces.append((512, 640, kring[ct][rb:rb + 64,
                                                       slot * 512 + 128 * ti:slot * 512 + 128 * ti + 128]))
                    for pi, (sc0, sc1, kap) in enumerate(pieces):
                        bld.op("tensor", lambda e, ct=ct, rb=rb, qoff=qoff,
                               sc0=sc0, sc1=sc1, kap=kap, simps=simps:
                               e.matmul(simps[:, sc0:sc1],
                                        qbuf[ct][rb:rb + 64, qoff:qoff + 128],
                                        kap, start=True, stop=True,
                                        tile_position=(rb, 0)),
                               inc="tm" if pi == len(pieces) - 1 else None)
                    sim_tm = bld.now("tm")
                    bld.wait("vector", "tm", sim_tm)
                    bld.wait("vector", "dL", mtok)
                    if at["exp"] is not None:
                        bld.wait("vector", "ac", at["exp"])
                    bld.op("vector", lambda e, simps=simps: e.scalar_tensor_tensor(
                        masked[:, :], simps[:, 0:640], 8.0, mask_sb[:, :],
                        op0=ALU.mult, op1=ALU.add))
                    at["stt"][par] = bld.op("vector", lambda e: e.reduce_max(
                        rowst[:, 0:1], masked[:, :], axis=AX, negate=True),
                        inc="dv")
                    bld.wait("scalar", "dv", at["stt"][par])
                    if at["tr"] is not None:
                        bld.wait("scalar", "tm", at["tr"])
                    at["exp"] = bld.op("scalar", lambda e: e.activation(
                        attn_bf[:, :], masked[:, :], AF.Exp, bias=rowst[:, 0:1],
                        scale=1.0), inc="ac")
                    bld.wait("tensor", "ac", at["exp"])
                    if at["evac"] is not None:
                        bld.wait("tensor", "dv", at["evac"])
                    for kb in range(5):
                        bld.op("tensor", lambda e, kb=kb: e.transpose(
                            ps_aT[:, kb * 128:(kb + 1) * 128],
                            attn_bf[:, kb * 128:(kb + 1) * 128], id_bf[:, :]),
                            inc="tm" if kb == 4 else None)
                    at["tr"] = bld.now("tm")
                    bld.wait("vector", "tm", at["tr"])
                    at["evac"] = bld.op("vector", lambda e: e.tensor_copy(
                        attnT_sb[:, :], ps_aT[:, :]), inc="dv")
                    bld.wait("tensor", "dv", at["evac"])
                    if h == 0 and at["oev"] is not None:
                        bld.wait("tensor", "dv", at["oev"])
                    rps = ps_ms if h < 4 else ps_rot
                    rcol = (h % 4) * 128
                    for kb in range(5):
                        bld.op("tensor", lambda e, kb=kb, rps=rps, rcol=rcol:
                               e.matmul(rps[0:1, rcol:rcol + 128], onesc_bf[:, 0:1],
                                        attnT_sb[:, kb * 128:(kb + 1) * 128],
                                        start=(kb == 0), stop=(kb == 4)))
                    for kb in range(5):
                        kcol = ws - 512 + 128 * ti + 128 * kb
                        wk = (kcol - QKV_OFF[l]) // 512
                        twt = (kcol - QKV_OFF[l] - 512 * wk) // 128
                        vbase = ((wk % 2) * 4 + twt) * 512
                        bld.op("tensor", lambda e, kb=kb, vbase=vbase, h=h, rb=rb, ct=ct:
                               e.matmul(ps_o[rb:rb + 64, ct * 128:ct * 128 + 128],
                                        vring[:, vbase + h * 64:vbase + h * 64 + 64],
                                        attnT_sb[:, kb * 128:(kb + 1) * 128],
                                        start=(kb == 0), stop=(kb == 4),
                                        tile_position=(0, rb)),
                               inc="tm" if kb == 4 else None)
                    at["av"] = bld.now("tm")
                bld.wait("scalar", "tm", at["av"])
                bld.op("scalar", lambda e: e.activation(
                    scrB[0:1, 0:512], ps_ms[0:1, 0:512], AF.Ln, bias=0.0, scale=1.0))
                bld.op("scalar", lambda e: e.activation(
                    scrC[0:1, 0:512], ps_rot[0:1, 0:512], AF.Ln, bias=0.0, scale=1.0),
                    inc="ac")
                lr_ac = bld.now("ac")
                bld.wait("scalar", "ac", lr_ac)
                bld.op("scalar", lambda e: e.activation(
                    scrB[0:1, 0:512], scrB[0:1, 0:512], AF.Exp, bias=0.0, scale=-1.0))
                bld.op("scalar", lambda e: e.activation(
                    scrC[0:1, 0:512], scrC[0:1, 0:512], AF.Exp, bias=0.0, scale=-1.0),
                    inc="ac")
                rdv = bld.now("ac")
                bld.wait("tensor", "ac", rdv)
                for h2 in range(8):
                    ct2, half2 = divmod(h2, 2)
                    rb2 = half2 * 64
                    srow = scrB if h2 < 4 else scrC
                    scol = (h2 % 4) * 128
                    bld.op("tensor", lambda e, ct2=ct2, rb2=rb2, srow=srow, scol=scol:
                           e.matmul(ps_rot[rb2:rb2 + 64, ct2 * 128:ct2 * 128 + 128],
                                    ones1_f[0:1, 0:64], srow[0:1, scol:scol + 128],
                                    start=True, stop=True, tile_position=(0, rb2)),
                           inc="tm" if h2 == 7 else None)
                btm = bld.now("tm")
                bld.wait("vector", "tm", btm)
                bld.op("vector", lambda e: e.tensor_copy(
                    scrA[:, 0:512], ps_rot[:, 0:512]), inc="dv")
                bld.W("vector", "dv")
                for ct in range(4):
                    bld.op("vector", lambda e, ct=ct, qoff=qoff: e.tensor_tensor(
                        o_sb[ct][:, qoff:qoff + 128], ps_o[:, ct * 128:ct * 128 + 128],
                        scrA[:, ct * 128:ct * 128 + 128], ALU.mult),
                        inc="dv" if ct == 3 else None)
                at["oev"] = bld.now("dv")
                if "o_dump" in dumps:
                    bld.W("gpsimd", "dv")
                    for ct in range(4):
                        bld.dma("gpsimd", dump_t["o_dump"][ct * 128:(ct + 1) * 128, qcol:qcol + 128],
                                o_sb[ct][:, qoff:qoff + 128], "dW")
            bld.W("vector", "tm")
            if w + 1 < nw:
                emit_hcast(w + 1)
            if qts:
                qo0 = qts[0][1] - ws
                wlen = 512 - qo0
                for dt in range(8):
                    bld.W("tensor", "dv")
                    for ct in range(4):
                        bld.op("tensor", lambda e, dt=dt, ct=ct, qo0=qo0, wlen=wlen:
                               e.matmul(ps_o[:, 0:wlen],
                                        wo_sb[:, ct * 1024 + dt * 128:ct * 1024 + dt * 128 + 128],
                                        o_sb[ct][:, qo0:512],
                                        start=(ct == 0), stop=(ct == 3)),
                               inc="tm" if ct == 3 else None)
                    bld.W("vector", "tm")
                    off = 0
                    for ap, ww in x_read(dt, ws + qo0, wlen):
                        bld.op("vector", lambda e, ap=ap, ww=ww, off=off:
                               e.tensor_tensor(ap, ps_o[:, off:off + ww], ap, ALU.add),
                               inc="dv" if off + ww == wlen else None)
                        off += ww
        bld.gate()
        p1.close()

    def emit_ffn_pass2(l):
        p2 = ExitStack()
        ps_ln = psum(p2, f"psln{l}", [128, 1024], F32)
        ps_ag = psum(p2, f"psag{l}", [128, 1024], F32)
        ps_out = psum(p2, f"psout{l}", [128, 2048], F32)
        hh_sb = sb(p2, f"hh_sb{l}", [128, 22 * 512], BF16)
        h2 = [sb(p2, f"h2_{l}_{i}", [128, 512], BF16) for i in range(8)]
        aa_bf = sb(p2, f"aa_bf{l}", [128, 512], BF16)
        gg_bf = sb(p2, f"gg_bf{l}", [128, 512], BF16)
        w1_sb = [sb(p2, f"w1_sb{l}_{i}", [128, 2048], BF16) for i in range(2)]
        w2_sb = [sb(p2, f"w2_sb{l}_{i}", [128, 1024], BF16) for i in range(2)]
        xe2 = [sb(p2, f"xe2_{l}_{i}", [128, 8], F32) for i in range(4)]
        lng_sb = sb(p2, f"lng_sb{l}", [128, 8], F32)
        lnb_sb = sb(p2, f"lnb_sb{l}", [128, 8], F32)
        b1a_sb = sb(p2, f"b1a_sb{l}", [128, 22], F32)
        b1g_sb = sb(p2, f"b1g_sb{l}", [128, 22], F32)
        b2_sb = sb(p2, f"b2_sb{l}", [128, 8], F32)
        outbanks = [ps_out[:, i * 512:(i + 1) * 512] for i in range(4)] + \
                   [ps_ag[:, 0:512], ps_ag[:, 512:1024],
                    ps_ln[:, 0:512], ps_ln[:, 512:1024]]

        bld.dma("gpsimd", lng_sb[:, :], f_lng[l], "dW")
        bld.dma("gpsimd", lnb_sb[:, :], f_lnb[l], "dW")
        bld.dma("gpsimd", b1a_sb[:, :], f_b1a[l], "dW")
        bld.dma("gpsimd", b1g_sb[:, :], f_b1g[l], "dW")
        bld.dma("gpsimd", b2_sb[:, :], f_b2[l], "dW")
        bld.W("vector", "dW")
        bld.W("scalar", "dW")

        o0 = OUT_OFF[l]
        total = T0 - o0
        chunks = []
        c = o0
        while c < T0:
            cw = min(512, T0 - c)
            chunks.append((c, cw))
            c += cw

        for ci, (c0, cw) in enumerate(chunks):
            for ct in range(4, 8):
                (r, s, ww), = x_slices(c0 - 1, 1)
                bld.op("vector", lambda e, ct=ct, r=r, s=s, ci=ci: e.tensor_copy(
                    xe2[ct - 4][:, ci:ci + 1], xap(r, ct, s, 1)),
                    inc="dv" if (ct == 7 and ci == len(chunks) - 1) else None)

        for ci, (c0, cw) in enumerate(chunks):
            def shifted_pieces(ct, ci=ci, c0=c0, cw=cw):
                if ct < 4:
                    return x_read(ct, c0, cw)
                return [(xe2[ct - 4][:, ci:ci + 1], 1)] + \
                    (x_read(ct, c0, cw - 1) if cw > 1 else [])
            bld.W("vector", "tm", "dv")
            for ct in range(8):
                off = 0
                for ap, ww in shifted_pieces(ct):
                    bld.op("vector", lambda e, ct=ct, ap=ap, ww=ww, off=off:
                           e.tensor_copy(h2[ct][:, off:off + ww], ap))
                    off += ww
            for ct in range(8):
                if ct > 0:
                    bld.W("vector", "tm")
                bld.op("vector", lambda e, ct=ct, cw=cw: e.tensor_tensor(
                    aa_bf[:, 0:cw], h2[ct][:, 0:cw], h2[ct][:, 0:cw], ALU.mult),
                    inc="dv")
                bld.W("tensor", "dv")
                bld.op("tensor", lambda e, ct=ct, cw=cw: e.matmul(
                    ps_ln[0:1, 0:cw], onesc_bf[:, 0:1], h2[ct][:, 0:cw],
                    start=(ct == 0), stop=(ct == 7)))
                bld.op("tensor", lambda e, ct=ct, cw=cw: e.matmul(
                    ps_ln[0:1, 512:512 + cw], onesc_bf[:, 0:1], aa_bf[:, 0:cw],
                    start=(ct == 0), stop=(ct == 7)), inc="tm")
            bld.W("vector", "tm")
            bld.op("vector", lambda e, cw=cw: e.tensor_scalar(
                rows1[0:1, 0:cw], ps_ln[0:1, 0:cw], 1.0 / D, None, op0=ALU.mult))
            bld.op("vector", lambda e, cw=cw: e.tensor_scalar(
                rows1[0:1, 1024:1024 + cw], ps_ln[0:1, 512:512 + cw], 1.0 / D,
                None, op0=ALU.mult), inc="dv")
            bld.W("vector", "dv")
            bld.op("vector", lambda e, cw=cw: e.tensor_tensor(
                rows2[0:1, 0:cw], rows1[0:1, 0:cw], rows1[0:1, 0:cw], ALU.mult),
                inc="dv")
            bld.W("vector", "dv")
            bld.op("vector", lambda e, cw=cw: e.tensor_tensor(
                rows2[0:1, 1024:1024 + cw], rows1[0:1, 1024:1024 + cw],
                rows2[0:1, 0:cw], ALU.subtract), inc="dv")
            bld.W("vector", "dv")
            bld.op("vector", lambda e, cw=cw: e.tensor_scalar(
                rows1[0:1, 1024:1024 + cw], rows1[0:1, 0:cw], -1.0, None,
                op0=ALU.mult), inc="dv")
            bld.W("scalar", "dv")
            bld.op("scalar", lambda e, cw=cw: e.activation(
                rows2[0:1, 0:cw], rows2[0:1, 1024:1024 + cw], AF.Ln, bias=1e-5,
                scale=1.0))
            bld.op("scalar", lambda e, cw=cw: e.activation(
                rows2[0:1, 0:cw], rows2[0:1, 0:cw], AF.Exp, bias=0.0,
                scale=-0.5), inc="ac")
            bld.W("tensor", "ac")
            bld.op("tensor", lambda e, cw=cw: e.matmul(
                ps_ln[:, 0:cw], ones1_f[0:1, :], rows1[0:1, 1024:1024 + cw],
                start=True, stop=True))
            bld.op("tensor", lambda e, cw=cw: e.matmul(
                ps_ln[:, 512:512 + cw], ones1_f[0:1, :], rows2[0:1, 0:cw],
                start=True, stop=True), inc="tm")
            bld.W("vector", "tm")
            for ct in range(8):
                off = 0
                for ap, ww in shifted_pieces(ct):
                    bld.op("vector", lambda e, ap=ap, ww=ww, off=off:
                           e.scalar_tensor_tensor(
                               scrA[:, off:off + ww], ps_ln[:, off:off + ww], 1.0,
                               ap, op0=ALU.mult, op1=ALU.add))
                    off += ww
                bld.op("vector", lambda e, cw=cw: e.tensor_tensor(
                    scrB[:, 0:cw], scrA[:, 0:cw], ps_ln[:, 512:512 + cw], ALU.mult))
                bld.op("vector", lambda e, ct=ct, cw=cw: e.tensor_scalar(
                    h2[ct][:, 0:cw], scrB[:, 0:cw], lng_sb[:, ct:ct + 1],
                    lnb_sb[:, ct:ct + 1], op0=ALU.mult, op1=ALU.add),
                    inc="dv")
            if dbg2 is not None and l == 0 and ci == 2:
                bld.gate()
                for ct in range(8):
                    bld.dma("gpsimd", dbg2[ct * 128:(ct + 1) * 128, 0:512],
                            h2[ct][:, 0:512], "dW")
                bld.dma("sync", dbg2[0:1, 512:512 + cw], rows1[0:1, 1024:1024 + cw], "dS")
                bld.dma("sync", dbg2[1:2, 512:512 + cw], rows2[0:1, 0:cw], "dS")
                bld.gate()
            w1tok = [None, None]

            def load_w1(j):
                mw = 128 if j < 21 else 42
                buf = w1_sb[j % 2]
                if w1tok[j % 2] is not None:
                    bld.wait("gpsimd", "tm", w1tok[j % 2])
                bld.dma("gpsimd",
                        buf[:, :].rearrange("p (k c) -> p k c", c=256)[:, :, 0:mw],
                        f_w1[l].rearrange("(k p) c -> p k c", p=128)[:, :, j * 128:j * 128 + mw],
                        "dW")
                return bld.dma("gpsimd",
                               buf[:, :].rearrange("p (k c) -> p k c", c=256)[:, :, 128:128 + mw],
                               f_w1[l].rearrange("(k p) c -> p k c", p=128)[:, :, HID + j * 128:HID + j * 128 + mw],
                               "dW")

            w1dw = {0: load_w1(0)}
            gate_dv = None
            ag_ac = None
            tdv_h2 = bld.now("dv")
            for j in range(22):
                mw = 128 if j < 21 else 42
                buf = w1_sb[j % 2]
                bld.wait("tensor", "dW", w1dw.pop(j))
                if j == 0:
                    bld.wait("tensor", "dv", tdv_h2)
                if ag_ac is not None:
                    bld.wait("tensor", "ac", ag_ac)
                for kt in range(8):
                    bld.op("tensor", lambda e, buf=buf, kt=kt, mw=mw, cw=cw:
                           e.matmul(ps_ag[0:mw, 0:cw],
                                    buf[:, kt * 256:kt * 256 + mw],
                                    h2[kt][:, 0:cw], start=(kt == 0), stop=(kt == 7)))
                for kt in range(8):
                    bld.op("tensor", lambda e, buf=buf, kt=kt, mw=mw, cw=cw:
                           e.matmul(ps_ag[0:mw, 512:512 + cw],
                                    buf[:, kt * 256 + 128:kt * 256 + 128 + mw],
                                    h2[kt][:, 0:cw], start=(kt == 0), stop=(kt == 7)),
                           inc="tm" if kt == 7 else None)
                w1tok[j % 2] = bld.now("tm")
                if j + 1 < 22:
                    w1dw[j + 1] = load_w1(j + 1)
                bld.wait("scalar", "tm", w1tok[j % 2])
                if gate_dv is not None:
                    bld.wait("scalar", "dv", gate_dv)
                bld.op("scalar", lambda e, j=j, mw=mw, cw=cw: e.activation(
                    aa_bf[0:mw, 0:cw], ps_ag[0:mw, 0:cw], AF.Identity,
                    bias=b1a_sb[0:mw, j:j + 1], scale=1.0))
                ag_ac = bld.op("scalar", lambda e, j=j, mw=mw, cw=cw: e.activation(
                    gg_bf[0:mw, 0:cw], ps_ag[0:mw, 512:512 + cw], AF.Gelu,
                    bias=b1g_sb[0:mw, j:j + 1], scale=1.0), inc="ac")
                bld.wait("vector", "ac", ag_ac)
                gate_dv = bld.op("vector", lambda e, j=j, mw=mw, cw=cw: e.tensor_tensor(
                    hh_sb[0:mw, j * 512:j * 512 + cw], aa_bf[0:mw, 0:cw],
                    gg_bf[0:mw, 0:cw], ALU.mult), inc="dv")
                if dbg2 is not None and l == 0 and ci == 2 and j in (0, 1):
                    bld.gate()
                    bld.dma("gpsimd", dbg2[j * 128:(j + 1) * 128, 1024:1024 + cw],
                            hh_sb[0:mw, j * 512:j * 512 + cw], "dW")
                    bld.gate()
            bld.wait("tensor", "dv", gate_dv)
            w2tok = [None, None]

            def load_w2(j):
                mw = 128 if j < 21 else 42
                buf = w2_sb[j % 2]
                if w2tok[j % 2] is not None:
                    bld.wait("gpsimd", "tm", w2tok[j % 2])
                return bld.dma("gpsimd", buf[0:mw, :],
                               f_w2[l][j * 128:j * 128 + mw, :], "dW")

            w2dw = {0: load_w2(0)}
            for j in range(22):
                mw = 128 if j < 21 else 42
                buf = w2_sb[j % 2]
                bld.wait("tensor", "dW", w2dw.pop(j))
                for i in range(8):
                    bld.op("tensor", lambda e, buf=buf, i=i, j=j, mw=mw, cw=cw:
                           e.matmul(outbanks[i][:, 0:cw],
                                    buf[0:mw, i * 128:(i + 1) * 128],
                                    hh_sb[0:mw, j * 512:j * 512 + cw],
                                    start=(j == 0), stop=(j == 21)),
                           inc="tm" if i == 7 else None)
                w2tok[j % 2] = bld.now("tm")
                if j + 1 < 22:
                    w2dw[j + 1] = load_w2(j + 1)
            bld.W("vector", "tm")
            for i in range(8):
                (r, s, ww), = x_slices(c0, cw)
                bld.op("vector", lambda e, i=i, r=r, s=s, ww=ww: e.scalar_tensor_tensor(
                    xap(r, i, s, ww), outbanks[i][:, 0:ww], b2_sb[:, i:i + 1],
                    xap(r, i, s, ww), op0=ALU.add, op1=ALU.add), inc="dv")
        bld.gate()
        p2.close()

    def emit_silu_park():
        for nn in range((TOK + 511) // 512):
            c0 = H0 + nn * 512
            for ct in range(8):
                bld.W("scalar", "dS")
                bld.op("scalar", lambda e, ct=ct, c0=c0: e.activation(
                    scrA[:, 0:512], xB[ct][:, c0 - XH_W:c0 - XH_W + 512],
                    AF.Silu), inc="ac")
                bld.W("sync", "ac")
                bld.dma("sync", g_park[ct * 128:(ct + 1) * 128, c0 - H0:c0 - H0 + 512],
                        scrA[:, 0:512], "dS")
                if "g_dump" in dumps:
                    bld.dma("sync", dump_t["g_dump"][ct * 128:(ct + 1) * 128, c0 - H0:c0 - H0 + 512],
                            scrA[:, 0:512], "dS")
        bld.gate()

    emit_top_ln()
    if "xn_dump" in dumps:
        dump_x("xn_dump")
    for l in range(n_ffn):
        emit_ffn_pass1(l)
        if l == 0 and "xa1_dump" in dumps:
            dump_x("xa1_dump")
        emit_ffn_pass2(l)
        if l == 0 and "xl1_dump" in dumps:
            dump_x("xl1_dump")
    if n_ffn == 2 and "xl2_dump" in dumps:
        dump_x("xl2_dump")
    if n_ffn == 2:
        emit_silu_park()
    bld.replay()
    fstack.close()

    astack = ExitStack()
    x_a = [sb(astack, f"xa{i}", [128, TOK], F32) for i in range(8)]
    v_sb = [sb(astack, f"vsb{i}", [128, TOK], BF16) for i in range(8)]
    Eqp = [sb(astack, f"eqp{i}", [128, 1024], BF16) for i in range(8)]
    Ek = [sb(astack, f"ek{i}", [128, 1024], BF16) for i in range(8)]
    ha_bf = [sb(astack, f"habf{i}", [128, 1024], BF16) for i in range(8)]
    Msb = [sb(astack, f"msb{i}", [128, 1024], BF16) for i in range(8)]
    Ablk = sb(astack, "ablk", [128, 1024], BF16)
    zq_bf = Ablk
    wsb = sb(astack, "wsb", [128, 4096], BF16)
    wsb2 = sb(astack, "wsb2", [128, 4096], BF16)
    gam_sb = sb(astack, "gam_sb", [128, 8], F32)
    bet_sb = sb(astack, "bet_sb", [128, 8], F32)
    bo_sb = sb(astack, "bo_sb", [128, 8], F32)
    zkl_sb = sb(astack, "zkl_sb", [128, 8], F32)
    zk0_sb = sb(astack, "zk0_sb", [128, 8], F32)
    zkb_sb = sb(astack, "zkb_sb", [128, 8], BF16)
    zq_acc = sb(astack, "zq_acc", [1, 1024], F32)
    zqt = sb(astack, "zqt", [128, 8], F32)
    PS = psum(astack, "PS", [128, 4096], F32)

    def emit_attn_layer(l):
        btok = emit_attn_layer.btok
        wtok = emit_attn_layer.wtok

        def bank_wait(*banks):
            for bk in banks:
                if bk in btok:
                    s, v = btok[bk]
                    bld.wait("tensor", s, v)

        bld.dma("gpsimd", gam_sb[:, :], a_gamma[l], "dW")
        bld.dma("gpsimd", bet_sb[:, :], a_beta[l], "dW")
        tW = bld.dma("gpsimd", bo_sb[:, :], a_bout[l], "dW")
        bld.op("gpsimd", lambda e: e.memset(zq_acc[:, :], 0.0), inc="gp")
        tgp = bld.now("gp")
        bld.wait("vector", "gp", tgp)
        bld.wait("vector", "dW", tW)
        bld.wait("scalar", "dW", tW)

        for c in range(2):
            tc = c * 1024
            bld.W("vector", "tm", "dv")
            for ct in range(8):
                bld.op("vector", lambda e, ct=ct, tc=tc: e.tensor_copy(
                    ha_bf[ct][:, :], x_a[ct][:, tc:tc + 1024]))
            for ct in range(8):
                bld.op("vector", lambda e, ct=ct: e.tensor_tensor(
                    Ek[ct][:, :], ha_bf[ct][:, :], ha_bf[ct][:, :], ALU.mult),
                    inc="dv" if ct == 7 else None)
            tdv = bld.now("dv")
            bld.wait("tensor", "dv", tdv)
            bank_wait(0, 1, 6, 7)
            for hh in range(2):
                for ct in range(8):
                    bld.op("tensor", lambda e, ct=ct, hh=hh: e.matmul(
                        PS[0:1, 3072 + hh * 512:3584 + hh * 512],
                        onesc_bf[:, 0:1], ha_bf[ct][:, hh * 512:(hh + 1) * 512],
                        start=(ct == 0), stop=(ct == 7)))
                for ct in range(8):
                    bld.op("tensor", lambda e, ct=ct, hh=hh: e.matmul(
                        PS[0:1, hh * 512:512 + hh * 512],
                        onesc_bf[:, 0:1], Ek[ct][:, hh * 512:(hh + 1) * 512],
                        start=(ct == 0), stop=(ct == 7)),
                        inc="tm" if ct == 7 else None)
            ttm = bld.now("tm")
            bld.wait("vector", "tm", ttm)
            bld.op("vector", lambda e: e.tensor_scalar(
                rows1[0:1, 0:1024], PS[0:1, 3072:4096], 1.0 / D, None, op0=ALU.mult))
            bld.op("vector", lambda e: e.tensor_scalar(
                rows1[0:1, 1024:2048], PS[0:1, 0:1024], 1.0 / D, None,
                op0=ALU.mult), inc="dv")
            bld.W("vector", "dv")
            bld.op("vector", lambda e: e.tensor_tensor(
                rows2[0:1, 0:1024], rows1[0:1, 0:1024], rows1[0:1, 0:1024],
                ALU.mult), inc="dv")
            bld.W("vector", "dv")
            bld.op("vector", lambda e: e.tensor_tensor(
                rows2[0:1, 1024:2048], rows1[0:1, 1024:2048], rows2[0:1, 0:1024],
                ALU.subtract), inc="dv")
            bld.W("scalar", "dv")
            bld.op("scalar", lambda e: e.activation(
                rows2[0:1, 0:1024], rows2[0:1, 1024:2048], AF.Ln, bias=1e-5,
                scale=1.0))
            bld.op("scalar", lambda e: e.activation(
                rows2[0:1, 0:1024], rows2[0:1, 0:1024], AF.Exp, bias=0.0,
                scale=-0.5), inc="ac")
            bld.W("vector", "dv")
            bld.op("vector", lambda e: e.tensor_scalar(
                rows1[0:1, 1024:2048], rows1[0:1, 0:1024], -1.0, None,
                op0=ALU.mult), inc="dv")
            tdv = bld.now("dv"); tac = bld.now("ac")
            bld.wait("tensor", "ac", tac)
            bld.wait("tensor", "dv", tdv)
            bank_wait(4, 5, 6, 7)
            for hh in range(2):
                bld.op("tensor", lambda e, hh=hh: e.matmul(
                    PS[:, 2048 + hh * 512:2560 + hh * 512], ones1_f[0:1, :],
                    rows1[0:1, 1024 + hh * 512:1024 + (hh + 1) * 512],
                    start=True, stop=True))
                bld.op("tensor", lambda e, hh=hh: e.matmul(
                    PS[:, 3072 + hh * 512:3584 + hh * 512], ones1_f[0:1, :],
                    rows2[0:1, hh * 512:(hh + 1) * 512], start=True, stop=True),
                    inc="tm" if hh == 1 else None)
            ttm = bld.now("tm")
            bld.wait("vector", "tm", ttm)
            for ct in range(8):
                for hh in range(2):
                    bld.op("vector", lambda e, ct=ct, tc=tc, hh=hh: e.scalar_tensor_tensor(
                        scrA[:, 0:512], PS[:, 2048 + hh * 512:2560 + hh * 512], 1.0,
                        x_a[ct][:, tc + hh * 512:tc + (hh + 1) * 512],
                        op0=ALU.mult, op1=ALU.add))
                    bld.op("vector", lambda e, hh=hh: e.tensor_tensor(
                        scrA[:, 0:512], scrA[:, 0:512],
                        PS[:, 3072 + hh * 512:3584 + hh * 512], ALU.mult))
                    bld.op("vector", lambda e, ct=ct, hh=hh: e.tensor_scalar(
                        ha_bf[ct][:, hh * 512:(hh + 1) * 512], scrA[:, 0:512],
                        gam_sb[:, ct:ct + 1], bet_sb[:, ct:ct + 1],
                        op0=ALU.mult, op1=ALU.add),
                        inc="dv")
            tdv_h = bld.now("dv")
            for bk in (4, 5, 6, 7):
                btok[bk] = ("dv", tdv_h)
            h_waited = [False]

            def load_w(oc, ci):
                buf = wsb if ci % 2 == 0 else wsb2
                if wtok[ci % 2] is not None:
                    bld.wait("gpsimd", "tm", wtok[ci % 2])
                t = bld.dma("gpsimd",
                            buf[:, :].rearrange("p (k c) -> p k c", c=512),
                            a_wqkv[l].rearrange("(k p) c -> p k c", p=128)[:, :, oc * 512:(oc + 1) * 512],
                            "dW")
                return buf, t

            ocs = (2, 3, 0, 1, 4, 5)
            loads = {}
            loads[0] = load_w(ocs[0], emit_attn_layer.ci)
            emit_attn_layer.ci += 1
            for oci, oc in enumerate(ocs):
                buf, tw = loads.pop(oci)
                mybuf = 0 if buf is wsb else 1
                bld.wait("tensor", "dW", tw)
                if not h_waited[0]:
                    bld.wait("tensor", "dv", tdv_h)
                    h_waited[0] = True
                last_tm = None
                if oc < 4:
                    for tt in range(8):
                        bk = tt % 4
                        bank = PS[:, bk * 512:bk * 512 + 512]
                        bank_wait(bk)
                        for kt in range(8):
                            bld.op("tensor", lambda e, b=bank, kt=kt, tt=tt, mybuf=mybuf:
                                   e.matmul(b, ha_bf[kt][:, tt * 128:tt * 128 + 128],
                                            (wsb if mybuf == 0 else wsb2)[:, kt * 512:(kt + 1) * 512],
                                            start=(kt == 0), stop=(kt == 7)),
                                   inc="tm" if kt == 7 else None)
                        gtok = bld.now("tm")
                        last_tm = gtok
                        if tt == 0 and oci + 1 < len(ocs):
                            loads[oci + 1] = load_w(ocs[oci + 1], emit_attn_layer.ci)
                            emit_attn_layer.ci += 1
                        bld.wait("scalar", "tm", gtok)
                        if oc >= 2:
                            kh = oc - 2
                            tac = bld.op("scalar", lambda e, b=bank, tt=tt, kh=kh:
                                         e.activation(Ek[tt][:, kh * 512:(kh + 1) * 512],
                                                      b, AF.Exp, bias=0.0, scale=1.0),
                                         inc="ac")
                            btok[bk] = ("ac", tac)
                            tdv = bld.op("vector", lambda e, tt=tt, kh=kh: (
                                e.wait_ge(bld.sems["ac"], 0)) if False else None)                                 if False else None
                            bld.wait("vector", "ac", tac)
                            if kh == 0:
                                tdv = bld.op("vector", lambda e, tt=tt: e.reduce_sum(
                                    rowst[:, 8:9], Ek[tt][:, 0:512], axis=AX),
                                    inc="dv")
                                bld.W("vector", "dv")
                                bld.op("vector", lambda e, tt=tt: e.tensor_copy(
                                    zk0_sb[:, tt:tt + 1], rowst[:, 8:9]), inc="dv")
                            else:
                                bld.op("vector", lambda e, tt=tt: e.reduce_sum(
                                    rowst[:, 9:10], Ek[tt][:, 512:1024], axis=AX),
                                    inc="dv")
                                bld.W("vector", "dv")
                                bld.op("vector", lambda e, tt=tt: e.tensor_tensor(
                                    rowst[:, 10:11], rowst[:, 9:10],
                                    zk0_sb[:, tt:tt + 1], ALU.add), inc="dv")
                                bld.W("vector", "dv")
                                bld.op("vector", lambda e, tt=tt: e.tensor_copy(
                                    zkb_sb[:, tt:tt + 1], rowst[:, 10:11]),
                                    inc="dv")
                                tdv = bld.now("dv")
                                bld.wait("scalar", "dv", tdv)
                                bld.op("scalar", lambda e, tt=tt: e.activation(
                                    zkl_sb[:, tt:tt + 1], rowst[:, 10:11], AF.Ln,
                                    bias=0.0, scale=1.0), inc="ac")
                                tac2 = bld.now("ac")
                                bld.wait("vector", "ac", tac2)
                                bld.op("vector", lambda e, tt=tt: e.tensor_scalar(
                                    zkl_sb[:, tt:tt + 1], zkl_sb[:, tt:tt + 1],
                                    -1.0, None, op0=ALU.mult), inc="dv")
                                emit_attn_layer.zktok[tt] = bld.now("dv")
                        else:
                            qh = oc
                            bld.wait("scalar", "dv", emit_attn_layer.zktok[tt])
                            tac = bld.op("scalar", lambda e, b=bank, tt=tt, qh=qh:
                                         e.activation(Eqp[tt][:, qh * 512:(qh + 1) * 512],
                                                      b, AF.Exp, bias=zkl_sb[:, tt:tt + 1],
                                                      scale=1.0), inc="ac")
                            btok[bk] = ("ac", tac)
                else:
                    for mt in range(4):
                        vt = (oc - 4) * 4 + mt
                        for nn2 in range(2):
                            bk = 4 + (mt * 2 + nn2) % 2
                            bank = PS[:, bk * 512:bk * 512 + 512]
                            bank_wait(bk)
                            for kt in range(8):
                                bld.op("tensor", lambda e, b=bank, kt=kt, mt=mt,
                                       nn2=nn2, mybuf=mybuf:
                                       e.matmul(b, (wsb if mybuf == 0 else wsb2)[:, kt * 512 + mt * 128:kt * 512 + mt * 128 + 128],
                                                ha_bf[kt][:, nn2 * 512:(nn2 + 1) * 512],
                                                start=(kt == 0), stop=(kt == 7)),
                                       inc="tm" if kt == 7 else None)
                            gtok = bld.now("tm")
                            last_tm = gtok
                            if mt == 0 and nn2 == 0 and oci + 1 < len(ocs):
                                loads[oci + 1] = load_w(ocs[oci + 1], emit_attn_layer.ci)
                                emit_attn_layer.ci += 1
                            bld.wait("vector", "tm", gtok)
                            tdv = bld.op("vector", lambda e, b=bank, vt=vt, tc=tc, nn2=nn2:
                                         e.tensor_copy(v_sb[vt][:, tc + nn2 * 512:tc + (nn2 + 1) * 512], b),
                                         inc="dv")
                            btok[bk] = ("dv", tdv)
                wtok[mybuf] = last_tm
            bld.W("tensor", "ac", "dv")
            for ct in range(8):
                for half in range(2):
                    bk = (2 * ct + half) % 8
                    bank = PS[:, bk * 512:bk * 512 + 512]
                    bank_wait(bk)
                    for tt in range(8):
                        bld.op("tensor", lambda e, b=bank, ct=ct, half=half, tt=tt:
                               e.matmul(b, Eqp[tt][:, ct * 128:ct * 128 + 128],
                                        Ek[tt][:, half * 512:(half + 1) * 512],
                                        start=(tt == 0), stop=(tt == 7)),
                               inc="tm" if tt == 7 else None)
                gtok = bld.now("tm")
                bld.wait("vector", "tm", gtok)
                pcols = PS[:, ((2 * ct) % 8) * 512:((2 * ct) % 8) * 512 + 1024]
                if c == 0:
                    tdv = bld.op("vector", lambda e, ct=ct, pcols=pcols: e.tensor_copy(
                        Msb[ct][:, :], pcols), inc="dv")
                else:
                    tdv = bld.op("vector", lambda e, ct=ct, pcols=pcols: e.tensor_tensor(
                        Msb[ct][:, :], pcols, Msb[ct][:, :], ALU.add), inc="dv")
                btok[(2 * ct) % 8] = ("dv", tdv)
                btok[(2 * ct + 1) % 8] = ("dv", tdv)
            for half in range(2):
                bk = 4 + half
                bank_wait(bk)
                for tt in range(8):
                    bld.op("tensor", lambda e, half=half, tt=tt: e.matmul(
                        PS[0:1, 2048 + half * 512:2560 + half * 512],
                        zkb_sb[:, tt:tt + 1], Eqp[tt][:, half * 512:(half + 1) * 512],
                        start=(tt == 0), stop=(tt == 7)),
                        inc="tm" if tt == 7 else None)
            gtok = bld.now("tm")
            bld.wait("vector", "tm", gtok)
            tdv = bld.op("vector", lambda e: e.tensor_tensor(
                zq_acc[0:1, :], PS[0:1, 2048:3072], zq_acc[0:1, :], ALU.add),
                inc="dv")
            btok[4] = ("dv", tdv)
            btok[5] = ("dv", tdv)
        bld.W("vector", "dv")
        bld.op("vector", lambda e: e.tensor_copy(zq_bf[0:1, 0:1024], zq_acc[:, :]),
               inc="dv")
        bld.W("sync", "dv")
        for ct in range(8):
            bld.dma("sync", ar_in[ct * 128:(ct + 1) * 128, :], Msb[ct][:, :], "dS")
        bld.dma("sync", ar_in[1024:1025, :], zq_bf[0:1, 0:1024], "dS")
        bld.W("gpsimd", "dS")
        ccname = f"cc{l}"
        bld.op("gpsimd", lambda e: e.collective_compute(
            "AllReduce", ALU.add, replica_groups=GROUPS,
            ins=[ar_in.ap().opt()], outs=[ar_out.ap().opt()]), inc=ccname)
        bld.W("sync", ccname)
        for ct in range(8):
            bld.dma("sync", Msb[ct][:, :], ar_out[ct * 128:(ct + 1) * 128, :], "dL")
        bld.dma("sync", zq_bf[0:1, 0:1024], ar_out[1024:1025, :], "dL")
        bld.W("vector", "dL")
        bld.op("vector", lambda e: e.reciprocal(rows1[0:1, 0:1024], zq_bf[0:1, :]),
               inc="dv")
        tdv = bld.now("dv")
        bld.wait("tensor", "dv", tdv)
        bank_wait(6)
        for i in range(8):
            bld.op("tensor", lambda e, i=i: e.transpose(
                PS[:, 3072 + i:3073 + i], rows1[0:1, i * 128:(i + 1) * 128],
                id1_f[:, :]), inc="tm" if i == 7 else None)
        ttm = bld.now("tm")
        bld.wait("vector", "tm", ttm)
        bld.op("vector", lambda e: e.tensor_copy(zqt[:, :], PS[:, 3072:3080]),
               inc="dv")
        bld.W("vector", "dv")
        tz = None
        for ct in range(8):
            tz = bld.op("vector", lambda e, ct=ct: e.tensor_scalar(
                Msb[ct][:, :], Msb[ct][:, :], zqt[:, ct:ct + 1], None,
                op0=ALU.mult), inc="dv")
        btok[6] = ("dv", tz)
        bld.wait("tensor", "dv", tz)
        for nn in range(2):
            ci = emit_attn_layer.ci
            buf = wsb if ci % 2 == 0 else wsb2
            mybuf = ci % 2
            emit_attn_layer.ci += 1
            if wtok[mybuf] is not None:
                bld.wait("gpsimd", "tm", wtok[mybuf])
            tw = bld.dma("gpsimd",
                         buf[:, :].rearrange("p (k c) -> p k c", c=512),
                         a_wout[l].rearrange("(k p) c -> p k c", p=128)[:, :, nn * 512:(nn + 1) * 512],
                         "dW")
            bld.wait("tensor", "dW", tw)
            last_tm = None
            for dbi in range(4):
                db = nn * 4 + dbi
                bank_wait(0, 1)
                for j in range(8):
                    bld.op("tensor", lambda e, j=j, dbi=dbi, mybuf=mybuf: [
                        e.matmul(PS[:, j * 128:(j + 1) * 128],
                                 Msb[ct2][:, j * 128:j * 128 + 128],
                                 (wsb if mybuf == 0 else wsb2)[:, ct2 * 512 + dbi * 128:ct2 * 512 + dbi * 128 + 128],
                                 start=(ct2 == 0), stop=(ct2 == 7))
                        for ct2 in range(8)][-1],
                           inc="tm")
                gtok = bld.now("tm")
                last_tm = gtok
                bld.wait("vector", "tm", gtok)
                tdv = bld.op("vector", lambda e: e.tensor_copy(Ablk[:, :], PS[:, 0:1024]),
                             inc="dv")
                btok[0] = ("dv", tdv)
                btok[1] = ("dv", tdv)
                bld.wait("tensor", "dv", tdv)
                for nn2 in range(4):
                    bk = 2 + nn2 % 2
                    bank = PS[:, bk * 512:bk * 512 + 512]
                    bank_wait(bk)
                    for j in range(8):
                        bld.op("tensor", lambda e, b=bank, j=j, nn2=nn2: e.matmul(
                            b, Ablk[:, j * 128:(j + 1) * 128],
                            v_sb[j][:, nn2 * 512:(nn2 + 1) * 512],
                            start=(j == 0), stop=(j == 7)),
                            inc="tm" if j == 7 else None)
                    gtok = bld.now("tm")
                    last_tm = gtok
                    bld.wait("vector", "tm", gtok)
                    tdv = bld.op("vector", lambda e, b=bank, db=db, nn2=nn2:
                                 e.scalar_tensor_tensor(
                                     x_a[db][:, nn2 * 512:(nn2 + 1) * 512], b,
                                     bo_sb[:, db:db + 1],
                                     x_a[db][:, nn2 * 512:(nn2 + 1) * 512],
                                     op0=ALU.add, op1=ALU.add), inc="dv")
                    btok[bk] = ("dv", tdv)
            wtok[mybuf] = last_tm
        bld.gate()
        emit_attn_layer.btok = {}
        emit_attn_layer.wtok = [None, None]
    emit_attn_layer.btok = {}
    emit_attn_layer.wtok = [None, None]
    emit_attn_layer.ci = 0
    emit_attn_layer.zktok = [0] * 8

    for ct in range(8):
        bld.dma("sync", x_a[ct][:, :], xn_park[ct * 128:(ct + 1) * 128, :], "dL")
    bld.W("vector", "dL")
    bld.W("tensor", "dL")

    for l in range(n_attn):
        emit_attn_layer(l)
        if xal is not None:
            for ct in range(8):
                bld.dma("sync", xal[l][ct * 128:(ct + 1) * 128, :], x_a[ct][:, :], "dS")
            bld.W("sync", "dS")
        if l == 0 and "s1_dump" in dumps:
            bld.gate()
            for ct in range(8):
                bld.dma("sync", dump_t["s1_dump"][ct * 128:(ct + 1) * 128, :],
                        x_a[ct][:, :], "dS")
            bld.gate()
    if "s_dump" in dumps:
        bld.gate()
        for ct in range(8):
            bld.dma("sync", dump_t["s_dump"][ct * 128:(ct + 1) * 128, :],
                    x_a[ct][:, :], "dS")
        bld.gate()

    for nn in range(4):
        for ct in range(8):
            bld.W("sync", "dv")
            bld.dma("sync", scrA[:, 0:512],
                    xcore[ct * 128:(ct + 1) * 128, H0 + nn * 512:H0 + (nn + 1) * 512],
                    "dL")
            bld.dma("sync", scrB[:, 0:512],
                    g_park[ct * 128:(ct + 1) * 128, nn * 512:(nn + 1) * 512], "dL")
            bld.W("vector", "dL")
            bld.W("vector", "dS")
            bld.op("vector", lambda e, ct=ct, nn=nn: e.tensor_tensor(
                scrC[:, 0:512], x_a[ct][:, nn * 512:(nn + 1) * 512], scrB[:, 0:512],
                ALU.mult))
            bld.op("vector", lambda e: e.tensor_tensor(
                scrC[:, 0:512], scrC[:, 0:512], scrA[:, 0:512], ALU.add), inc="dv")
            bld.W("sync", "dv")
            bld.dma("sync", out_fm[ct * 128:(ct + 1) * 128, nn * 512:(nn + 1) * 512],
                    scrC[:, 0:512], "dS")
    bld.W("sync", "dS")
    bld.W("gpsimd", "dW")

    bld.replay()
    astack.close()
    pstack.close()
    bld.close()
    return nc



def kernel(**inputs):
    in_maps = [stage_core(inputs, p) for p in range(NCORES)]
    nc = build_program()
    res = run_bass_kernel_spmd(nc, in_maps, core_ids=list(range(NCORES)))
    out = np.empty((B, N, D), dtype=np.float32)
    for p in range(NCORES):
        b, pos = divmod(p, 4)
        t0 = pos * TOK
        out[b, t0:t0 + TOK, :] = res.results[p]["out_fm"].T
    return out


# version 10
# speedup vs baseline: 1.2340x; 1.0630x over previous
from collections import defaultdict
from contextlib import ExitStack

import numpy as np
import ml_dtypes

import concourse.bass as bass
import concourse.mybir as mybir
from concourse.bass_utils import run_bass_kernel_spmd

F32 = mybir.dt.float32
BF16 = mybir.dt.bfloat16
AF = mybir.ActivationFunctionType
ALU = mybir.AluOpType
AX = mybir.AxisListType.X

NCORES = 8
B, N, D = 2, 8192, 1024
TOK = 2048
H0 = 2176
T0 = H0 + TOK
XH_W = 1024
XB_W = T0 - XH_W
WIN = 512
HID = 2730
NEG = -1.0e30

QKV_OFF = {0: 128, 1: 1152}
ATT_OFF = {0: 896, 1: 2048}
OUT_OFF = {0: 1024, 1: 2176}
NWIN = {0: 8, 1: 6}
NQT = {0: 26, 1: 17}

ENGINES = ("tensor", "vector", "scalar", "gpsimd", "sync")
GROUPS = [[0, 1, 2, 3], [4, 5, 6, 7]]



def _rotary_tables(t0):
    t = np.arange(t0 - H0, t0 + TOK, dtype=np.float64)
    half = np.arange(0, 64, 2, dtype=np.float64)
    inv_freq = 1.0 / 10000.0 ** (half / 64.0)
    d = np.arange(64)
    f_d = inv_freq[d % 32]
    freqs = f_d[:, None] * t[None, :]
    base = (half + 0.4 * 64.0) / (1.4 * 64.0)
    base_d = base[d % 32]
    power = (t - N // 2) / float(WIN // 2)
    scale = base_d[:, None] ** power[None, :]
    cos, sin = np.cos(freqs), np.sin(freqs)
    tabs = [cos * scale, sin * scale, cos / scale, sin / scale]
    return [np.ascontiguousarray(np.concatenate([x, x], 0).astype(np.float32))
            for x in tabs]


def _masks(t0, l):
    nqt = NQT[l]
    out = np.zeros((nqt, 128, 640), dtype=np.float64)
    q0 = t0 - (H0 - ATT_OFF[l])
    for j in range(nqt):
        qg = q0 + 128 * j + np.arange(128)
        ws = (qg[0] // WIN) * WIN
        ti = (qg[0] - ws) // 128
        kpos = ws - WIN + 128 * ti + np.arange(640)
        ok = (kpos[None, :] >= 0) & (kpos[None, :] <= qg[:, None]) \
            & (qg[:, None] <= kpos[None, :] + WIN)
        out[j] = np.where(ok, 0.0, NEG)
    return out.astype(ml_dtypes.bfloat16)


def _rot_perm():
    P = np.zeros((128, 128), dtype=np.float32)
    for i in range(128):
        if i % 64 < 32:
            P[i, i + 32] = -1.0
        else:
            P[i, i - 32] = 1.0
    return np.ascontiguousarray(P.T).astype(ml_dtypes.bfloat16)


def _col_layout(vec, tiles):
    v = np.zeros(128 * tiles, dtype=np.float32)
    v[:len(vec)] = vec
    return np.ascontiguousarray(v.reshape(tiles, 128).T)


def stage_core(inputs, p):
    b, pos = divmod(p, 4)
    t0 = pos * TOK
    xT = np.ascontiguousarray(np.asarray(inputs["L0_attn_x"][b]).T)
    lo = t0 - H0
    if lo >= 0:
        xc = xT[:, lo:t0 + TOK]
    else:
        xc = np.concatenate(
            [np.zeros((D, -lo), dtype=np.float32), xT[:, :t0 + TOK]], axis=1)
    d = {"xcore": np.ascontiguousarray(xc)}
    d["f_wqkv"] = np.ascontiguousarray(inputs["L0_ffn_wqkv"])
    d["f_wo"] = np.ascontiguousarray(inputs["L0_ffn_wo"])
    d["f_w1"] = np.ascontiguousarray(inputs["L0_ffn_w1"])
    d["f_w2"] = np.ascontiguousarray(inputs["L0_ffn_w2"])
    d["f_qs"] = np.ascontiguousarray(
        np.concatenate([inputs["L0_ffn_q_scale"]] * 2, axis=1)[:, :, None])
    d["f_ks"] = np.ascontiguousarray(
        np.concatenate([inputs["L0_ffn_k_scale"]] * 2, axis=1)[:, :, None])
    d["f_lng"] = np.stack([_col_layout(inputs["L0_ffn_ln_g"][l], 8) for l in range(2)])
    d["f_lnb"] = np.stack([_col_layout(inputs["L0_ffn_ln_b"][l], 8) for l in range(2)])
    d["f_b1a"] = np.stack([_col_layout(inputs["L0_ffn_b1"][l][:HID], 22) for l in range(2)])
    d["f_b1g"] = np.stack([_col_layout(inputs["L0_ffn_b1"][l][HID:], 22) for l in range(2)])
    d["f_b2"] = np.stack([_col_layout(inputs["L0_ffn_b2"][l], 8) for l in range(2)])
    d["a_wqkv"] = np.ascontiguousarray(inputs["L0_attn_w_qkv"])
    d["a_wout"] = np.ascontiguousarray(inputs["L0_attn_w_out"])
    d["a_bout"] = np.stack([_col_layout(inputs["L0_attn_b_out"][l], 8) for l in range(6)])
    d["a_gamma"] = np.stack([_col_layout(inputs["L0_attn_gamma"][l], 8) for l in range(6)])
    d["a_beta"] = np.stack([_col_layout(inputs["L0_attn_beta"][l], 8) for l in range(6)])
    rt = _rotary_tables(t0)
    d["rot_q1"], d["rot_q2"], d["rot_k1"], d["rot_k2"] = rt
    d["mask0"] = np.ascontiguousarray(_masks(t0, 0))
    d["mask1"] = np.ascontiguousarray(_masks(t0, 1))
    d["rotp"] = _rot_perm()
    d["ident"] = np.ascontiguousarray(np.eye(128, dtype=np.float32)).astype(
        ml_dtypes.bfloat16)
    d["bo_col"] = np.ascontiguousarray(np.stack(
        [np.r_[np.ones(64), np.zeros(64)], np.r_[np.zeros(64), np.ones(64)]],
        axis=1)).astype(ml_dtypes.bfloat16)
    d["bo_row"] = np.ascontiguousarray(np.stack(
        [np.r_[np.ones(64), np.zeros(64)], np.r_[np.zeros(64), np.ones(64)]],
        axis=0)).astype(np.float32)
    return d



class Bld:
    def __init__(self, nc):
        self.nc = nc
        self.q = {e: [] for e in ENGINES}
        self.cnt = defaultdict(int)
        self.sems = {}
        self._sem_stack = ExitStack()

    def sem(self, name):
        if name not in self.sems:
            self.sems[name] = self._sem_stack.enter_context(self.nc.semaphore(name))
        return self.sems[name]

    def op(self, eng, fn, inc=None, amt=1):
        if inc is not None:
            s = self.sem(inc)
            self.cnt[inc] += amt

            def wrapped(e, fn=fn, s=s, amt=amt):
                fn(e).then_inc(s, amt)
            self.q[eng].append(wrapped)
            return self.cnt[inc]
        self.q[eng].append(fn)
        return None

    def dma(self, eng, out, in_, inc, **kw):
        return self.op(eng, lambda e, o=out, i=in_, kw=kw: e.dma_start(out=o, in_=i, **kw),
                       inc=inc, amt=16)

    def wait(self, eng, name, val):
        if val <= 0:
            return
        s = self.sem(name)
        self.q[eng].append(lambda e, s=s, v=val: e.wait_ge(s, v))

    def W(self, eng, *names):
        for n in names:
            self.wait(eng, n, self.cnt[n])

    def gate(self):
        allsems = ["tm", "dv", "ac", "gp", "dW", "dL", "dS"]
        for e in ENGINES:
            self.W(e, *allsems)

    def now(self, name):
        return self.cnt[name]

    def replay(self):
        with self.nc.Block() as block:
            for name in ENGINES:
                ops = self.q[name]
                if not ops:
                    continue

                def runner(e, ops=tuple(ops)):
                    for f in ops:
                        f(e)
                getattr(block, name)(runner)
        self.q = {e: [] for e in ENGINES}

    def close(self):
        self._sem_stack.close()


def x_slices(c0, w):
    out = []
    if c0 < XH_W:
        ww = min(w, XH_W - c0)
        out.append(("H", c0, ww))
        c0 += ww
        w -= ww
    if w > 0:
        out.append(("B", c0 - XH_W, w))
    return out



def build_program(dev_dumps=(), n_ffn=2, n_attn=6):
    nc = bass.Bass("TRN2", target_bir_lowering=False)
    dumps = set(dev_dumps)

    xcore = nc.dram_tensor("xcore", [D, T0], F32, kind="ExternalInput")
    f_wqkv = nc.dram_tensor("f_wqkv", [2, D, 1536], F32, kind="ExternalInput")
    f_wo = nc.dram_tensor("f_wo", [2, 512, D], F32, kind="ExternalInput")
    f_w1 = nc.dram_tensor("f_w1", [2, D, 5460], F32, kind="ExternalInput")
    f_w2 = nc.dram_tensor("f_w2", [2, HID, D], F32, kind="ExternalInput")
    f_qs = nc.dram_tensor("f_qs", [2, 128, 1], F32, kind="ExternalInput")
    f_ks = nc.dram_tensor("f_ks", [2, 128, 1], F32, kind="ExternalInput")
    f_lng = nc.dram_tensor("f_lng", [2, 128, 8], F32, kind="ExternalInput")
    f_lnb = nc.dram_tensor("f_lnb", [2, 128, 8], F32, kind="ExternalInput")
    f_b1a = nc.dram_tensor("f_b1a", [2, 128, 22], F32, kind="ExternalInput")
    f_b1g = nc.dram_tensor("f_b1g", [2, 128, 22], F32, kind="ExternalInput")
    f_b2 = nc.dram_tensor("f_b2", [2, 128, 8], F32, kind="ExternalInput")
    a_wqkv = nc.dram_tensor("a_wqkv", [6, D, 3072], F32, kind="ExternalInput")
    a_wout = nc.dram_tensor("a_wout", [6, D, D], F32, kind="ExternalInput")
    a_bout = nc.dram_tensor("a_bout", [6, 128, 8], F32, kind="ExternalInput")
    a_gamma = nc.dram_tensor("a_gamma", [6, 128, 8], F32, kind="ExternalInput")
    a_beta = nc.dram_tensor("a_beta", [6, 128, 8], F32, kind="ExternalInput")
    rot_t = [nc.dram_tensor(f"rot_{s}", [128, T0], F32, kind="ExternalInput")
             for s in ("q1", "q2", "k1", "k2")]
    mask_d = {0: nc.dram_tensor("mask0", [NQT[0], 128, 640], BF16, kind="ExternalInput"),
              1: nc.dram_tensor("mask1", [NQT[1], 128, 640], BF16, kind="ExternalInput")}
    rotp = nc.dram_tensor("rotp", [128, 128], BF16, kind="ExternalInput")
    ident = nc.dram_tensor("ident", [128, 128], BF16, kind="ExternalInput")
    bo_col = nc.dram_tensor("bo_col", [128, 2], BF16, kind="ExternalInput")
    bo_row = nc.dram_tensor("bo_row", [2, 128], F32, kind="ExternalInput")

    out_fm = nc.dram_tensor("out_fm", [D, TOK], F32, kind="ExternalOutput")
    xn_park = nc.dram_tensor("xn_park", [D, TOK], F32, kind="Internal")
    g_park = nc.dram_tensor("g_park", [D, TOK], F32, kind="Internal")
    ar_in = nc.dram_tensor("ar_in", [1025, 1024], BF16, kind="Internal")
    ar_out = nc.dram_tensor("ar_out", [1025, 1024], BF16, kind="Internal")

    dbg = nc.dram_tensor("dbg", [128, 4 * 640], F32, kind="ExternalOutput") \
        if "dbg" in dumps else None
    dbg2 = nc.dram_tensor("dbg2", [1024, 1664], F32, kind="ExternalOutput") \
        if "dbg2" in dumps else None
    xal = nc.dram_tensor("xal", [6, D, TOK], F32, kind="ExternalOutput") \
        if "xal" in dumps else None
    dump_t = {}
    for name in dumps:
        if name in ("dbg", "dbg2", "xal"):
            continue
        shp = {"xn_dump": [D, T0], "xl1_dump": [D, T0], "xl2_dump": [D, T0],
               "xa1_dump": [D, T0], "g_dump": [D, TOK],
               "q_dump": [512, T0], "k_dump": [512, T0],
               "v_dump": [T0, 512], "o_dump": [512, T0],
               "s1_dump": [D, TOK], "s_dump": [D, TOK]}[name]
        dump_t[name] = nc.dram_tensor(name, shp, F32, kind="ExternalOutput")

    bld = Bld(nc)
    pstack = ExitStack()

    def sb(stack, name, shape, dt):
        return stack.enter_context(nc.sbuf_tensor(name, shape, dt))

    def psum(stack, name, shape, dt=F32):
        return stack.enter_context(nc.psum_tensor(name, shape, dt))

    id_bf = sb(pstack, "id_bf", [128, 128], BF16)
    rotp_sb = sb(pstack, "rotp_sb", [128, 128], BF16)
    bo_col_sb = sb(pstack, "bo_col_sb", [128, 2], BF16)
    bo_row_sb = sb(pstack, "bo_row_sb", [2, 128], F32)
    ones1_f = sb(pstack, "ones1_f", [1, 128], F32)
    onesc_bf = sb(pstack, "onesc_bf", [128, 1], BF16)
    id1_f = sb(pstack, "id1_f", [1, 1], F32)
    eps_f = sb(pstack, "eps_f", [128, 1], F32)
    nc.const_aps.register(F32, 1e-5, eps_f[:, 0:1])
    scrA = sb(pstack, "scrA", [128, 512], F32)
    scrB = sb(pstack, "scrB", [128, 512], F32)
    scrC = sb(pstack, "scrC", [128, 512], F32)
    rowst = sb(pstack, "rowst", [128, 16], F32)
    rows1 = sb(pstack, "rows1", [1, 2048], F32)
    rows2 = sb(pstack, "rows2", [1, 2048], F32)

    bld.dma("gpsimd", id_bf[:, :], ident[:, :], "dW")
    bld.dma("gpsimd", rotp_sb[:, :], rotp[:, :], "dW")
    bld.dma("gpsimd", bo_col_sb[:, :], bo_col[:, :], "dW")
    bld.dma("gpsimd", bo_row_sb[:, :], bo_row[:, :], "dW")
    bld.op("gpsimd", lambda e: e.memset(ones1_f[:, :], 1.0))
    bld.op("gpsimd", lambda e: e.memset(onesc_bf[:, :], 1.0))
    bld.op("gpsimd", lambda e: e.memset(id1_f[:, :], 1.0))
    bld.op("gpsimd", lambda e: e.memset(eps_f[:, :], 1e-5), inc="gp")
    bld.gate()

    fstack = ExitStack()
    xH = [sb(fstack, f"xH{i}", [128, XH_W], BF16) for i in range(8)]
    xB = [sb(fstack, f"xB{i}", [128, XB_W], F32) for i in range(8)]

    def xap(region, ct, c0, w):
        t = (xH if region == "H" else xB)[ct]
        return t[:, c0:c0 + w]

    def x_read(ct, c0, w):
        return [(xap(r, ct, s, ww), ww) for r, s, ww in x_slices(c0, w)]

    def dump_x(name):
        bld.gate()
        for ct in range(8):
            bld.dma("gpsimd", dump_t[name][ct * 128:(ct + 1) * 128, 0:XH_W],
                    xH[ct][:, :], "dW")
            bld.dma("sync", dump_t[name][ct * 128:(ct + 1) * 128, XH_W:T0],
                    xB[ct][:, :], "dS")
        bld.gate()


    def emit_top_ln():
        ts_ = ExitStack()
        topx = [sb(ts_, f"topx{i}", [128, 512], F32) for i in range(8)]
        topb = [sb(ts_, f"topb{i}", [128, 512], BF16) for i in range(8)]
        tops = [sb(ts_, f"tops{i}", [128, 512], BF16) for i in range(8)]
        ps = psum(ts_, "ps_top", [128, 2048], F32)
        chunks = [(i * 512, 512) for i in range(8)] + [(4096, 128)]
        for (c0, w) in chunks:
            for ct in range(8):
                bld.dma("sync", topx[ct][:, 0:w],
                        xcore[ct * 128:(ct + 1) * 128, c0:c0 + w], "dL")
            bld.W("vector", "dL")
            bld.W("vector", "tm")
            for ct in range(8):
                bld.op("vector", lambda e, ct=ct, w=w: e.tensor_copy(
                    topb[ct][:, 0:w], topx[ct][:, 0:w]))
            for ct in range(8):
                bld.op("vector", lambda e, ct=ct, w=w: e.tensor_tensor(
                    tops[ct][:, 0:w], topb[ct][:, 0:w], topb[ct][:, 0:w], ALU.mult),
                    inc="dv" if ct == 7 else None)
            bld.W("tensor", "dv")
            for ct in range(8):
                bld.op("tensor", lambda e, ct=ct, w=w: e.matmul(
                    ps[0:1, 0:w], onesc_bf[:, 0:1], topb[ct][:, 0:w],
                    start=(ct == 0), stop=(ct == 7)))
            for ct in range(8):
                bld.op("tensor", lambda e, ct=ct, w=w: e.matmul(
                    ps[0:1, 512:512 + w], onesc_bf[:, 0:1], tops[ct][:, 0:w],
                    start=(ct == 0), stop=(ct == 7)),
                    inc="tm" if ct == 7 else None)
            bld.W("vector", "tm")
            bld.op("vector", lambda e, w=w: e.tensor_scalar(
                rows1[0:1, 0:w], ps[0:1, 0:w], 1.0 / D, None, op0=ALU.mult))
            bld.op("vector", lambda e, w=w: e.tensor_scalar(
                rows1[0:1, 1024:1024 + w], ps[0:1, 512:512 + w], 1.0 / D, None,
                op0=ALU.mult), inc="dv")
            bld.W("vector", "dv")
            bld.op("vector", lambda e, w=w: e.tensor_tensor(
                rows2[0:1, 0:w], rows1[0:1, 0:w], rows1[0:1, 0:w], ALU.mult),
                inc="dv")
            bld.W("vector", "dv")
            bld.op("vector", lambda e, w=w: e.tensor_tensor(
                rows2[0:1, 1024:1024 + w], rows1[0:1, 1024:1024 + w],
                rows2[0:1, 0:w], ALU.subtract), inc="dv")
            bld.W("scalar", "dv")
            bld.op("scalar", lambda e, w=w: e.activation(
                rows2[0:1, 0:w], rows2[0:1, 1024:1024 + w], AF.Ln, bias=1e-5,
                scale=1.0))
            bld.op("scalar", lambda e, w=w: e.activation(
                rows2[0:1, 0:w], rows2[0:1, 0:w], AF.Exp, bias=0.0, scale=-0.5),
                inc="ac")
            bld.W("vector", "dv")
            bld.op("vector", lambda e, w=w: e.tensor_scalar(
                rows1[0:1, 1024:1024 + w], rows1[0:1, 0:w], -1.0, None,
                op0=ALU.mult), inc="dv")
            bld.W("tensor", "ac", "dv")
            bld.op("tensor", lambda e, w=w: e.matmul(
                ps[:, 1024:1024 + w], ones1_f[0:1, :], rows1[0:1, 1024:1024 + w],
                start=True, stop=True))
            bld.op("tensor", lambda e, w=w: e.matmul(
                ps[:, 1536:1536 + w], ones1_f[0:1, :], rows2[0:1, 0:w],
                start=True, stop=True), inc="tm")
            bld.W("vector", "tm")
            for ct in range(8):
                bld.op("vector", lambda e, ct=ct, w=w: e.scalar_tensor_tensor(
                    topx[ct][:, 0:w], ps[:, 1024:1024 + w], 1.0, topx[ct][:, 0:w],
                    op0=ALU.mult, op1=ALU.add))
                (r, s, ww), = x_slices(c0, w)
                bld.op("vector", lambda e, ct=ct, r=r, s=s, ww=ww: e.tensor_tensor(
                    xap(r, ct, s, ww), topx[ct][:, 0:ww], ps[:, 1536:1536 + ww],
                    ALU.mult), inc="dv" if ct == 7 else None)
            bld.W("sync", "dv")
            if c0 + w > H0:
                s0 = max(c0, H0)
                for ct in range(8):
                    bld.dma("sync", xn_park[ct * 128:(ct + 1) * 128, s0 - H0:c0 + w - H0],
                            xB[ct][:, s0 - XH_W:c0 + w - XH_W], "dS")
        bld.gate()
        ts_.close()

    def qtiles_of_window(l, w):
        ws = QKV_OFF[l] + 512 * w
        out = []
        for j in range(NQT[l]):
            c = ATT_OFF[l] + 128 * j
            if ws <= c < ws + 512:
                out.append((j, c))
        return out

    def emit_ffn_pass1(l):
        p1 = ExitStack()
        ps_qkv = psum(p1, f"psqkv{l}", [128, 1024], F32)
        ps_ms = psum(p1, f"psms{l}", [128, 512], F32)
        ps_rot = psum(p1, f"psrot{l}", [128, 512], F32)
        ps_sim = psum(p1, f"pssim{l}", [128, 640], F32)
        ps_aT = psum(p1, f"psaT{l}", [128, 640], BF16)
        ps_o = psum(p1, f"pso{l}", [128, 512], F32)
        h_bf = [sb(p1, f"hbf{l}_{i}", [128, 512], BF16) for i in range(8)]
        kring = [sb(p1, f"kring{l}_{i}", [128, 1024], BF16) for i in range(4)]
        vring = sb(p1, f"vring{l}", [128, 8 * 512], BF16)
        qbuf = [sb(p1, f"qbuf{l}_{i}", [128, 512], BF16) for i in range(4)]
        o_sb = [sb(p1, f"osb{l}_{i}", [128, 512], BF16) for i in range(4)]
        masked = sb(p1, f"masked{l}", [128, 640], F32)
        attn_bf = sb(p1, f"attn_bf{l}", [128, 640], BF16)
        attnT_sb = sb(p1, f"attnT_sb{l}", [128, 640], BF16)
        tabs = [sb(p1, f"tab{l}_{i}", [128, 512], F32) for i in range(2)]
        mask_sb = sb(p1, f"mask_sb{l}", [128, 640], BF16)
        wqkv_sb = sb(p1, f"wqkv_sb{l}", [128, 4096], BF16)
        wo_sb = sb(p1, f"wo_sb{l}", [128, 4096], BF16)
        sq_bf = sb(p1, f"sq_bf{l}", [128, 512], BF16)
        q1_bf = sb(p1, f"q1_bf{l}", [128, 512], BF16)
        qsk_sb = sb(p1, f"qsk_sb{l}", [128, 2], F32)
        rowsp = sb(p1, f"rowsp{l}", [2, 1024], F32)

        def emit_hcast(w):
            ws = QKV_OFF[l] + 512 * w
            for ct in range(8):
                src0 = ws if ct < 4 else ws - 1
                off = 0
                for ap, ww in x_read(ct, src0, 512):
                    bld.op("vector", lambda e, ct=ct, ap=ap, ww=ww, off=off:
                           e.tensor_copy(h_bf[ct][:, off:off + ww], ap),
                           inc="dv" if (ct == 7 and off + ww == 512) else None)
                    off += ww

        pt = {"sq_tr": None, "msA": None, "msB": None, "rot": None,
              "q1": None, "bc": None, "tab": None}

        def emit_post(kind, mt, bank, grp_tm, slot):
            t1, t2 = tabs[0], tabs[1]
            qs_col = qsk_sb[:, 0:1] if kind == "q" else qsk_sb[:, 1:2]
            dst = qbuf[mt][:, 0:512] if kind == "q" \
                else kring[mt][:, slot * 512:(slot + 1) * 512]
            bld.wait("scalar", "tm", grp_tm)
            if pt["msA"] is not None:
                bld.wait("scalar", "tm", pt["msA"])
            sq_ac = bld.op("scalar", lambda e, bank=bank: e.activation(
                sq_bf[:, :], bank, AF.Square), inc="ac")
            bld.wait("tensor", "ac", sq_ac)
            if pt["bc"] is not None:
                bld.wait("tensor", "dv", pt["bc"])
            bld.op("tensor", lambda e: e.matmul(
                ps_ms[0:2, 0:512], bo_col_sb[:, :], sq_bf[:, :], start=True,
                stop=True), inc="tm")
            pt["msA"] = bld.now("tm")
            bld.wait("vector", "tm", pt["msA"])
            mx_dv = bld.op("vector", lambda e: e.tensor_scalar(
                rowsp[0:2, 0:512], ps_ms[0:2, 0:512], 1e-24, None, op0=ALU.max),
                inc="dv")
            bld.wait("scalar", "dv", mx_dv)
            bld.op("scalar", lambda e: e.activation(
                rowsp[0:2, 512:1024], rowsp[0:2, 0:512], AF.Ln, bias=0.0,
                scale=1.0))
            rs_ac = bld.op("scalar", lambda e: e.activation(
                rowsp[0:2, 512:1024], rowsp[0:2, 512:1024], AF.Exp, bias=0.0,
                scale=-0.5), inc="ac")
            bld.wait("tensor", "ac", rs_ac)
            bld.wait("tensor", "dv", mx_dv)
            bld.op("tensor", lambda e: e.matmul(
                ps_ms[:, 0:512], bo_row_sb[:, :], rowsp[0:2, 512:1024],
                start=True, stop=True), inc="tm")
            msB = bld.now("tm")
            bld.wait("vector", "tm", msB)
            if pt["tab"] is not None:
                bld.wait("vector", "dL", pt["tab"])
            bld.op("vector", lambda e: e.tensor_copy(scrB[:, 0:512], ps_ms[:, 0:512]))
            pt["bc"] = bld.now("dv") + 1
            q1_dv = bld.op("vector", lambda e, bank=bank, qs_col=qs_col:
                           e.scalar_tensor_tensor(q1_bf[:, :], bank, qs_col,
                                                  scrB[:, 0:512], op0=ALU.mult,
                                                  op1=ALU.mult), inc="dv")
            bld.wait("tensor", "dv", q1_dv)
            bld.op("tensor", lambda e: e.matmul(
                ps_rot[:, 0:512], rotp_sb[:, :], q1_bf[:, :], start=True,
                stop=True), inc="tm")
            rot_tm = bld.now("tm")
            bld.op("vector", lambda e, t1=t1: e.tensor_tensor(
                scrC[:, 0:512], q1_bf[:, :], t1[:, :], ALU.mult))
            bld.wait("vector", "tm", rot_tm)
            bld.op("vector", lambda e, t2=t2: e.tensor_tensor(
                scrB[:, 0:512], ps_rot[:, 0:512], t2[:, :], ALU.mult),
                inc="dv")
            pt["rot"] = bld.now("dv")
            bld.op("vector", lambda e, dst=dst: e.tensor_tensor(
                dst, scrC[:, 0:512], scrB[:, 0:512], ALU.add), inc="dv")
            return bld.now("dv")

        bld.dma("gpsimd", wo_sb[:, :].rearrange("p (k c) -> p k c", c=1024),
                f_wo[l].rearrange("(k p) c -> p k c", p=128), "dW")
        bld.dma("gpsimd", qsk_sb[:, 0:1], f_qs[l], "dW")
        bld.dma("gpsimd", qsk_sb[:, 1:2], f_ks[l], "dW")
        bld.W("vector", "dW")
        bld.W("tensor", "dW")

        emit_ffn_pass1.at = {"stt": [None, None], "sim": [None, None],
                             "exp": None, "tr": None, "evac": None,
                             "av": None, "oev": None}
        emit_ffn_pass1.qb = [None, None]
        nw = NWIN[l]
        emit_hcast(0)
        for w in range(nw):
            ws = QKV_OFF[l] + 512 * w
            slot = w % 2
            qts = qtiles_of_window(l, w)
            chunks = ([0] if qts else []) + [1, 2]
            for ch in chunks:
                if ch < 2:
                    if pt["rot"] is not None:
                        bld.wait("sync", "dv", pt["rot"])
                    for ti_ in range(2):
                        pt["tab"] = bld.dma("sync", tabs[ti_][:, :],
                                            rot_t[ch * 2 + ti_][:, ws:ws + 512], "dL")
                buf = wqkv_sb
                bld.W("gpsimd", "tm")
                bld.dma("gpsimd",
                        buf[:, :].rearrange("p (k c) -> p k c", c=512),
                        f_wqkv[l].rearrange("(k p) c -> p k c", p=128)[:, :, ch * 512:(ch + 1) * 512],
                        "dW")
                bld.W("tensor", "dW", "dv")
                if ch < 2:
                    for mt in range(4):
                        par = mt % 2
                        bank = ps_qkv[:, par * 512:par * 512 + 512]
                        if emit_ffn_pass1.qb[par] is not None:
                            bld.wait("tensor", "dv", emit_ffn_pass1.qb[par])
                        for kt in range(8):
                            bld.op("tensor", lambda e, b=bank, buf=buf, kt=kt, mt=mt:
                                   e.matmul(b, buf[:, kt * 512 + mt * 128:kt * 512 + mt * 128 + 128],
                                            h_bf[kt][:, 0:512],
                                            start=(kt == 0), stop=(kt == 7)),
                                   inc="tm" if kt == 7 else None)
                        grp_tm = bld.now("tm")
                        emit_ffn_pass1.qb[par] = emit_post(
                            "q" if ch == 0 else "k", mt, bank, grp_tm, slot)
                else:
                    for tt in range(4):
                        par = tt % 2
                        bank = ps_qkv[:, par * 512:par * 512 + 512]
                        if emit_ffn_pass1.qb[par] is not None:
                            bld.wait("tensor", "dv", emit_ffn_pass1.qb[par])
                        for kt in range(8):
                            bld.op("tensor", lambda e, b=bank, buf=buf, kt=kt, tt=tt:
                                   e.matmul(b, h_bf[kt][:, tt * 128:tt * 128 + 128],
                                            buf[:, kt * 512:(kt + 1) * 512],
                                            start=(kt == 0), stop=(kt == 7)),
                                   inc="tm" if kt == 7 else None)
                        bld.W("vector", "tm")
                        bld.op("vector", lambda e, b=bank, slot=slot, tt=tt:
                               e.tensor_copy(vring[:, (slot * 4 + tt) * 512:(slot * 4 + tt + 1) * 512], b),
                               inc="dv")
                        emit_ffn_pass1.qb[par] = bld.now("dv")
                        if "v_dump" in dumps:
                            bld.W("gpsimd", "dv")
                            bld.dma("gpsimd",
                                    dump_t["v_dump"][ws + tt * 128:ws + tt * 128 + 128, :],
                                    vring[:, (slot * 4 + tt) * 512:(slot * 4 + tt + 1) * 512],
                                    "dW")
            if "q_dump" in dumps and qts:
                bld.W("gpsimd", "dv")
                for mt in range(4):
                    bld.dma("gpsimd", dump_t["q_dump"][mt * 128:(mt + 1) * 128, ws:ws + 512],
                            qbuf[mt][:, :], "dW")
            if "k_dump" in dumps:
                bld.W("gpsimd", "dv")
                for mt in range(4):
                    bld.dma("gpsimd", dump_t["k_dump"][mt * 128:(mt + 1) * 128, ws:ws + 512],
                            kring[mt][:, slot * 512:(slot + 1) * 512], "dW")

            at = emit_ffn_pass1.at
            if qts:
                dvnow = bld.now("dv")
                at["stt"][0] = max(at["stt"][0] or 0, dvnow)
                at["stt"][1] = max(at["stt"][1] or 0, dvnow)
            for (j, qcol) in qts:
                qoff = qcol - ws
                ti = qoff // 128
                bld.W("sync", "dv")
                mtok = bld.dma("sync", mask_sb[:, :], mask_d[l][j], "dL")

                def emit_sim(h):
                    ct, half = divmod(h, 2)
                    rb = half * 64
                    par = h % 2
                    simps = ps_sim if par == 0 else ps_qkv
                    p1w = 512 - 128 * ti
                    prev_slot = (w - 1) % 2
                    if at["stt"][par] is not None:
                        bld.wait("tensor", "dv", at["stt"][par])
                    pieces = [(0, p1w, kring[ct][rb:rb + 64,
                                                prev_slot * 512 + 128 * ti:(prev_slot + 1) * 512])]
                    if ti > 0:
                        pieces.append((p1w, 512, kring[ct][rb:rb + 64,
                                                          slot * 512:slot * 512 + 128 * ti]))
                    pieces.append((512, 640, kring[ct][rb:rb + 64,
                                                       slot * 512 + 128 * ti:slot * 512 + 128 * ti + 128]))
                    for pi, (sc0, sc1, kap) in enumerate(pieces):
                        bld.op("tensor", lambda e, ct=ct, rb=rb, qoff=qoff,
                               sc0=sc0, sc1=sc1, kap=kap, simps=simps:
                               e.matmul(simps[:, sc0:sc1],
                                        qbuf[ct][rb:rb + 64, qoff:qoff + 128],
                                        kap, start=True, stop=True,
                                        tile_position=(rb, 0)),
                               inc="tm" if pi == len(pieces) - 1 else None)
                    at["sim"][par] = bld.now("tm")

                def emit_ladder(h):
                    par = h % 2
                    simps = ps_sim if par == 0 else ps_qkv
                    bld.wait("vector", "tm", at["sim"][par])
                    bld.wait("vector", "dL", mtok)
                    if at["exp"] is not None:
                        bld.wait("vector", "ac", at["exp"])
                    bld.op("vector", lambda e, simps=simps: e.scalar_tensor_tensor(
                        masked[:, :], simps[:, 0:640], 8.0, mask_sb[:, :],
                        op0=ALU.mult, op1=ALU.add))
                    at["stt"][par] = bld.op("vector", lambda e: e.reduce_max(
                        rowst[:, 0:1], masked[:, :], axis=AX, negate=True),
                        inc="dv")
                    bld.wait("scalar", "dv", at["stt"][par])
                    if at["tr"] is not None:
                        bld.wait("scalar", "tm", at["tr"])
                    at["exp"] = bld.op("scalar", lambda e: e.activation(
                        attn_bf[:, :], masked[:, :], AF.Exp, bias=rowst[:, 0:1],
                        scale=1.0), inc="ac")

                def emit_trav(h):
                    ct, half = divmod(h, 2)
                    rb = half * 64
                    bld.wait("tensor", "ac", at["exp_t"][h % 2] if False else at["exp"])
                    if at["evac"] is not None:
                        bld.wait("tensor", "dv", at["evac"])
                    for kb in range(5):
                        bld.op("tensor", lambda e, kb=kb: e.transpose(
                            ps_aT[:, kb * 128:(kb + 1) * 128],
                            attn_bf[:, kb * 128:(kb + 1) * 128], id_bf[:, :]),
                            inc="tm" if kb == 4 else None)
                    at["tr"] = bld.now("tm")
                    bld.wait("vector", "tm", at["tr"])
                    at["evac"] = bld.op("vector", lambda e: e.tensor_copy(
                        attnT_sb[:, :], ps_aT[:, :]), inc="dv")
                    bld.wait("tensor", "dv", at["evac"])
                    if h == 0 and at["oev"] is not None:
                        bld.wait("tensor", "dv", at["oev"])
                    rps = ps_ms if h < 4 else ps_rot
                    rcol = (h % 4) * 128
                    for kb in range(5):
                        bld.op("tensor", lambda e, kb=kb, rps=rps, rcol=rcol:
                               e.matmul(rps[0:1, rcol:rcol + 128], onesc_bf[:, 0:1],
                                        attnT_sb[:, kb * 128:(kb + 1) * 128],
                                        start=(kb == 0), stop=(kb == 4)))
                    for kb in range(5):
                        kcol = ws - 512 + 128 * ti + 128 * kb
                        wk = (kcol - QKV_OFF[l]) // 512
                        twt = (kcol - QKV_OFF[l] - 512 * wk) // 128
                        vbase = ((wk % 2) * 4 + twt) * 512
                        bld.op("tensor", lambda e, kb=kb, vbase=vbase, h=h, rb=rb, ct=ct:
                               e.matmul(ps_o[rb:rb + 64, ct * 128:ct * 128 + 128],
                                        vring[:, vbase + h * 64:vbase + h * 64 + 64],
                                        attnT_sb[:, kb * 128:(kb + 1) * 128],
                                        start=(kb == 0), stop=(kb == 4),
                                        tile_position=(0, rb)),
                               inc="tm" if kb == 4 else None)
                    at["av"] = bld.now("tm")

                emit_sim(0)
                emit_ladder(0)
                for h in range(8):
                    if h + 1 < 8:
                        emit_sim(h + 1)
                    emit_trav(h)
                    if h + 1 < 8:
                        emit_ladder(h + 1)
                bld.wait("scalar", "tm", at["av"])
                bld.op("scalar", lambda e: e.activation(
                    scrB[0:1, 0:512], ps_ms[0:1, 0:512], AF.Ln, bias=0.0, scale=1.0))
                bld.op("scalar", lambda e: e.activation(
                    scrC[0:1, 0:512], ps_rot[0:1, 0:512], AF.Ln, bias=0.0, scale=1.0),
                    inc="ac")
                lr_ac = bld.now("ac")
                bld.wait("scalar", "ac", lr_ac)
                bld.op("scalar", lambda e: e.activation(
                    scrB[0:1, 0:512], scrB[0:1, 0:512], AF.Exp, bias=0.0, scale=-1.0))
                bld.op("scalar", lambda e: e.activation(
                    scrC[0:1, 0:512], scrC[0:1, 0:512], AF.Exp, bias=0.0, scale=-1.0),
                    inc="ac")
                rdv = bld.now("ac")
                bld.wait("tensor", "ac", rdv)
                for h2 in range(8):
                    ct2, half2 = divmod(h2, 2)
                    rb2 = half2 * 64
                    srow = scrB if h2 < 4 else scrC
                    scol = (h2 % 4) * 128
                    bld.op("tensor", lambda e, ct2=ct2, rb2=rb2, srow=srow, scol=scol:
                           e.matmul(ps_rot[rb2:rb2 + 64, ct2 * 128:ct2 * 128 + 128],
                                    ones1_f[0:1, 0:64], srow[0:1, scol:scol + 128],
                                    start=True, stop=True, tile_position=(0, rb2)),
                           inc="tm" if h2 == 7 else None)
                btm = bld.now("tm")
                bld.wait("vector", "tm", btm)
                bld.op("vector", lambda e: e.tensor_copy(
                    scrA[:, 0:512], ps_rot[:, 0:512]), inc="dv")
                bld.W("vector", "dv")
                for ct in range(4):
                    bld.op("vector", lambda e, ct=ct, qoff=qoff: e.tensor_tensor(
                        o_sb[ct][:, qoff:qoff + 128], ps_o[:, ct * 128:ct * 128 + 128],
                        scrA[:, ct * 128:ct * 128 + 128], ALU.mult),
                        inc="dv" if ct == 3 else None)
                at["oev"] = bld.now("dv")
                if "o_dump" in dumps:
                    bld.W("gpsimd", "dv")
                    for ct in range(4):
                        bld.dma("gpsimd", dump_t["o_dump"][ct * 128:(ct + 1) * 128, qcol:qcol + 128],
                                o_sb[ct][:, qoff:qoff + 128], "dW")
            bld.W("vector", "tm")
            if w + 1 < nw:
                emit_hcast(w + 1)
            if qts:
                qo0 = qts[0][1] - ws
                wlen = 512 - qo0
                for dt in range(8):
                    bld.W("tensor", "dv")
                    for ct in range(4):
                        bld.op("tensor", lambda e, dt=dt, ct=ct, qo0=qo0, wlen=wlen:
                               e.matmul(ps_o[:, 0:wlen],
                                        wo_sb[:, ct * 1024 + dt * 128:ct * 1024 + dt * 128 + 128],
                                        o_sb[ct][:, qo0:512],
                                        start=(ct == 0), stop=(ct == 3)),
                               inc="tm" if ct == 3 else None)
                    bld.W("vector", "tm")
                    off = 0
                    for ap, ww in x_read(dt, ws + qo0, wlen):
                        bld.op("vector", lambda e, ap=ap, ww=ww, off=off:
                               e.tensor_tensor(ap, ps_o[:, off:off + ww], ap, ALU.add),
                               inc="dv" if off + ww == wlen else None)
                        off += ww
        bld.gate()
        p1.close()

    def emit_ffn_pass2(l):
        p2 = ExitStack()
        ps_ln = psum(p2, f"psln{l}", [128, 1024], F32)
        ps_ag = psum(p2, f"psag{l}", [128, 1024], F32)
        ps_out = psum(p2, f"psout{l}", [128, 2048], F32)
        hh_sb = sb(p2, f"hh_sb{l}", [128, 22 * 512], BF16)
        h2 = [sb(p2, f"h2_{l}_{i}", [128, 512], BF16) for i in range(8)]
        aa_bf = sb(p2, f"aa_bf{l}", [128, 512], BF16)
        gg_bf = sb(p2, f"gg_bf{l}", [128, 512], BF16)
        w1_sb = [sb(p2, f"w1_sb{l}_{i}", [128, 2048], BF16) for i in range(2)]
        w2_sb = [sb(p2, f"w2_sb{l}_{i}", [128, 1024], BF16) for i in range(2)]
        xe2 = [sb(p2, f"xe2_{l}_{i}", [128, 8], F32) for i in range(4)]
        lng_sb = sb(p2, f"lng_sb{l}", [128, 8], F32)
        lnb_sb = sb(p2, f"lnb_sb{l}", [128, 8], F32)
        b1a_sb = sb(p2, f"b1a_sb{l}", [128, 22], F32)
        b1g_sb = sb(p2, f"b1g_sb{l}", [128, 22], F32)
        b2_sb = sb(p2, f"b2_sb{l}", [128, 8], F32)
        outbanks = [ps_out[:, i * 512:(i + 1) * 512] for i in range(4)] + \
                   [ps_ag[:, 0:512], ps_ag[:, 512:1024],
                    ps_ln[:, 0:512], ps_ln[:, 512:1024]]

        bld.dma("gpsimd", lng_sb[:, :], f_lng[l], "dW")
        bld.dma("gpsimd", lnb_sb[:, :], f_lnb[l], "dW")
        bld.dma("gpsimd", b1a_sb[:, :], f_b1a[l], "dW")
        bld.dma("gpsimd", b1g_sb[:, :], f_b1g[l], "dW")
        bld.dma("gpsimd", b2_sb[:, :], f_b2[l], "dW")
        bld.W("vector", "dW")
        bld.W("scalar", "dW")

        o0 = OUT_OFF[l]
        total = T0 - o0
        chunks = []
        c = o0
        while c < T0:
            cw = min(512, T0 - c)
            chunks.append((c, cw))
            c += cw

        for ci, (c0, cw) in enumerate(chunks):
            for ct in range(4, 8):
                (r, s, ww), = x_slices(c0 - 1, 1)
                bld.op("vector", lambda e, ct=ct, r=r, s=s, ci=ci: e.tensor_copy(
                    xe2[ct - 4][:, ci:ci + 1], xap(r, ct, s, 1)),
                    inc="dv" if (ct == 7 and ci == len(chunks) - 1) else None)

        for ci, (c0, cw) in enumerate(chunks):
            def shifted_pieces(ct, ci=ci, c0=c0, cw=cw):
                if ct < 4:
                    return x_read(ct, c0, cw)
                return [(xe2[ct - 4][:, ci:ci + 1], 1)] + \
                    (x_read(ct, c0, cw - 1) if cw > 1 else [])
            bld.W("vector", "tm", "dv")
            for ct in range(8):
                off = 0
                for ap, ww in shifted_pieces(ct):
                    bld.op("vector", lambda e, ct=ct, ap=ap, ww=ww, off=off:
                           e.tensor_copy(h2[ct][:, off:off + ww], ap))
                    off += ww
            for ct in range(8):
                if ct > 0:
                    bld.W("vector", "tm")
                bld.op("vector", lambda e, ct=ct, cw=cw: e.tensor_tensor(
                    aa_bf[:, 0:cw], h2[ct][:, 0:cw], h2[ct][:, 0:cw], ALU.mult),
                    inc="dv")
                bld.W("tensor", "dv")
                bld.op("tensor", lambda e, ct=ct, cw=cw: e.matmul(
                    ps_ln[0:1, 0:cw], onesc_bf[:, 0:1], h2[ct][:, 0:cw],
                    start=(ct == 0), stop=(ct == 7)))
                bld.op("tensor", lambda e, ct=ct, cw=cw: e.matmul(
                    ps_ln[0:1, 512:512 + cw], onesc_bf[:, 0:1], aa_bf[:, 0:cw],
                    start=(ct == 0), stop=(ct == 7)), inc="tm")
            bld.W("vector", "tm")
            bld.op("vector", lambda e, cw=cw: e.tensor_scalar(
                rows1[0:1, 0:cw], ps_ln[0:1, 0:cw], 1.0 / D, None, op0=ALU.mult))
            bld.op("vector", lambda e, cw=cw: e.tensor_scalar(
                rows1[0:1, 1024:1024 + cw], ps_ln[0:1, 512:512 + cw], 1.0 / D,
                None, op0=ALU.mult), inc="dv")
            bld.W("vector", "dv")
            bld.op("vector", lambda e, cw=cw: e.tensor_tensor(
                rows2[0:1, 0:cw], rows1[0:1, 0:cw], rows1[0:1, 0:cw], ALU.mult),
                inc="dv")
            bld.W("vector", "dv")
            bld.op("vector", lambda e, cw=cw: e.tensor_tensor(
                rows2[0:1, 1024:1024 + cw], rows1[0:1, 1024:1024 + cw],
                rows2[0:1, 0:cw], ALU.subtract), inc="dv")
            bld.W("vector", "dv")
            bld.op("vector", lambda e, cw=cw: e.tensor_scalar(
                rows1[0:1, 1024:1024 + cw], rows1[0:1, 0:cw], -1.0, None,
                op0=ALU.mult), inc="dv")
            bld.W("scalar", "dv")
            bld.op("scalar", lambda e, cw=cw: e.activation(
                rows2[0:1, 0:cw], rows2[0:1, 1024:1024 + cw], AF.Ln, bias=1e-5,
                scale=1.0))
            bld.op("scalar", lambda e, cw=cw: e.activation(
                rows2[0:1, 0:cw], rows2[0:1, 0:cw], AF.Exp, bias=0.0,
                scale=-0.5), inc="ac")
            bld.W("tensor", "ac")
            bld.op("tensor", lambda e, cw=cw: e.matmul(
                ps_ln[:, 0:cw], ones1_f[0:1, :], rows1[0:1, 1024:1024 + cw],
                start=True, stop=True))
            bld.op("tensor", lambda e, cw=cw: e.matmul(
                ps_ln[:, 512:512 + cw], ones1_f[0:1, :], rows2[0:1, 0:cw],
                start=True, stop=True), inc="tm")
            bld.W("vector", "tm")
            for ct in range(8):
                off = 0
                for ap, ww in shifted_pieces(ct):
                    bld.op("vector", lambda e, ap=ap, ww=ww, off=off:
                           e.scalar_tensor_tensor(
                               scrA[:, off:off + ww], ps_ln[:, off:off + ww], 1.0,
                               ap, op0=ALU.mult, op1=ALU.add))
                    off += ww
                bld.op("vector", lambda e, cw=cw: e.tensor_tensor(
                    scrB[:, 0:cw], scrA[:, 0:cw], ps_ln[:, 512:512 + cw], ALU.mult))
                bld.op("vector", lambda e, ct=ct, cw=cw: e.tensor_scalar(
                    h2[ct][:, 0:cw], scrB[:, 0:cw], lng_sb[:, ct:ct + 1],
                    lnb_sb[:, ct:ct + 1], op0=ALU.mult, op1=ALU.add),
                    inc="dv")
            if dbg2 is not None and l == 0 and ci == 2:
                bld.gate()
                for ct in range(8):
                    bld.dma("gpsimd", dbg2[ct * 128:(ct + 1) * 128, 0:512],
                            h2[ct][:, 0:512], "dW")
                bld.dma("sync", dbg2[0:1, 512:512 + cw], rows1[0:1, 1024:1024 + cw], "dS")
                bld.dma("sync", dbg2[1:2, 512:512 + cw], rows2[0:1, 0:cw], "dS")
                bld.gate()
            w1tok = [None, None]

            def load_w1(j):
                mw = 128 if j < 21 else 42
                buf = w1_sb[j % 2]
                if w1tok[j % 2] is not None:
                    bld.wait("gpsimd", "tm", w1tok[j % 2])
                bld.dma("gpsimd",
                        buf[:, :].rearrange("p (k c) -> p k c", c=256)[:, :, 0:mw],
                        f_w1[l].rearrange("(k p) c -> p k c", p=128)[:, :, j * 128:j * 128 + mw],
                        "dW")
                return bld.dma("gpsimd",
                               buf[:, :].rearrange("p (k c) -> p k c", c=256)[:, :, 128:128 + mw],
                               f_w1[l].rearrange("(k p) c -> p k c", p=128)[:, :, HID + j * 128:HID + j * 128 + mw],
                               "dW")

            w1dw = {0: load_w1(0)}
            gate_dv = None
            ag_ac = None
            tdv_h2 = bld.now("dv")
            for j in range(22):
                mw = 128 if j < 21 else 42
                buf = w1_sb[j % 2]
                bld.wait("tensor", "dW", w1dw.pop(j))
                if j == 0:
                    bld.wait("tensor", "dv", tdv_h2)
                if ag_ac is not None:
                    bld.wait("tensor", "ac", ag_ac)
                for kt in range(8):
                    bld.op("tensor", lambda e, buf=buf, kt=kt, mw=mw, cw=cw:
                           e.matmul(ps_ag[0:mw, 0:cw],
                                    buf[:, kt * 256:kt * 256 + mw],
                                    h2[kt][:, 0:cw], start=(kt == 0), stop=(kt == 7)))
                for kt in range(8):
                    bld.op("tensor", lambda e, buf=buf, kt=kt, mw=mw, cw=cw:
                           e.matmul(ps_ag[0:mw, 512:512 + cw],
                                    buf[:, kt * 256 + 128:kt * 256 + 128 + mw],
                                    h2[kt][:, 0:cw], start=(kt == 0), stop=(kt == 7)),
                           inc="tm" if kt == 7 else None)
                w1tok[j % 2] = bld.now("tm")
                if j + 1 < 22:
                    w1dw[j + 1] = load_w1(j + 1)
                bld.wait("scalar", "tm", w1tok[j % 2])
                if gate_dv is not None:
                    bld.wait("scalar", "dv", gate_dv)
                bld.op("scalar", lambda e, j=j, mw=mw, cw=cw: e.activation(
                    aa_bf[0:mw, 0:cw], ps_ag[0:mw, 0:cw], AF.Identity,
                    bias=b1a_sb[0:mw, j:j + 1], scale=1.0))
                ag_ac = bld.op("scalar", lambda e, j=j, mw=mw, cw=cw: e.activation(
                    gg_bf[0:mw, 0:cw], ps_ag[0:mw, 512:512 + cw], AF.Gelu,
                    bias=b1g_sb[0:mw, j:j + 1], scale=1.0), inc="ac")
                bld.wait("vector", "ac", ag_ac)
                gate_dv = bld.op("vector", lambda e, j=j, mw=mw, cw=cw: e.tensor_tensor(
                    hh_sb[0:mw, j * 512:j * 512 + cw], aa_bf[0:mw, 0:cw],
                    gg_bf[0:mw, 0:cw], ALU.mult), inc="dv")
                if dbg2 is not None and l == 0 and ci == 2 and j in (0, 1):
                    bld.gate()
                    bld.dma("gpsimd", dbg2[j * 128:(j + 1) * 128, 1024:1024 + cw],
                            hh_sb[0:mw, j * 512:j * 512 + cw], "dW")
                    bld.gate()
            bld.wait("tensor", "dv", gate_dv)
            w2tok = [None, None]

            def load_w2(j):
                mw = 128 if j < 21 else 42
                buf = w2_sb[j % 2]
                if w2tok[j % 2] is not None:
                    bld.wait("gpsimd", "tm", w2tok[j % 2])
                return bld.dma("gpsimd", buf[0:mw, :],
                               f_w2[l][j * 128:j * 128 + mw, :], "dW")

            w2dw = {0: load_w2(0)}
            for j in range(22):
                mw = 128 if j < 21 else 42
                buf = w2_sb[j % 2]
                bld.wait("tensor", "dW", w2dw.pop(j))
                for i in range(8):
                    bld.op("tensor", lambda e, buf=buf, i=i, j=j, mw=mw, cw=cw:
                           e.matmul(outbanks[i][:, 0:cw],
                                    buf[0:mw, i * 128:(i + 1) * 128],
                                    hh_sb[0:mw, j * 512:j * 512 + cw],
                                    start=(j == 0), stop=(j == 21)),
                           inc="tm" if i == 7 else None)
                w2tok[j % 2] = bld.now("tm")
                if j + 1 < 22:
                    w2dw[j + 1] = load_w2(j + 1)
            bld.W("vector", "tm")
            for i in range(8):
                (r, s, ww), = x_slices(c0, cw)
                bld.op("vector", lambda e, i=i, r=r, s=s, ww=ww: e.scalar_tensor_tensor(
                    xap(r, i, s, ww), outbanks[i][:, 0:ww], b2_sb[:, i:i + 1],
                    xap(r, i, s, ww), op0=ALU.add, op1=ALU.add), inc="dv")
        bld.gate()
        p2.close()

    def emit_silu_park():
        for nn in range((TOK + 511) // 512):
            c0 = H0 + nn * 512
            for ct in range(8):
                bld.W("scalar", "dS")
                bld.op("scalar", lambda e, ct=ct, c0=c0: e.activation(
                    scrA[:, 0:512], xB[ct][:, c0 - XH_W:c0 - XH_W + 512],
                    AF.Silu), inc="ac")
                bld.W("sync", "ac")
                bld.dma("sync", g_park[ct * 128:(ct + 1) * 128, c0 - H0:c0 - H0 + 512],
                        scrA[:, 0:512], "dS")
                if "g_dump" in dumps:
                    bld.dma("sync", dump_t["g_dump"][ct * 128:(ct + 1) * 128, c0 - H0:c0 - H0 + 512],
                            scrA[:, 0:512], "dS")
        bld.gate()

    emit_top_ln()
    if "xn_dump" in dumps:
        dump_x("xn_dump")
    for l in range(n_ffn):
        emit_ffn_pass1(l)
        if l == 0 and "xa1_dump" in dumps:
            dump_x("xa1_dump")
        emit_ffn_pass2(l)
        if l == 0 and "xl1_dump" in dumps:
            dump_x("xl1_dump")
    if n_ffn == 2 and "xl2_dump" in dumps:
        dump_x("xl2_dump")
    if n_ffn == 2:
        emit_silu_park()
    bld.replay()
    fstack.close()

    astack = ExitStack()
    x_a = [sb(astack, f"xa{i}", [128, TOK], F32) for i in range(8)]
    v_sb = [sb(astack, f"vsb{i}", [128, TOK], BF16) for i in range(8)]
    Eqp = [sb(astack, f"eqp{i}", [128, 1024], BF16) for i in range(8)]
    Ek = [sb(astack, f"ek{i}", [128, 1024], BF16) for i in range(8)]
    ha_bf = [sb(astack, f"habf{i}", [128, 1024], BF16) for i in range(8)]
    Msb = [sb(astack, f"msb{i}", [128, 1024], BF16) for i in range(8)]
    Ablk = sb(astack, "ablk", [128, 1024], BF16)
    zq_bf = Ablk
    wsb = sb(astack, "wsb", [128, 4096], BF16)
    wsb2 = sb(astack, "wsb2", [128, 4096], BF16)
    gam_sb = sb(astack, "gam_sb", [128, 8], F32)
    bet_sb = sb(astack, "bet_sb", [128, 8], F32)
    bo_sb = sb(astack, "bo_sb", [128, 8], F32)
    zkl_sb = sb(astack, "zkl_sb", [128, 8], F32)
    zk0_sb = sb(astack, "zk0_sb", [128, 8], F32)
    zkb_sb = sb(astack, "zkb_sb", [128, 8], BF16)
    zq_acc = sb(astack, "zq_acc", [1, 1024], F32)
    zqt = sb(astack, "zqt", [128, 8], F32)
    PS = psum(astack, "PS", [128, 4096], F32)

    def emit_attn_layer(l):
        btok = emit_attn_layer.btok
        wtok = emit_attn_layer.wtok

        def bank_wait(*banks):
            for bk in banks:
                if bk in btok:
                    s, v = btok[bk]
                    bld.wait("tensor", s, v)

        bld.dma("gpsimd", gam_sb[:, :], a_gamma[l], "dW")
        bld.dma("gpsimd", bet_sb[:, :], a_beta[l], "dW")
        tW = bld.dma("gpsimd", bo_sb[:, :], a_bout[l], "dW")
        bld.op("gpsimd", lambda e: e.memset(zq_acc[:, :], 0.0), inc="gp")
        tgp = bld.now("gp")
        bld.wait("vector", "gp", tgp)
        bld.wait("vector", "dW", tW)
        bld.wait("scalar", "dW", tW)

        for c in range(2):
            tc = c * 1024
            bld.W("vector", "tm", "dv")
            for ct in range(8):
                bld.op("vector", lambda e, ct=ct, tc=tc: e.tensor_copy(
                    ha_bf[ct][:, :], x_a[ct][:, tc:tc + 1024]))
            for ct in range(8):
                bld.op("vector", lambda e, ct=ct: e.tensor_tensor(
                    Ek[ct][:, :], ha_bf[ct][:, :], ha_bf[ct][:, :], ALU.mult),
                    inc="dv" if ct == 7 else None)
            tdv = bld.now("dv")
            bld.wait("tensor", "dv", tdv)
            bank_wait(0, 1, 6, 7)
            for hh in range(2):
                for ct in range(8):
                    bld.op("tensor", lambda e, ct=ct, hh=hh: e.matmul(
                        PS[0:1, 3072 + hh * 512:3584 + hh * 512],
                        onesc_bf[:, 0:1], ha_bf[ct][:, hh * 512:(hh + 1) * 512],
                        start=(ct == 0), stop=(ct == 7)))
                for ct in range(8):
                    bld.op("tensor", lambda e, ct=ct, hh=hh: e.matmul(
                        PS[0:1, hh * 512:512 + hh * 512],
                        onesc_bf[:, 0:1], Ek[ct][:, hh * 512:(hh + 1) * 512],
                        start=(ct == 0), stop=(ct == 7)),
                        inc="tm" if ct == 7 else None)
            ttm = bld.now("tm")
            bld.wait("vector", "tm", ttm)
            bld.op("vector", lambda e: e.tensor_scalar(
                rows1[0:1, 0:1024], PS[0:1, 3072:4096], 1.0 / D, None, op0=ALU.mult))
            bld.op("vector", lambda e: e.tensor_scalar(
                rows1[0:1, 1024:2048], PS[0:1, 0:1024], 1.0 / D, None,
                op0=ALU.mult), inc="dv")
            bld.W("vector", "dv")
            bld.op("vector", lambda e: e.tensor_tensor(
                rows2[0:1, 0:1024], rows1[0:1, 0:1024], rows1[0:1, 0:1024],
                ALU.mult), inc="dv")
            bld.W("vector", "dv")
            bld.op("vector", lambda e: e.tensor_tensor(
                rows2[0:1, 1024:2048], rows1[0:1, 1024:2048], rows2[0:1, 0:1024],
                ALU.subtract), inc="dv")
            bld.W("scalar", "dv")
            bld.op("scalar", lambda e: e.activation(
                rows2[0:1, 0:1024], rows2[0:1, 1024:2048], AF.Ln, bias=1e-5,
                scale=1.0))
            bld.op("scalar", lambda e: e.activation(
                rows2[0:1, 0:1024], rows2[0:1, 0:1024], AF.Exp, bias=0.0,
                scale=-0.5), inc="ac")
            bld.W("vector", "dv")
            bld.op("vector", lambda e: e.tensor_scalar(
                rows1[0:1, 1024:2048], rows1[0:1, 0:1024], -1.0, None,
                op0=ALU.mult), inc="dv")
            tdv = bld.now("dv"); tac = bld.now("ac")
            bld.wait("tensor", "ac", tac)
            bld.wait("tensor", "dv", tdv)
            bank_wait(4, 5, 6, 7)
            for hh in range(2):
                bld.op("tensor", lambda e, hh=hh: e.matmul(
                    PS[:, 2048 + hh * 512:2560 + hh * 512], ones1_f[0:1, :],
                    rows1[0:1, 1024 + hh * 512:1024 + (hh + 1) * 512],
                    start=True, stop=True))
                bld.op("tensor", lambda e, hh=hh: e.matmul(
                    PS[:, 3072 + hh * 512:3584 + hh * 512], ones1_f[0:1, :],
                    rows2[0:1, hh * 512:(hh + 1) * 512], start=True, stop=True),
                    inc="tm" if hh == 1 else None)
            ttm = bld.now("tm")
            bld.wait("vector", "tm", ttm)
            for ct in range(8):
                for hh in range(2):
                    bld.op("vector", lambda e, ct=ct, tc=tc, hh=hh: e.scalar_tensor_tensor(
                        scrA[:, 0:512], PS[:, 2048 + hh * 512:2560 + hh * 512], 1.0,
                        x_a[ct][:, tc + hh * 512:tc + (hh + 1) * 512],
                        op0=ALU.mult, op1=ALU.add))
                    bld.op("vector", lambda e, hh=hh: e.tensor_tensor(
                        scrA[:, 0:512], scrA[:, 0:512],
                        PS[:, 3072 + hh * 512:3584 + hh * 512], ALU.mult))
                    bld.op("vector", lambda e, ct=ct, hh=hh: e.tensor_scalar(
                        ha_bf[ct][:, hh * 512:(hh + 1) * 512], scrA[:, 0:512],
                        gam_sb[:, ct:ct + 1], bet_sb[:, ct:ct + 1],
                        op0=ALU.mult, op1=ALU.add),
                        inc="dv")
            tdv_h = bld.now("dv")
            for bk in (4, 5, 6, 7):
                btok[bk] = ("dv", tdv_h)
            h_waited = [False]

            def load_w(oc, ci):
                buf = wsb if ci % 2 == 0 else wsb2
                if wtok[ci % 2] is not None:
                    bld.wait("gpsimd", "tm", wtok[ci % 2])
                t = bld.dma("gpsimd",
                            buf[:, :].rearrange("p (k c) -> p k c", c=512),
                            a_wqkv[l].rearrange("(k p) c -> p k c", p=128)[:, :, oc * 512:(oc + 1) * 512],
                            "dW")
                return buf, t

            ocs = (2, 3, 0, 1, 4, 5)
            loads = {}
            loads[0] = load_w(ocs[0], emit_attn_layer.ci)
            emit_attn_layer.ci += 1
            for oci, oc in enumerate(ocs):
                buf, tw = loads.pop(oci)
                mybuf = 0 if buf is wsb else 1
                bld.wait("tensor", "dW", tw)
                if not h_waited[0]:
                    bld.wait("tensor", "dv", tdv_h)
                    h_waited[0] = True
                last_tm = None
                if oc < 4:
                    for tt in range(8):
                        bk = tt % 4
                        bank = PS[:, bk * 512:bk * 512 + 512]
                        bank_wait(bk)
                        for kt in range(8):
                            bld.op("tensor", lambda e, b=bank, kt=kt, tt=tt, mybuf=mybuf:
                                   e.matmul(b, ha_bf[kt][:, tt * 128:tt * 128 + 128],
                                            (wsb if mybuf == 0 else wsb2)[:, kt * 512:(kt + 1) * 512],
                                            start=(kt == 0), stop=(kt == 7)),
                                   inc="tm" if kt == 7 else None)
                        gtok = bld.now("tm")
                        last_tm = gtok
                        if tt == 0 and oci + 1 < len(ocs):
                            loads[oci + 1] = load_w(ocs[oci + 1], emit_attn_layer.ci)
                            emit_attn_layer.ci += 1
                        bld.wait("scalar", "tm", gtok)
                        if oc >= 2:
                            kh = oc - 2
                            tac = bld.op("scalar", lambda e, b=bank, tt=tt, kh=kh:
                                         e.activation(Ek[tt][:, kh * 512:(kh + 1) * 512],
                                                      b, AF.Exp, bias=0.0, scale=1.0),
                                         inc="ac")
                            btok[bk] = ("ac", tac)
                            tdv = bld.op("vector", lambda e, tt=tt, kh=kh: (
                                e.wait_ge(bld.sems["ac"], 0)) if False else None)                                 if False else None
                            bld.wait("vector", "ac", tac)
                            if kh == 0:
                                tdv = bld.op("vector", lambda e, tt=tt: e.reduce_sum(
                                    rowst[:, 8:9], Ek[tt][:, 0:512], axis=AX),
                                    inc="dv")
                                bld.W("vector", "dv")
                                bld.op("vector", lambda e, tt=tt: e.tensor_copy(
                                    zk0_sb[:, tt:tt + 1], rowst[:, 8:9]), inc="dv")
                            else:
                                bld.op("vector", lambda e, tt=tt: e.reduce_sum(
                                    rowst[:, 9:10], Ek[tt][:, 512:1024], axis=AX),
                                    inc="dv")
                                bld.W("vector", "dv")
                                bld.op("vector", lambda e, tt=tt: e.tensor_tensor(
                                    rowst[:, 10:11], rowst[:, 9:10],
                                    zk0_sb[:, tt:tt + 1], ALU.add), inc="dv")
                                bld.W("vector", "dv")
                                bld.op("vector", lambda e, tt=tt: e.tensor_copy(
                                    zkb_sb[:, tt:tt + 1], rowst[:, 10:11]),
                                    inc="dv")
                                tdv = bld.now("dv")
                                bld.wait("scalar", "dv", tdv)
                                bld.op("scalar", lambda e, tt=tt: e.activation(
                                    zkl_sb[:, tt:tt + 1], rowst[:, 10:11], AF.Ln,
                                    bias=0.0, scale=1.0), inc="ac")
                                tac2 = bld.now("ac")
                                bld.wait("vector", "ac", tac2)
                                bld.op("vector", lambda e, tt=tt: e.tensor_scalar(
                                    zkl_sb[:, tt:tt + 1], zkl_sb[:, tt:tt + 1],
                                    -1.0, None, op0=ALU.mult), inc="dv")
                                emit_attn_layer.zktok[tt] = bld.now("dv")
                        else:
                            qh = oc
                            bld.wait("scalar", "dv", emit_attn_layer.zktok[tt])
                            tac = bld.op("scalar", lambda e, b=bank, tt=tt, qh=qh:
                                         e.activation(Eqp[tt][:, qh * 512:(qh + 1) * 512],
                                                      b, AF.Exp, bias=zkl_sb[:, tt:tt + 1],
                                                      scale=1.0), inc="ac")
                            btok[bk] = ("ac", tac)
                else:
                    for mt in range(4):
                        vt = (oc - 4) * 4 + mt
                        for nn2 in range(2):
                            bk = 4 + (mt * 2 + nn2) % 2
                            bank = PS[:, bk * 512:bk * 512 + 512]
                            bank_wait(bk)
                            for kt in range(8):
                                bld.op("tensor", lambda e, b=bank, kt=kt, mt=mt,
                                       nn2=nn2, mybuf=mybuf:
                                       e.matmul(b, (wsb if mybuf == 0 else wsb2)[:, kt * 512 + mt * 128:kt * 512 + mt * 128 + 128],
                                                ha_bf[kt][:, nn2 * 512:(nn2 + 1) * 512],
                                                start=(kt == 0), stop=(kt == 7)),
                                       inc="tm" if kt == 7 else None)
                            gtok = bld.now("tm")
                            last_tm = gtok
                            if mt == 0 and nn2 == 0 and oci + 1 < len(ocs):
                                loads[oci + 1] = load_w(ocs[oci + 1], emit_attn_layer.ci)
                                emit_attn_layer.ci += 1
                            bld.wait("vector", "tm", gtok)
                            tdv = bld.op("vector", lambda e, b=bank, vt=vt, tc=tc, nn2=nn2:
                                         e.tensor_copy(v_sb[vt][:, tc + nn2 * 512:tc + (nn2 + 1) * 512], b),
                                         inc="dv")
                            btok[bk] = ("dv", tdv)
                wtok[mybuf] = last_tm
            bld.W("tensor", "ac", "dv")
            for ct in range(8):
                for half in range(2):
                    bk = (2 * ct + half) % 8
                    bank = PS[:, bk * 512:bk * 512 + 512]
                    bank_wait(bk)
                    for tt in range(8):
                        bld.op("tensor", lambda e, b=bank, ct=ct, half=half, tt=tt:
                               e.matmul(b, Eqp[tt][:, ct * 128:ct * 128 + 128],
                                        Ek[tt][:, half * 512:(half + 1) * 512],
                                        start=(tt == 0), stop=(tt == 7)),
                               inc="tm" if tt == 7 else None)
                gtok = bld.now("tm")
                bld.wait("vector", "tm", gtok)
                pcols = PS[:, ((2 * ct) % 8) * 512:((2 * ct) % 8) * 512 + 1024]
                if c == 0:
                    tdv = bld.op("vector", lambda e, ct=ct, pcols=pcols: e.tensor_copy(
                        Msb[ct][:, :], pcols), inc="dv")
                else:
                    tdv = bld.op("vector", lambda e, ct=ct, pcols=pcols: e.tensor_tensor(
                        Msb[ct][:, :], pcols, Msb[ct][:, :], ALU.add), inc="dv")
                btok[(2 * ct) % 8] = ("dv", tdv)
                btok[(2 * ct + 1) % 8] = ("dv", tdv)
            for half in range(2):
                bk = 4 + half
                bank_wait(bk)
                for tt in range(8):
                    bld.op("tensor", lambda e, half=half, tt=tt: e.matmul(
                        PS[0:1, 2048 + half * 512:2560 + half * 512],
                        zkb_sb[:, tt:tt + 1], Eqp[tt][:, half * 512:(half + 1) * 512],
                        start=(tt == 0), stop=(tt == 7)),
                        inc="tm" if tt == 7 else None)
            gtok = bld.now("tm")
            bld.wait("vector", "tm", gtok)
            tdv = bld.op("vector", lambda e: e.tensor_tensor(
                zq_acc[0:1, :], PS[0:1, 2048:3072], zq_acc[0:1, :], ALU.add),
                inc="dv")
            btok[4] = ("dv", tdv)
            btok[5] = ("dv", tdv)
        bld.W("vector", "dv")
        bld.op("vector", lambda e: e.tensor_copy(zq_bf[0:1, 0:1024], zq_acc[:, :]),
               inc="dv")
        bld.W("sync", "dv")
        for ct in range(8):
            bld.dma("sync", ar_in[ct * 128:(ct + 1) * 128, :], Msb[ct][:, :], "dS")
        bld.dma("sync", ar_in[1024:1025, :], zq_bf[0:1, 0:1024], "dS")
        bld.W("gpsimd", "dS")
        ccname = f"cc{l}"
        bld.op("gpsimd", lambda e: e.collective_compute(
            "AllReduce", ALU.add, replica_groups=GROUPS,
            ins=[ar_in.ap().opt()], outs=[ar_out.ap().opt()]), inc=ccname)
        bld.W("sync", ccname)
        for ct in range(8):
            bld.dma("sync", Msb[ct][:, :], ar_out[ct * 128:(ct + 1) * 128, :], "dL")
        bld.dma("sync", zq_bf[0:1, 0:1024], ar_out[1024:1025, :], "dL")
        bld.W("vector", "dL")
        bld.op("vector", lambda e: e.reciprocal(rows1[0:1, 0:1024], zq_bf[0:1, :]),
               inc="dv")
        tdv = bld.now("dv")
        bld.wait("tensor", "dv", tdv)
        bank_wait(6)
        for i in range(8):
            bld.op("tensor", lambda e, i=i: e.transpose(
                PS[:, 3072 + i:3073 + i], rows1[0:1, i * 128:(i + 1) * 128],
                id1_f[:, :]), inc="tm" if i == 7 else None)
        ttm = bld.now("tm")
        bld.wait("vector", "tm", ttm)
        bld.op("vector", lambda e: e.tensor_copy(zqt[:, :], PS[:, 3072:3080]),
               inc="dv")
        bld.W("vector", "dv")
        tz = None
        for ct in range(8):
            tz = bld.op("vector", lambda e, ct=ct: e.tensor_scalar(
                Msb[ct][:, :], Msb[ct][:, :], zqt[:, ct:ct + 1], None,
                op0=ALU.mult), inc="dv")
        btok[6] = ("dv", tz)
        bld.wait("tensor", "dv", tz)
        for nn in range(2):
            ci = emit_attn_layer.ci
            buf = wsb if ci % 2 == 0 else wsb2
            mybuf = ci % 2
            emit_attn_layer.ci += 1
            if wtok[mybuf] is not None:
                bld.wait("gpsimd", "tm", wtok[mybuf])
            tw = bld.dma("gpsimd",
                         buf[:, :].rearrange("p (k c) -> p k c", c=512),
                         a_wout[l].rearrange("(k p) c -> p k c", p=128)[:, :, nn * 512:(nn + 1) * 512],
                         "dW")
            bld.wait("tensor", "dW", tw)
            last_tm = None
            for dbi in range(4):
                db = nn * 4 + dbi
                bank_wait(0, 1)
                for j in range(8):
                    bld.op("tensor", lambda e, j=j, dbi=dbi, mybuf=mybuf: [
                        e.matmul(PS[:, j * 128:(j + 1) * 128],
                                 Msb[ct2][:, j * 128:j * 128 + 128],
                                 (wsb if mybuf == 0 else wsb2)[:, ct2 * 512 + dbi * 128:ct2 * 512 + dbi * 128 + 128],
                                 start=(ct2 == 0), stop=(ct2 == 7))
                        for ct2 in range(8)][-1],
                           inc="tm")
                gtok = bld.now("tm")
                last_tm = gtok
                bld.wait("vector", "tm", gtok)
                tdv = bld.op("vector", lambda e: e.tensor_copy(Ablk[:, :], PS[:, 0:1024]),
                             inc="dv")
                btok[0] = ("dv", tdv)
                btok[1] = ("dv", tdv)
                bld.wait("tensor", "dv", tdv)
                for nn2 in range(4):
                    bk = 2 + nn2 % 2
                    bank = PS[:, bk * 512:bk * 512 + 512]
                    bank_wait(bk)
                    for j in range(8):
                        bld.op("tensor", lambda e, b=bank, j=j, nn2=nn2: e.matmul(
                            b, Ablk[:, j * 128:(j + 1) * 128],
                            v_sb[j][:, nn2 * 512:(nn2 + 1) * 512],
                            start=(j == 0), stop=(j == 7)),
                            inc="tm" if j == 7 else None)
                    gtok = bld.now("tm")
                    last_tm = gtok
                    bld.wait("vector", "tm", gtok)
                    tdv = bld.op("vector", lambda e, b=bank, db=db, nn2=nn2:
                                 e.scalar_tensor_tensor(
                                     x_a[db][:, nn2 * 512:(nn2 + 1) * 512], b,
                                     bo_sb[:, db:db + 1],
                                     x_a[db][:, nn2 * 512:(nn2 + 1) * 512],
                                     op0=ALU.add, op1=ALU.add), inc="dv")
                    btok[bk] = ("dv", tdv)
            wtok[mybuf] = last_tm
        bld.gate()
        emit_attn_layer.btok = {}
        emit_attn_layer.wtok = [None, None]
    emit_attn_layer.btok = {}
    emit_attn_layer.wtok = [None, None]
    emit_attn_layer.ci = 0
    emit_attn_layer.zktok = [0] * 8

    for ct in range(8):
        bld.dma("sync", x_a[ct][:, :], xn_park[ct * 128:(ct + 1) * 128, :], "dL")
    bld.W("vector", "dL")
    bld.W("tensor", "dL")

    for l in range(n_attn):
        emit_attn_layer(l)
        if xal is not None:
            for ct in range(8):
                bld.dma("sync", xal[l][ct * 128:(ct + 1) * 128, :], x_a[ct][:, :], "dS")
            bld.W("sync", "dS")
        if l == 0 and "s1_dump" in dumps:
            bld.gate()
            for ct in range(8):
                bld.dma("sync", dump_t["s1_dump"][ct * 128:(ct + 1) * 128, :],
                        x_a[ct][:, :], "dS")
            bld.gate()
    if "s_dump" in dumps:
        bld.gate()
        for ct in range(8):
            bld.dma("sync", dump_t["s_dump"][ct * 128:(ct + 1) * 128, :],
                    x_a[ct][:, :], "dS")
        bld.gate()

    for nn in range(4):
        for ct in range(8):
            bld.W("sync", "dv")
            bld.dma("sync", scrA[:, 0:512],
                    xcore[ct * 128:(ct + 1) * 128, H0 + nn * 512:H0 + (nn + 1) * 512],
                    "dL")
            bld.dma("sync", scrB[:, 0:512],
                    g_park[ct * 128:(ct + 1) * 128, nn * 512:(nn + 1) * 512], "dL")
            bld.W("vector", "dL")
            bld.W("vector", "dS")
            bld.op("vector", lambda e, ct=ct, nn=nn: e.tensor_tensor(
                scrC[:, 0:512], x_a[ct][:, nn * 512:(nn + 1) * 512], scrB[:, 0:512],
                ALU.mult))
            bld.op("vector", lambda e: e.tensor_tensor(
                scrC[:, 0:512], scrC[:, 0:512], scrA[:, 0:512], ALU.add), inc="dv")
            bld.W("sync", "dv")
            bld.dma("sync", out_fm[ct * 128:(ct + 1) * 128, nn * 512:(nn + 1) * 512],
                    scrC[:, 0:512], "dS")
    bld.W("sync", "dS")
    bld.W("gpsimd", "dW")

    bld.replay()
    astack.close()
    pstack.close()
    bld.close()
    return nc



def kernel(**inputs):
    in_maps = [stage_core(inputs, p) for p in range(NCORES)]
    nc = build_program()
    res = run_bass_kernel_spmd(nc, in_maps, core_ids=list(range(NCORES)))
    out = np.empty((B, N, D), dtype=np.float32)
    for p in range(NCORES):
        b, pos = divmod(p, 4)
        t0 = pos * TOK
        out[b, t0:t0 + TOK, :] = res.results[p]["out_fm"].T
    return out
